# version 13
# baseline (speedup 1.0000x reference)
import numpy as np
import ml_dtypes
from contextlib import ExitStack
import concourse.bass as bass
import concourse.mybir as mybir
from concourse.bass_utils import run_bass_kernel_spmd

F32 = mybir.dt.float32
BF16 = mybir.dt.bfloat16
U8 = mybir.dt.uint8
AF = mybir.ActivationFunctionType
ALU = mybir.AluOpType

D = 2048
NH = 16
HD = 128
NBLK = 16
QK_SCALE = float(1.0 / np.sqrt(HD))
EXP_BIAS = -16.0
MASKVAL = -1e9
LN_EPS = 1e-5


def pi_perm():
    idx = []
    for c in range(8):
        idx.extend(range(256 * c, 256 * c + 256, 2))
        idx.extend(range(256 * c + 1, 256 * c + 256, 2))
    return np.array(idx, dtype=np.int64)


class Sem:
    def __init__(self, ctx, nc, name):
        self.h = ctx.enter_context(nc.semaphore(name))
        self.n = 0


class B:

    def __init__(self, sems):
        self.sems = sems
        self.mode = None
        self.eng = None
        self.M = {}

    def begin(self, mode, eng):
        self.mode = mode
        self.eng = eng
        for s in self.sems:
            s.n = 0

    def op(self, name, fn, sem=None, k=1):
        if self.mode == name:
            ins = fn(self.eng)
            if sem is not None:
                ins.then_inc(sem.h, k)
        if sem is not None:
            sem.n += k
            return sem.n
        return None

    def dynoff(self, e, coeff, const):
        if getattr(self, "_dreg_eng", None) is not e:
            self._dreg_eng = e
            self._dreg = e.alloc_register()
            self._pid = e.partition_id()
        e.reg_alu(self._dreg, self._pid, coeff, mybir.AluOpType.mult)
        e.reg_alu(self._dreg, self._dreg, const, mybir.AluOpType.add)
        return self._dreg

    def wait(self, name, sem, key):
        if isinstance(key, str):
            if key not in self.M:
                assert self.mode is None, f"missing milestone {key} in live pass"
                return
            val = self.M[key]
        else:
            val = key
        if self.mode == name:
            self.eng.wait_ge(sem.h, val)


DEC_INLINE = True


def build_program(MLP, S):
    TB = S // NBLK
    TQ = min(128, TB)
    NTQ = TB // TQ
    NG = MLP // 128
    NSP = MLP // 256
    AN = MLP // 2048
    VS = HD + 1
    NCH = min(4096, MLP)
    assert MLP % 2048 == 0 and S % NBLK == 0

    nc = bass.Bass("TRN2", target_bir_lowering=False)

    xb = [nc.dram_tensor(f"xb{b}", [D, TB], F32, kind="ExternalInput").ap() for b in range(2)]
    gpk = nc.dram_tensor("gpk", [128, MLP], U8, kind="ExternalInput").ap()
    gsc = nc.dram_tensor("gsc", [128, MLP], F32, kind="ExternalInput").ap()
    upk = nc.dram_tensor("upk", [128, MLP], U8, kind="ExternalInput").ap()
    usc = nc.dram_tensor("usc", [128, MLP], F32, kind="ExternalInput").ap()
    dpk = nc.dram_tensor("dpk", [128, AN * D], U8, kind="ExternalInput").ap()
    dsc = nc.dram_tensor("dsc", [128, AN * D], F32, kind="ExternalInput").ap()
    aw = nc.dram_tensor("aw", [256, 4 * D], F32, kind="ExternalInput").ap()
    bqk = nc.dram_tensor("bqk", [128, 32], F32, kind="ExternalInput").ap()
    bvr = nc.dram_tensor("bvr", [1, D], F32, kind="ExternalInput").ap()
    bo = nc.dram_tensor("bo", [128, 16], F32, kind="ExternalInput").ap()
    ob = [nc.dram_tensor(f"ob{b}", [D, TB], F32, kind="ExternalOutput").ap() for b in range(2)]

    NCHK = max(MLP // 4096, 1)
    NCHC = MLP // NCHK
    blob_gu_t = [nc.dram_tensor(f"blob_gu{k}", [512, NCHC], BF16) for k in range(NCHK)]
    blob_dn_t = [nc.dram_tensor(f"blob_dn{j}", [2 * 8 * 128, AN * 128], BF16)
                 for j in range(2)]
    blob_at_t = nc.dram_tensor("blob_at", [256, 4 * D], BF16)
    WGU_t = [nc.dram_tensor(f"WGU{k}", [8 * 512, NCHC], BF16, addr_space="Shared") for k in range(NCHK)]
    WDN_t = [nc.dram_tensor(f"WDN{j}", [8 * 2 * 8 * 128, AN * 128], BF16, addr_space="Shared")
             for j in range(2)]
    WAT_t = nc.dram_tensor("WAT", [8 * 256, 4 * D], BF16, addr_space="Shared")
    QIN = [nc.dram_tensor(f"QIN{i}", [D, TB], BF16) for i in range(2)]
    KIN = [nc.dram_tensor(f"KIN{i}", [D, TB], BF16) for i in range(2)]
    VIN = [nc.dram_tensor(f"VIN{i}", [TB, NH * VS], BF16) for i in range(2)]
    QOUT = [nc.dram_tensor(f"QOUT{i}", [8 * D, TB], BF16, addr_space="Shared") for i in range(2)]
    KOUT = [nc.dram_tensor(f"KOUT{i}", [8 * D, TB], BF16, addr_space="Shared") for i in range(2)]
    VOUT = [nc.dram_tensor(f"VOUT{i}", [8 * TB, NH * VS], BF16, addr_space="Shared") for i in range(2)]
    AIN = [nc.dram_tensor(f"AIN{i}", [256, 8 * TB], BF16) for i in range(2)]
    AOUT = [nc.dram_tensor(f"AOUT{i}", [8 * 256, 8 * TB], BF16, addr_space="Shared") for i in range(2)]
    R1_t = nc.dram_tensor("R1", [2, D, TB], F32)
    R2_t = nc.dram_tensor("R2", [2, D, TB], F32)

    with ExitStack() as ctx:
        def sbt(name, shape, dt):
            return ctx.enter_context(nc.sbuf_tensor(name, shape, dt))

        off_k = 0
        off_v = off_k + 2 * NBLK * TB
        off_q = off_v + 2 * NBLK * NTQ * VS
        off_p = off_q + 2 * NBLK * TB
        off_ao = off_p + 2 * TB
        off_aot = off_ao + 2 * NTQ * HD
        off_att = off_aot + 2 * TQ
        off_mask = off_att + 16 * TB
        off_smask = off_mask + NTQ * TB
        attn_el = off_smask + 2 * TB
        DW = max(NCH // 2, D)
        N_SETS = 3
        N_OSETS = 3
        dk_pk = [0, NCH]
        _ub = 2 * NCH
        dk_dqu = [[_ub + (2 * s + i) * DW for i in range(2)] for s in range(N_SETS)]
        _fb = (_ub + 2 * N_SETS * DW) // 2
        dk_dqf = [[_fb + (4 * s + i) * DW for i in range(4)] for s in range(N_SETS)]
        dk_dqs = (_fb + 4 * N_SETS * DW) // 2
        dk_dqo = [2 * dk_dqs + 2 * NCH + s * DW for s in range(N_OSETS)]
        dk_awb = dk_dqo[0] + N_OSETS * DW
        decode_el = dk_awb + 4 * D
        AR_EL = max(NG * TB, attn_el, decode_el)
        arena = sbt("arena", [128, AR_EL], BF16)

        def AR(off, width):
            return arena[:, off:off + width]

        arf = arena[:, :].bitcast(F32)
        aru = arena[:, :].bitcast(U8)
        off_bvrep = off_k

        strm = sbt("strm", [128, 16384], BF16)
        x_n = sbt("x_n", [128, 16 * TB], BF16)
        rbuf = [sbt(f"rbuf{i}", [128, TB], F32) for i in range(2)]
        arep = sbt("arep", [128, TB], F32)
        brep = sbt("brep", [128, TB], F32)
        lnt = [sbt(f"lnt{i}", [128, TB], F32) for i in range(2)]
        sqb = lnt[0]
        gelu_b = [sbt(f"gelu{i}", [128, TB], BF16) for i in range(2)]
        outb = [sbt(f"outb{i}", [128, TB], F32) for i in range(2)]
        evb = [sbt(f"evb{i}", [128, 512], BF16) for i in range(2)]
        stat = sbt("stat", [1, 4 * TB], F32)
        ones_col = sbt("ones_col", [128, 1], F32)
        ones_row = sbt("ones_row", [1, 128], F32)
        bqk_sb = sbt("bqk_sb", [128, 32], F32)
        bo_sb = sbt("bo_sb", [128, 16], F32)
        onesv = sbt("onesv", [128, max(NH, 1)], BF16)
        lrec = sbt("lrec", [128, 2], F32)
        cb = sbt("cb", [128, 8], F32)
        idf = sbt("idf", [128, 128], F32)
        idbf = sbt("idbf", [128, 128], BF16)

        for i, val in enumerate([EXP_BIAS, LN_EPS, 1.0, 0.5, -2.0, -6.0, -1.0, -3.0]):
            nc.const_aps.register(F32, val, cb[:, i:i + 1])

        def dqu_ap(s, i, w):
            return aru[:, dk_dqu[s][i]:dk_dqu[s][i] + w]

        def dqf_ap(s, i, w):
            return arena[:, dk_dqf[s][i]:dk_dqf[s][i] + w]

        def dqs_ap(off, w):
            return arf[:, dk_dqs + off:dk_dqs + off + w]

        def dqo_ap(s, w):
            return arena[:, dk_dqo[s]:dk_dqo[s] + w]

        def pk_ap(i, off, w):
            return aru[:, dk_pk[i] + off:dk_pk[i] + off + w]

        awb_ap = lambda: arena[:, dk_awb:dk_awb + 4 * D]
        bv_row_ap = lambda: arf[0:1, off_v // 2:off_v // 2 + D]
        bv_rep_ap = lambda: arena[:, off_bvrep:off_bvrep + D]
        maskT_ap = lambda: arena[:, off_mask:off_mask + NTQ * TB]
        smask_ap = lambda: arf[:, off_smask // 2:off_smask // 2 + TB]

        psA = ctx.enter_context(nc.psum_tensor("psA", [128, 512], F32))
        psB = ctx.enter_context(nc.psum_tensor("psB", [128, 512], F32))
        psC = ctx.enter_context(nc.psum_tensor("psC", [128, 512], F32))
        psD = ctx.enter_context(nc.psum_tensor("psD", [128, 512], F32))
        psE = ctx.enter_context(nc.psum_tensor("psE", [128, 512], F32))
        psF = ctx.enter_context(nc.psum_tensor("psF", [128, 512], F32))
        pst1 = ctx.enter_context(nc.psum_tensor("pst1", [128, 512], F32))
        pst2 = ctx.enter_context(nc.psum_tensor("pst2", [128, 512], F32))
        pqo = [psC, psD, psE, psF][:NTQ]

        s_pk = Sem(ctx, nc, "s_pk")
        s_in = Sem(ctx, nc, "s_in")
        s_pq = Sem(ctx, nc, "s_pq")
        s_vo = Sem(ctx, nc, "s_vo")
        s_k1 = Sem(ctx, nc, "s_k1")
        s_b1 = Sem(ctx, nc, "s_b1")
        s_b2 = Sem(ctx, nc, "s_b2")
        s_s0 = Sem(ctx, nc, "s_s0")
        s_s1 = Sem(ctx, nc, "s_s1")
        s_r0 = Sem(ctx, nc, "s_r0")
        s_r1 = Sem(ctx, nc, "s_r1")
        s_w0 = Sem(ctx, nc, "s_w0")
        s_w1 = Sem(ctx, nc, "s_w1")
        s_q0 = Sem(ctx, nc, "s_q0")
        s_q1 = Sem(ctx, nc, "s_q1")
        s_kv = Sem(ctx, nc, "s_kv")
        s_ao = Sem(ctx, nc, "s_ao")
        s_bw = Sem(ctx, nc, "s_bw")
        s_aw = Sem(ctx, nc, "s_aw")
        s_cc = Sem(ctx, nc, "s_cc")
        s_pe = Sem(ctx, nc, "s_pe")
        s_ac = Sem(ctx, nc, "s_ac")
        s_dv = Sem(ctx, nc, "s_dv")
        s_gp = Sem(ctx, nc, "s_gp")
        sems = [s_pk, s_in, s_pq, s_vo, s_k1, s_b1, s_b2, s_s0, s_s1, s_r0, s_r1, s_w0, s_w1, s_q0, s_q1,
                s_kv, s_ao, s_bw, s_aw, s_cc, s_pe, s_ac, s_dv, s_gp]
        sS = [s_s0, s_s1]
        sR = [s_r0, s_r1]
        sW = [s_w0, s_w1]
        sQ = [s_q0, s_q1]

        b = B(sems)
        M = b.M

        WGU_v = [t.ap().rearrange("(c q p) n -> p c q n", q=4, p=128) for t in WGU_t]
        WDN_v = [WDN_t[j].ap().rearrange("(c pl hc p) m -> p c pl hc m", pl=2, hc=8, p=128)
                 for j in range(2)]
        WATkc = WAT_t.ap().rearrange("(kc p) e -> p kc e", p=128)

        def gu_view(half):
            return strm[:, half * 8192:half * 8192 + 8192].rearrange(
                "p (c q n) -> p c q n", c=8, q=4)

        def dn_view(half):
            return strm[:, half * 8192:half * 8192 + 2 * AN * 128].rearrange(
                "p (pl m) -> p pl m", pl=2)

        def at_view(half):
            return strm[:, half * 8192:half * 8192 + 16 * 512].rearrange(
                "p (kc e) -> p kc e", kc=16)

        def inter_ap(g):
            return arena[:, g * TB:(g + 1) * TB]

        RG = [list(range(8))]

        def program():
            for i, val in enumerate([EXP_BIAS, LN_EPS, 1.0, 0.5, -2.0, -6.0, -1.0, -3.0]):
                b.op("gpsimd", lambda e, i=i, val=val: e.memset(cb[:, i:i + 1], val), s_gp)
            b.op("gpsimd", lambda e: e.memset(ones_col[:, :], 1.0), s_gp)
            b.op("gpsimd", lambda e: e.memset(ones_row[:, :], 1.0), s_gp)
            b.op("gpsimd", lambda e: e.memset(onesv[:, :], 1.0), s_gp)
            M["idz"] = b.op("gpsimd", lambda e: e.memset(idf[:, :], 0.0), s_gp)
            b.wait("gpsimd", s_gp, "idz")
            M["idsel"] = b.op("gpsimd", lambda e: e.affine_select(
                out=idf[:, :], in_=idf[:, :], compare_op=ALU.not_equal, fill=1.0,
                base=0, pattern=[[-1, 128]], channel_multiplier=1), s_gp)
            M["consts"] = s_gp.n
            for en in ("tensor", "vector", "scalar", "sync"):
                b.wait(en, s_gp, "consts")
            M["idbf"] = b.op("vector", lambda e: e.tensor_copy(out=idbf[:, :], in_=idf[:, :]), s_dv)

            b.op("sync", lambda e: e.dma_start(out=bqk_sb[:, :], in_=bqk), s_in, 16)
            b.op("sync", lambda e: e.dma_start(out=bo_sb[:, :], in_=bo), s_in, 16)
            M["bias_ld"] = s_in.n
            b.wait("vector", s_in, "bias_ld")
            M["bqs"] = b.op("vector", lambda e: e.tensor_scalar(
                out=bqk_sb[:, 0:16], in0=bqk_sb[:, 0:16], scalar1=QK_SCALE,
                scalar2=None, op0=ALU.mult), s_dv)

            blob_at = blob_at_t.ap()

            st = {"hist": [], "pkuse": [None, None], "ss": 0, "so": 0}

            def dq_pk_load(mat, bi_, pk_src, c0, w):
                if st["pkuse"][bi_] is not None:
                    b.wait("scalar", s_dv, st["pkuse"][bi_])
                M[f"pk_{mat}_{c0}"] = b.op("scalar", lambda e, pk_src=pk_src, bi_=bi_, c0=c0, w=w: e.dma_start(
                    out=pk_ap(bi_, 0, w), in_=pk_src[:, c0:c0 + w]), [s_pk, s_pq][bi_], 16)

            def dq_sc_load(mat, sc_src, c0, w, first=False):
                if st["hist"]:
                    b.wait("scalar", s_gp, f"v4_{st['hist'][-1][0]}")
                elif first:
                    b.wait("scalar", s_dv, "bqs")
                M[f"sc_{mat}_{c0}"] = b.op("scalar", lambda e, sc_src=sc_src, c0=c0, w=w: e.dma_start(
                    out=dqs_ap(0, w), in_=sc_src[:, c0:c0 + w]), s_in, 16)

            def dq_chunk(mat, bi_, c0, w, out_fn):
                for coff in range(0, w, DW):
                    sw = min(DW, w - coff)
                    for pl in range(2):
                        tag = f"{mat}_{c0 + coff}_{pl}"
                        ss = st["ss"]
                        st["ss"] = (ss + 1) % N_SETS
                        so = st["so"]
                        st["so"] = (so + 1) % N_OSETS
                        sbs = [s_bw, s_b1, s_b2]
                        hist = st["hist"]
                        ptN = hist[-N_SETS][0] if len(hist) >= N_SETS else None
                        ptO = hist[-N_OSETS][0] if len(hist) >= N_OSETS else None
                        scm = 1.0 if pl == 0 else (1.0 / 16.0)
                        sa = 0.5 if mat == "dn" else 1.0
                        mm_, sm_ = (7, 8) if pl == 0 else (0x70, 0x80)
                        b.wait("vector", [s_pk, s_pq][bi_], f"pk_{mat}_{c0}")
                        if ptN is not None:
                            b.wait("vector", s_ac, f"dqa_{ptN}")
                        M[f"nib_{tag}"] = b.op("vector", lambda e, bi_=bi_, coff=coff, w=sw, ss=ss, mm_=mm_, sm_=sm_: (
                            e.tensor_scalar(out=dqu_ap(ss, 0, w), in0=pk_ap(bi_, coff, w),
                                            scalar1=mm_, scalar2=None, op0=ALU.bitwise_and),
                            e.tensor_scalar(out=dqu_ap(ss, 1, w), in0=pk_ap(bi_, coff, w),
                                            scalar1=sm_, scalar2=None, op0=ALU.bitwise_and))[-1], s_dv)
                        b.wait("scalar", s_dv, f"nib_{tag}")
                        b.wait("scalar", s_gp, "consts")
                        if ptN is not None:
                            b.wait("scalar", s_gp, f"v4_{ptN}")
                            b.wait("scalar", s_dv, f"v3_{ptN}")
                        M[f"dqa_{tag}"] = b.op("scalar", lambda e, w=sw, ss=ss, scm=scm, sa=sa: (
                            e.activation(out=dqf_ap(ss, 0, w), in_=dqu_ap(ss, 0, w), func=AF.Copy, scale=0.5 * scm * sa),
                            e.activation(out=dqf_ap(ss, 2, w), in_=dqu_ap(ss, 0, w), func=AF.Relu, scale=0.5 * scm * sa, bias=-2.0 * sa),
                            e.activation(out=dqf_ap(ss, 3, w), in_=dqu_ap(ss, 0, w), func=AF.Relu, scale=scm * sa, bias=-6.0 * sa),
                            e.activation(out=dqf_ap(ss, 1, w), in_=dqu_ap(ss, 1, w), func=AF.Copy,
                                         scale=-0.25 * scm, bias=1.0))[-1], s_ac)
                        b.wait("vector", s_ac, f"dqa_{tag}")
                        M[f"v1_{tag}"] = b.op("vector", lambda e, w=sw, ss=ss: e.tensor_add(
                            out=dqf_ap(ss, 0, w), in0=dqf_ap(ss, 0, w), in1=dqf_ap(ss, 2, w)), s_dv)
                        b.wait("vector", s_dv, f"v1_{tag}")
                        M[f"v2_{tag}"] = b.op("vector", lambda e, w=sw, ss=ss: e.tensor_add(
                            out=dqf_ap(ss, 0, w), in0=dqf_ap(ss, 0, w), in1=dqf_ap(ss, 3, w)), s_dv)
                        b.wait("vector", s_dv, f"v2_{tag}")
                        M[f"v3_{tag}"] = b.op("vector", lambda e, w=sw, ss=ss: e.tensor_mul(
                            out=dqf_ap(ss, 0, w), in0=dqf_ap(ss, 0, w), in1=dqf_ap(ss, 1, w)), s_dv)
                        b.wait("gpsimd", s_dv, f"v3_{tag}")
                        b.wait("gpsimd", s_in, f"sc_{mat}_{c0}")
                        if ptO is not None:
                            b.wait("gpsimd", sbs[so], f"dq_wr_{ptO}")
                        M[f"v4_{tag}"] = b.op("gpsimd", lambda e, w=sw, ss=ss, so=so, coff=coff: e.tensor_mul(
                            out=dqo_ap(so, w), in0=dqf_ap(ss, 0, w), in1=dqs_ap(coff, w)), s_gp)
                        b.wait("gpsimd", s_gp, f"v4_{tag}")
                        fns = out_fn(pl, c0 + coff, sw, so)
                        if not isinstance(fns, list):
                            fns = [fns]
                        for f_ in fns:
                            M[f"dq_wr_{tag}"] = b.op("gpsimd", f_, sbs[so], 16)
                        hist.append((tag, ss, so))
                st["pkuse"][bi_] = f"nib_{mat}_{c0 + w - DW}_1"

            def gu_out(ch, rowbase):
                def f(pl, c0, w, so):
                    cr = c0 - ch * NCHC
                    return lambda e: e.dma_start(
                        out=blob_gu_t[ch].ap()[rowbase + 128 * pl: rowbase + 128 * pl + 128,
                                               cr:cr + w],
                        in_=dqo_ap(so, w))
                return f

            def dn_out(pl, c0, w, so):
                a0, na = c0 // D, w // D
                dviews = [blob_dn_t[j].ap().rearrange("(pl hc p) m -> pl hc p m", pl=2, hc=8)
                          for j in range(2)]
                return [
                    (lambda e, ai=ai, j=j: e.dma_start(
                        out=dviews[j][pl, :, :, (a0 + ai) * 128:(a0 + ai + 1) * 128]
                            .rearrange("hc p hh -> p hc hh"),
                        in_=dqo_ap(so, w).rearrange("p (a hc hh) -> p a hc hh", a=na, hc=16)[:, ai, 8 * j:8 * j + 8, :]))
                    for ai in range(na) for j in range(2)]

            def dec_gu(ch):
                for mi, (mat, pk_src, sc_src, rowb) in enumerate(
                        [("g", gpk, gsc, 0), ("u", upk, usc, 256)]):
                    c0 = ch * NCHC
                    dq_pk_load(mat, mi, pk_src, c0, NCHC)
                    dq_sc_load(mat, sc_src, c0, NCHC, first=(ch == 0 and mi == 0))
                    dq_chunk(mat, mi, c0, NCHC, gu_out(ch, rowb))
                M[f"gu_dec0_{ch}"] = s_bw.n
                M[f"gu_dec1_{ch}"] = s_b1.n
                M[f"gu_dec2_{ch}"] = s_b2.n
                b.wait("gpsimd", s_bw, f"gu_dec0_{ch}")
                b.wait("gpsimd", s_b1, f"gu_dec1_{ch}")
                b.wait("gpsimd", s_b2, f"gu_dec2_{ch}")
                M[f"cc_gu_{ch}"] = b.op("gpsimd", lambda e, ch=ch: e.collective_compute(
                    "AllGather", ALU.bypass, replica_groups=RG,
                    ins=[blob_gu_t[ch].ap()], outs=[WGU_t[ch].ap()]), s_cc)

            ND = (AN * D + NCH - 1) // NCH

            def dec_dn(ch):
                c0, w = ch * NCH, min(NCH, AN * D - ch * NCH)
                dq_pk_load("dn", ch % 2, dpk, c0, w)
                dq_sc_load("dn", dsc, c0, w)
                dq_chunk("dn", ch % 2, c0, w, dn_out)
                if ch == ND - 1:
                    M["dn_dec0"] = s_bw.n
                    M["dn_dec1"] = s_b1.n
                    M["dn_dec2"] = s_b2.n
                    b.wait("gpsimd", s_bw, "dn_dec0")
                    b.wait("gpsimd", s_b1, "dn_dec1")
                    b.wait("gpsimd", s_b2, "dn_dec2")
                    for j in range(2):
                        M[f"cc_dn_{j}"] = b.op("gpsimd", lambda e, j=j: e.collective_compute(
                            "AllGather", ALU.bypass, replica_groups=RG,
                            ins=[blob_dn_t[j].ap()], outs=[WDN_t[j].ap()]), s_cc)

            def dec_aw():
                for hl in range(2):
                    if hl == 0:
                        b.wait("gpsimd", s_dv, st["pkuse"][0])
                        b.wait("gpsimd", s_dv, st["pkuse"][1])
                        M["aw_pre0"] = s_bw.n
                        M["aw_pre1"] = s_b1.n
                        M["aw_pre2"] = s_b2.n
                        b.wait("gpsimd", s_bw, "aw_pre0")
                        b.wait("gpsimd", s_b1, "aw_pre1")
                        b.wait("gpsimd", s_b2, "aw_pre2")
                    M[f"aw_ld{hl}"] = b.op("gpsimd", lambda e, hl=hl: e.dma_start(
                        out=awb_ap(), in_=aw[128 * hl:128 * hl + 128, :]), s_aw, 16)
                    b.wait("gpsimd", s_aw, f"aw_ld{hl}")
                    M[f"aw_wr{hl}"] = b.op("gpsimd", lambda e, hl=hl: e.dma_start(
                        out=blob_at[128 * hl:128 * hl + 128, :], in_=awb_ap()), s_bw, 16)
                    b.wait("gpsimd", s_bw, f"aw_wr{hl}")
                M["cc_at"] = b.op("gpsimd", lambda e: e.collective_compute(
                    "AllGather", ALU.bypass, replica_groups=RG,
                    ins=[blob_at_t.ap()], outs=[WAT_t.ap()]), s_cc)

            def ffn0_hook(sp):
                if DEC_INLINE:
                    return
                ch = sp // 2
                if sp % 2 == 0 and ch < ND:
                    dec_dn(ch)
                elif sp == 2 * ND:
                    dec_aw()

            def emit_phase0():
                emit_R_loads(0, xb[0], [[], []])
                layer_norm(0, None, None)
                for ch in range(NCHK):
                    dec_gu(ch)
                if DEC_INLINE:
                    for ch in range(ND):
                        dec_dn(ch)
                    dec_aw()

            def phase_barrier(ph):
                if ph == 0:
                    return
                M[f"bar_dv_{ph}"] = s_dv.n
                M[f"bar_ac_{ph}"] = s_ac.n
                b.wait("tensor", s_dv, f"bar_dv_{ph}")
                b.wait("tensor", s_ac, f"bar_ac_{ph}")

            def emit_R_loads(ph, src, prev_keys):
                for idx in range(32):
                    j = idx % 16
                    par = idx % 2
                    if idx >= 2:
                        if idx - 2 < 16:
                            b.wait("sync", s_pe, f"sx2_{ph}_{idx - 2}")
                            b.wait("sync", s_ac, f"sq_{ph}_{idx - 2}")
                        else:
                            b.wait("sync", s_dv, f"nm1_{ph}_{idx - 18}")
                    else:
                        for (sem, key) in prev_keys[idx]:
                            b.wait("sync", sem, key)
                    M[f"r_ld_{ph}_{idx}"] = b.op("sync", lambda e, j=j, par=par, src=src: e.dma_start(
                        out=rbuf[par][:, :], in_=src[128 * j:128 * j + 128, :]), sR[par], 16)

            def emit_res_loads(ph, src, prev_keys):
                for hc in range(16):
                    par = hc % 2
                    if hc >= 2:
                        b.wait("sync", s_dv, f"resfree_{ph}_{hc - 2}")
                    else:
                        for (sem, key) in prev_keys[hc]:
                            b.wait("sync", sem, key)
                    M[f"res_ld_{ph}_{hc}"] = b.op("sync", lambda e, hc=hc, par=par, src=src: e.dma_start(
                        out=rbuf[par][:, :], in_=src[128 * hc:128 * hc + 128, :]), sR[par], 16)

            def layer_norm(ph, prev_xn_key, prev_lnt_keys):
                phase_barrier(ph)
                for j in range(16):
                    par = j % 2
                    b.wait("scalar", sR[par], f"r_ld_{ph}_{j}")
                    if j >= 1:
                        b.wait("scalar", s_pe, f"sx2_{ph}_{j - 1}")
                    M[f"sq_{ph}_{j}"] = b.op("scalar", lambda e, par=par: e.activation(
                        out=sqb[:, :], in_=rbuf[par][:, :], func=AF.Square), s_ac)
                    b.wait("tensor", sR[par], f"r_ld_{ph}_{j}")
                    M[f"sx_{ph}_{j}"] = b.op("tensor", lambda e, j=j, par=par: e.matmul(
                        pst1[0:1, 0:TB], ones_col[:, :], rbuf[par][:, :],
                        start=(j == 0), stop=(j == 15), skip_group_check=True), s_pe)
                    b.wait("tensor", s_ac, f"sq_{ph}_{j}")
                    M[f"sx2_{ph}_{j}"] = b.op("tensor", lambda e, j=j: e.matmul(
                        pst2[0:1, 0:TB], ones_col[:, :], sqb[:, :],
                        start=(j == 0), stop=(j == 15), skip_group_check=True), s_pe)
                st_ = stat
                b.wait("vector", s_pe, f"sx2_{ph}_15")
                M[f"st1_{ph}"] = b.op("vector", lambda e: e.tensor_scalar(
                    out=st_[:, 0:TB], in0=pst1[0:1, 0:TB], scalar1=1.0 / D,
                    scalar2=None, op0=ALU.mult), s_dv)
                M[f"st2_{ph}"] = b.op("vector", lambda e: e.tensor_scalar(
                    out=st_[:, TB:2 * TB], in0=pst2[0:1, 0:TB], scalar1=1.0 / D,
                    scalar2=None, op0=ALU.mult), s_dv)
                b.wait("vector", s_dv, f"st2_{ph}")
                M[f"st3_{ph}"] = b.op("vector", lambda e: e.tensor_mul(
                    out=st_[:, 2 * TB:3 * TB], in0=st_[:, 0:TB], in1=st_[:, 0:TB]), s_dv)
                b.wait("vector", s_dv, f"st3_{ph}")
                M[f"st4_{ph}"] = b.op("vector", lambda e: e.tensor_sub(
                    out=st_[:, TB:2 * TB], in0=st_[:, TB:2 * TB], in1=st_[:, 2 * TB:3 * TB]), s_dv)
                b.wait("scalar", s_dv, f"st4_{ph}")
                M[f"st5_{ph}"] = b.op("scalar", lambda e: e.activation(
                    out=st_[:, 2 * TB:3 * TB], in_=st_[:, TB:2 * TB], func=AF.Ln,
                    bias=LN_EPS), s_ac)
                b.wait("scalar", s_ac, f"st5_{ph}")
                M[f"st6_{ph}"] = b.op("scalar", lambda e: e.activation(
                    out=st_[:, 3 * TB:4 * TB], in_=st_[:, 2 * TB:3 * TB], func=AF.Exp,
                    scale=-0.5), s_ac)
                b.wait("vector", s_ac, f"st6_{ph}")
                M[f"st7_{ph}"] = b.op("vector", lambda e: e.tensor_mul(
                    out=st_[:, 0:TB], in0=st_[:, 0:TB], in1=st_[:, 3 * TB:4 * TB]), s_dv)
                b.wait("vector", s_dv, f"st7_{ph}")
                M[f"st8_{ph}"] = b.op("vector", lambda e: e.tensor_scalar(
                    out=st_[:, 0:TB], in0=st_[:, 0:TB], scalar1=-1.0,
                    scalar2=None, op0=ALU.mult), s_dv)
                b.wait("tensor", s_ac, f"st6_{ph}")
                b.wait("tensor", s_dv, f"st8_{ph}")
                M[f"rep1_{ph}"] = b.op("tensor", lambda e: e.matmul(
                    psA[:, 0:TB], ones_row[:, :], stat[0:1, 3 * TB:4 * TB],
                    start=True, stop=True, skip_group_check=True), s_pe)
                M[f"rep2_{ph}"] = b.op("tensor", lambda e: e.matmul(
                    psB[:, 0:TB], ones_row[:, :], stat[0:1, 0:TB],
                    start=True, stop=True, skip_group_check=True), s_pe)
                b.wait("vector", s_pe, f"rep2_{ph}")
                M[f"arep_{ph}"] = b.op("vector", lambda e: e.tensor_copy(
                    out=arep[:, :], in_=psA[:, 0:TB]), s_dv)
                b.wait("vector", s_dv, f"arep_{ph}")
                M[f"brepc_{ph}"] = b.op("vector", lambda e: e.tensor_copy(
                    out=brep[:, :], in_=psB[:, 0:TB]), s_dv)
                for j in range(16):
                    jj = 16 + j
                    par = jj % 2
                    b.wait("vector", sR[par], f"r_ld_{ph}_{jj}")
                    b.wait("vector", s_dv, f"brepc_{ph}")
                    if j < 2 and prev_lnt_keys is not None:
                        for (sem, key) in prev_lnt_keys[j]:
                            b.wait("vector", sem, key)
                    if j >= 2:
                        b.wait("vector", s_dv, f"nm2_{ph}_{j - 2}")
                    M[f"nm1_{ph}_{j}"] = b.op("vector", lambda e, j=j, par=par: e.tensor_mul(
                        out=lnt[j % 2][:, :], in0=rbuf[par][:, :], in1=arep[:, :]), s_dv)
                    b.wait("vector", s_dv, f"nm1_{ph}_{j}")
                    if j == 0 and prev_xn_key is not None:
                        b.wait("vector", s_pe, prev_xn_key)
                    M[f"nm2_{ph}_{j}"] = b.op("vector", lambda e, j=j: e.tensor_add(
                        out=x_n[:, j * TB:(j + 1) * TB], in0=lnt[j % 2][:, :],
                        in1=brep[:, :]), s_dv)
                M[f"xn_{ph}"] = s_dv.n

            def ffn(ph, src, dst, prev_gu, prev_dn, prev_xn, prev_r, prev_wr, hook=None,
                    pre_ln=False):
                if not pre_ln:
                    emit_R_loads(ph, src, prev_r)
                    layer_norm(ph, prev_xn, None)
                spc = NCHC // 256
                for sp in range(NSP):
                    if hook is not None:
                        hook(sp)
                    half = sp % 2
                    kch = sp // spc
                    if sp >= 2:
                        b.wait("sync", s_pe, f"gu_fin_{ph}_{sp - 2}")
                    else:
                        for (sem, key) in prev_gu[sp]:
                            b.wait("sync", sem, key)
                    if ph == 0 and (sp % spc) < 2:
                        b.wait("sync", s_cc, f"cc_gu_{kch}")
                    M[f"gu_ld_{ph}_{sp}"] = b.op("sync", lambda e, sp=sp, half=half, kch=kch: e.dma_start(
                        out=gu_view(half),
                        in_=WGU_v[kch][:, :, :, 256 * sp - kch * NCHC: 256 * sp - kch * NCHC + 256]), sS[half], 16)
                    b.wait("tensor", sS[half], f"gu_ld_{ph}_{sp}")
                    b.wait("tensor", s_dv, f"xn_{ph}")
                    for par2 in range(2):
                        g = 2 * sp + par2
                        pg, pu = (psA, psB) if par2 == 0 else (psE, psF)
                        if g >= 2:
                            b.wait("tensor", s_ac, f"gel_{ph}_{g - 2}")
                            b.wait("tensor", s_dv, f"inter_{ph}_{g - 2}")
                        else:
                            b.wait("tensor", s_dv, f"arep_{ph}")
                            b.wait("tensor", s_dv, f"brepc_{ph}")
                        for kc in range(16):
                            c0, pl = kc // 2, kc % 2
                            r = b.op("tensor", lambda e, half=half, c0=c0, pl=pl, par2=par2, kc=kc, pg=pg: e.matmul(
                                pg[:, 0:TB], gu_view(half)[:, c0, pl, par2:256:2],
                                x_n[:, (2 * c0 + pl) * TB:(2 * c0 + pl + 1) * TB],
                                start=(kc == 0), stop=(kc == 15), skip_group_check=True),
                                s_pe if kc == 15 else None)
                        M[f"gmm_{ph}_{g}"] = r
                        for kc in range(16):
                            c0, pl = kc // 2, kc % 2
                            r = b.op("tensor", lambda e, half=half, c0=c0, pl=pl, par2=par2, kc=kc, pu=pu: e.matmul(
                                pu[:, 0:TB], gu_view(half)[:, c0, 2 + pl, par2:256:2],
                                x_n[:, (2 * c0 + pl) * TB:(2 * c0 + pl + 1) * TB],
                                start=(kc == 0), stop=(kc == 15), skip_group_check=True),
                                s_pe if kc == 15 else None)
                        M[f"umm_{ph}_{g}"] = r
                        b.wait("scalar", s_pe, f"gmm_{ph}_{g}")
                        if g >= 2:
                            b.wait("scalar", s_dv, f"inter_{ph}_{g - 2}")
                        M[f"gel_{ph}_{g}"] = b.op("scalar", lambda e, g=g, pg=pg: e.activation(
                            out=gelu_b[g % 2][:, :], in_=pg[:, 0:TB],
                            func=AF.Gelu_apprx_tanh), s_ac)
                        b.wait("vector", s_ac, f"gel_{ph}_{g}")
                        b.wait("vector", s_pe, f"umm_{ph}_{g}")
                        M[f"inter_{ph}_{g}"] = b.op("vector", lambda e, g=g, pu=pu: e.tensor_tensor(
                            out=inter_ap(g), in0=pu[:, 0:TB], in1=gelu_b[g % 2][:, :],
                            op=ALU.mult), s_dv)
                    M[f"gu_fin_{ph}_{sp}"] = M[f"umm_{ph}_{2 * sp + 1}"]
                for hc in range(16):
                    par = hc % 2
                    if hc >= 2:
                        b.wait("sync", s_dv, f"resfree_{ph}_{hc - 2}")
                    else:
                        b.wait("sync", s_dv, f"nm1_{ph}_{14 + hc}")
                    M[f"res_ld_{ph}_{hc}"] = b.op("sync", lambda e, hc=hc, par=par, src=src: e.dma_start(
                        out=rbuf[par][:, :], in_=src[128 * hc:128 * hc + 128, :]), sR[par], 16)
                    pdn = psA if hc % 2 == 0 else psE
                    for dsp in range(8):
                        idx = hc * 8 + dsp
                        half = idx % 2
                        if idx >= 2:
                            b.wait("sync", s_pe, f"dn_fin_{ph}_{idx - 2}")
                        else:
                            for (sem, key) in prev_dn[idx]:
                                b.wait("sync", sem, key)
                        if ph == 0 and hc == 8 and dsp < 2:
                            b.wait("sync", s_cc, "cc_dn_1")
                        M[f"dn_ld_{ph}_{idx}"] = b.op("sync", lambda e, hc=hc, dsp=dsp, half=half: e.dma_start(
                            out=dn_view(half), in_=WDN_v[hc // 8][:, dsp, :, hc % 8, :]), sS[half], 16)
                    if hc == 0:
                        b.wait("tensor", s_dv, f"inter_{ph}_{NG - 1}")
                        b.wait("tensor", s_dv, f"inter_{ph}_{NG - 2}")
                        b.wait("tensor", s_ac, f"gel_{ph}_{NG - 1}")
                        b.wait("tensor", s_ac, f"gel_{ph}_{NG - 2}")
                    if hc >= 2:
                        b.wait("tensor", s_dv, f"resadd_{ph}_{hc - 2}")
                    first = True
                    for dsp in range(8):
                        idx = hc * 8 + dsp
                        half = idx % 2
                        b.wait("tensor", sS[half], f"dn_ld_{ph}_{idx}")
                        for pl in range(2):
                            for a in range(AN):
                                g = 2 * (dsp * AN + a) + pl
                                stop_ = (dsp == 7 and pl == 1 and a == AN - 1)
                                mark = (pl == 1 and a == AN - 1)
                                r = b.op("tensor", lambda e, half=half, pl=pl, a=a, g=g, pdn=pdn, first=first, stop_=stop_: e.matmul(
                                    pdn[:, 0:TB], dn_view(half)[:, pl, a * 128:(a + 1) * 128],
                                    inter_ap(g), start=first, stop=stop_,
                                    skip_group_check=True), s_pe if mark else None)
                                first = False
                        M[f"dn_fin_{ph}_{idx}"] = r
                    M[f"dn_mm_{ph}_{hc}"] = r
                    par = hc % 2
                    b.wait("vector", s_pe, f"dn_mm_{ph}_{hc}")
                    b.wait("vector", sR[par], f"res_ld_{ph}_{hc}")
                    if hc >= 2:
                        b.wait("vector", sW[par], f"rwr_{ph}_{hc - 2}")
                    else:
                        for (sem, key) in prev_wr[hc]:
                            b.wait("vector", sem, key)
                    M[f"resadd_{ph}_{hc}"] = b.op("vector", lambda e, par=par, pdn=pdn: e.tensor_add(
                        out=outb[par][:, :], in0=pdn[:, 0:TB], in1=rbuf[par][:, :]), s_dv)
                    M[f"resfree_{ph}_{hc}"] = M[f"resadd_{ph}_{hc}"]
                    b.wait("gpsimd", s_dv, f"resadd_{ph}_{hc}")
                    M[f"rwr_{ph}_{hc}"] = b.op("gpsimd", lambda e, hc=hc, par=par, dst=dst: e.dma_start(
                        out=dst[128 * hc:128 * hc + 128, :], in_=outb[par][:, :]), sW[par], 16)
                M[f"ffn_wr_tot0_{ph}"] = s_w0.n
                M[f"ffn_wr_tot1_{ph}"] = s_w1.n

            R1 = R1_t.ap()
            R2 = R2_t.ap()

            emit_phase0()
            ffn(0, xb[0], R1[0],
                prev_gu=[[], []],
                prev_dn=[[(s_cc, "cc_dn_0"), (s_pe, "gu_fin_0_%d" % (NSP - 2))],
                         [(s_cc, "cc_dn_0"), (s_pe, "gu_fin_0_%d" % (NSP - 1))]],
                prev_xn=None,
                prev_r=[[], []],
                prev_wr=[[], []],
                hook=ffn0_hook,
                pre_ln=True)
            ffn(1, xb[1], R1[1],
                prev_gu=[[(s_pe, "dn_fin_0_126")], [(s_pe, "dn_fin_0_127")]],
                prev_dn=[[(s_pe, "gu_fin_1_%d" % (NSP - 2))],
                         [(s_pe, "gu_fin_1_%d" % (NSP - 1))]],
                prev_xn="gu_fin_0_%d" % (NSP - 1),
                prev_r=[[(s_dv, "resadd_0_14")], [(s_dv, "resadd_0_15")]],
                prev_wr=[[(s_w0, "rwr_0_14")], [(s_w1, "rwr_0_15")]])

            for bi in range(2):
                ph = 10 + bi
                b.wait("sync", sW[0], f"ffn_wr_tot0_{bi}")
                b.wait("sync", sW[1], f"ffn_wr_tot1_{bi}")
                if bi == 0:
                    b.wait("sync", s_dv, "resadd_1_15")
                    b.wait("sync", s_pe, "dn_mm_1_15")
                    M["bvr_ld"] = b.op("sync", lambda e: e.dma_start(
                        out=bv_row_ap(), in_=bvr), s_in, 16)
                    b.wait("tensor", s_in, "bvr_ld")
                    for i in range(D // 512):
                        pg2 = psA if i % 2 == 0 else psB
                        if i >= 2:
                            b.wait("tensor", s_dv, f"bvcp{i-2}")
                        M[f"bvrep{i}"] = b.op("tensor", lambda e, i=i, pg2=pg2: e.matmul(
                            pg2[:, :], ones_row[:, :], bv_row_ap()[:, 512 * i:512 * i + 512],
                            start=True, stop=True, skip_group_check=True), s_pe)
                        b.wait("vector", s_pe, f"bvrep{i}")
                        M[f"bvcp{i}"] = b.op("vector", lambda e, i=i, pg2=pg2: e.tensor_copy(
                            out=bv_rep_ap()[:, 512 * i:512 * i + 512], in_=pg2[:, :]), s_dv)
                    M["bv_done"] = s_dv.n
                emit_R_loads(ph, R1[bi],
                             [[(s_dv, "resadd_1_14")], [(s_dv, "resadd_1_15")]] if bi == 0
                             else [[(s_dv, "nm1_10_14")], [(s_dv, "nm1_10_15")]])
                layer_norm(ph, prev_xn_key=("gu_fin_1_%d" % (NSP - 1)) if bi == 0 else "aw_fin_0_11",
                           prev_lnt_keys=None)
                for eb in range(12):
                    half = eb % 2
                    if eb >= 2:
                        b.wait("sync", s_pe, f"aw_fin_{bi}_{eb - 2}")
                    elif bi == 0:
                        b.wait("sync", s_cc, "cc_at")
                        b.wait("sync", s_pe, f"dn_fin_1_{126 + eb}")
                    else:
                        b.wait("sync", s_pe, f"aw_fin_0_{10 + eb}")
                    M[f"aw_ld_{bi}_{eb}"] = b.op("sync", lambda e, eb=eb, half=half: e.dma_start(
                        out=at_view(half), in_=WATkc[:, :, 512 * eb:512 * eb + 512]), sS[half], 16)
                    b.wait("tensor", sS[half], f"aw_ld_{bi}_{eb}")
                    b.wait("tensor", s_dv, f"xn_{ph}")
                    if eb < 8:
                        for sub in range(4):
                            ec = eb * 4 + sub
                            pg = psA if ec % 2 == 0 else psB
                            if ec >= 2:
                                b.wait("tensor", s_dv, f"qkev_{bi}_{ec - 2}")
                            else:
                                b.wait("tensor", s_dv, f"arep_{ph}")
                                b.wait("tensor", s_dv, f"brepc_{ph}")
                            for kc in range(16):
                                r = b.op("tensor", lambda e, half=half, kc=kc, sub=sub, pg=pg: e.matmul(
                                    pg[:, 0:TB], at_view(half)[:, kc, sub * 128:(sub + 1) * 128],
                                    x_n[:, kc * TB:(kc + 1) * TB],
                                    start=(kc == 0), stop=(kc == 15), skip_group_check=True),
                                    s_pe if kc == 15 else None)
                            M[f"qkmm_{bi}_{ec}"] = r
                            b.wait("vector", s_pe, f"qkmm_{bi}_{ec}")
                            if ec >= 2:
                                b.wait("vector", sQ[ec % 2], f"qkwr_{bi}_{ec - 2}")
                            elif bi == 1:
                                b.wait("vector", sQ[ec % 2], f"vwr_0_{4 * NTQ - 2 + ec}")
                            isq = ec < 16
                            M[f"qkev_{bi}_{ec}"] = b.op("vector", lambda e, ec=ec, pg=pg, isq=isq: e.tensor_scalar(
                                out=evb[ec % 2][:, 0:TB], in0=pg[:, 0:TB],
                                scalar1=QK_SCALE if isq else 1.0,
                                scalar2=bqk_sb[:, ec:ec + 1],
                                op0=ALU.mult, op1=ALU.add), s_dv)
                            dstt = QIN[bi] if isq else KIN[bi]
                            row = (ec % 16) * 128
                            b.wait("sync", s_dv, f"qkev_{bi}_{ec}")
                            M[f"qkwr_{bi}_{ec}"] = b.op("sync", lambda e, dstt=dstt, row=row, ec=ec: e.dma_start(
                                out=dstt.ap()[row:row + 128, :], in_=evb[ec % 2][:, 0:TB]), sQ[ec % 2], 16)
                        M[f"aw_fin_{bi}_{eb}"] = M[f"qkmm_{bi}_{eb * 4 + 3}"]
                    else:
                        evs = eb - 8
                        for ts in range(NTQ):
                            vi = evs * NTQ + ts
                            pg = psA if vi % 2 == 0 else psB
                            if vi >= 2:
                                b.wait("tensor", s_dv, f"vev_{bi}_{vi - 2}")
                            elif evs == 0:
                                b.wait("tensor", s_dv, f"qkev_{bi}_30")
                                b.wait("tensor", s_dv, f"qkev_{bi}_31")
                            for kc in range(16):
                                r = b.op("tensor", lambda e, half=half, kc=kc, ts=ts, pg=pg: e.matmul(
                                    pg[0:TQ, 0:512],
                                    x_n[:, kc * TB + ts * TQ: kc * TB + (ts + 1) * TQ],
                                    at_view(half)[:, kc, :],
                                    start=(kc == 0), stop=(kc == 15), skip_group_check=True),
                                    s_pe if kc == 15 else None)
                            M[f"vmm_{bi}_{vi}"] = r
                            b.wait("vector", s_pe, f"vmm_{bi}_{vi}")
                            if vi >= 2:
                                b.wait("vector", sQ[vi % 2], f"vwr_{bi}_{vi - 2}")
                            else:
                                b.wait("vector", sQ[vi % 2], f"qkwr_{bi}_{30 + vi}")
                            M[f"vev_{bi}_{vi}"] = b.op("vector", lambda e, vi=vi, evs=evs, pg=pg: e.tensor_tensor(
                                out=evb[vi % 2][0:TQ, 0:512], in0=pg[0:TQ, 0:512],
                                in1=bv_rep_ap()[0:TQ, 512 * evs:512 * evs + 512], op=ALU.add), s_dv)
                            b.wait("sync", s_dv, f"vev_{bi}_{vi}")
                            M[f"vwr_{bi}_{vi}"] = b.op("sync", lambda e, bi=bi, vi=vi, evs=evs, ts=ts: e.dma_start(
                                out=VIN[bi].ap().rearrange("t (h s) -> t h s", s=VS)[
                                    ts * TQ:(ts + 1) * TQ, 4 * evs:4 * evs + 4, 0:HD],
                                in_=evb[vi % 2][0:TQ, 0:512].rearrange("t (h s) -> t h s", s=HD)), sQ[vi % 2], 16)
                        M[f"aw_fin_{bi}_{eb}"] = M[f"vmm_{bi}_{(eb - 8 + 1) * NTQ - 1}"]
                if bi == 1:
                    b.wait("sync", s_cc, "cc_v0")
                for ts in range(NTQ):
                    M[f"vones_{bi}"] = b.op("sync", lambda e, bi=bi, ts=ts: e.dma_start(
                        out=VIN[bi].ap().rearrange("(s p) (h v) -> p s h v", p=TQ, v=VS)[:, ts, :, HD:VS],
                        in_=onesv[0:TQ, 0:NH].unsqueeze(2),
                        allow_slow_non_contiguous=True), s_vo, 16)
                M[f"q0_tot_{bi}"] = sQ[0].n
                M[f"q1_tot_{bi}"] = sQ[1].n
                b.wait("gpsimd", sQ[0], f"q0_tot_{bi}")
                b.wait("gpsimd", sQ[1], f"q1_tot_{bi}")
                b.wait("gpsimd", s_vo, f"vones_{bi}")
                M[f"cc_q{bi}"] = b.op("gpsimd", lambda e, bi=bi: e.collective_compute(
                    "AllGather", ALU.bypass, replica_groups=RG,
                    ins=[QIN[bi].ap()], outs=[QOUT[bi].ap()]), s_cc)
                M[f"cc_k{bi}"] = b.op("gpsimd", lambda e, bi=bi: e.collective_compute(
                    "AllGather", ALU.bypass, replica_groups=RG,
                    ins=[KIN[bi].ap()], outs=[KOUT[bi].ap()]), s_cc)
                M[f"cc_v{bi}"] = b.op("gpsimd", lambda e, bi=bi: e.collective_compute(
                    "AllGather", ALU.bypass, replica_groups=RG,
                    ins=[VIN[bi].ap()], outs=[VOUT[bi].ap()]), s_cc)

            def kv_rank(gb):
                return gb if gb < 8 else 15 - gb

            def slot(gb):
                return (0 if gb < 8 else 8) + kv_rank(gb)

            def kj_off(j, gb):
                return off_k + (j * NBLK + slot(gb)) * TB

            def vj_off(j, gb):
                return off_v + (j * NBLK + slot(gb)) * NTQ * VS

            def qj_off(j, gb):
                return off_q + (j * NBLK + slot(gb)) * TB

            for hf in range(2):
                skv_h = s_kv if hf == 0 else s_k1
                b.wait("sync", s_cc, "cc_v0" if hf == 0 else "cc_v1")
                for j in range(2):
                    b.op("sync", lambda e, hf=hf, j=j: e.dma_start(
                        out=AR(off_k + (j * NBLK + hf * 8) * TB, 8 * TB)
                            .rearrange("p (r t) -> p r t", r=8),
                        in_=bass.AP(KOUT[hf],
                                    b.dynoff(e, 256 * TB, (128 * TB) * j),
                                    [[TB, HD], [D * TB, 8], [1, TB]])), skv_h, 16)
                    b.op("sync", lambda e, hf=hf, j=j: e.dma_start(
                        out=AR(off_v + (j * NBLK + hf * 8) * NTQ * VS, 8 * NTQ * VS)
                            .rearrange("p (rs v) -> p rs v", v=VS)[0:TQ],
                        in_=bass.AP(VOUT[hf],
                                    b.dynoff(e, 2 * VS, VS * j),
                                    [[NH * VS, TQ], [TQ * NH * VS, 8 * NTQ], [1, VS]])), skv_h, 16)
                for j in range(2):
                    b.op("sync", lambda e, hf=hf, j=j: e.dma_start(
                        out=AR(off_q + (j * NBLK + hf * 8) * TB, 8 * TB)
                            .rearrange("p (r t) -> p r t", r=8)[0:HD],
                        in_=bass.AP(QOUT[hf],
                                    b.dynoff(e, 256 * TB, (128 * TB) * j),
                                    [[TB, HD], [D * TB, 8], [1, TB]])), skv_h, 16)
                if hf == 0:
                    M["kv_half0"] = s_kv.n
            M["kv_all"] = s_k1.n

            phase_barrier(99)
            b.wait("gpsimd", s_dv, "resadd_1_15")
            b.wait("gpsimd", s_pe, "dn_mm_1_15")
            M["mz"] = b.op("gpsimd", lambda e: e.memset(maskT_ap(), 0.0), s_gp)
            b.wait("gpsimd", s_gp, "mz")
            for tkc in range(NTQ):
                M["mbuild"] = b.op("gpsimd", lambda e, tkc=tkc: e.affine_select(
                    out=maskT_ap()[:, tkc * TB:(tkc + 1) * TB],
                    in_=maskT_ap()[:, tkc * TB:(tkc + 1) * TB],
                    compare_op=ALU.is_ge, fill=MASKVAL, base=-tkc * TQ,
                    pattern=[[1, TB]], channel_multiplier=-1), s_gp)
            b.wait("tensor", s_kv, "kv_half0")
            for gb in range(NBLK):
                if gb == 8:
                    b.wait("tensor", s_k1, "kv_all")
                for j in range(2):
                    nch_ = (gb + 1) * NTQ
                    ci = 0
                    if gb + j > 0:
                        pj, pgb = (0, gb) if j == 1 else (1, gb - 1)
                        for tqs in range(NTQ):
                            b.wait("tensor", s_dv, f"aofin_{pgb}_{pj}_{tqs}")
                    for kv in range(gb + 1):
                        for tkc in range(NTQ):
                            pss = psA if ci % 2 == 0 else psB
                            half = ci % 2
                            if ci >= 2:
                                b.wait("tensor", s_ac, f"exp_{gb}_{j}_{ci - 2}")
                            elif gb + j > 0:
                                pj2, pgb2 = (0, gb) if j == 1 else (1, gb - 1)
                                pn2 = (pgb2 + 1) * NTQ
                                li = pn2 - 1 - ((pn2 - 1 - ci) % 2)
                                if li >= 0:
                                    b.wait("tensor", s_ac, f"exp_{pgb2}_{pj2}_{li}")
                            M[f"smm_{gb}_{j}_{ci}"] = b.op("tensor", lambda e, j=j, kv=kv, tkc=tkc, gb=gb, pss=pss: e.matmul(
                                pss[0:TQ, 0:TB],
                                arena[0:HD, kj_off(j, kv) + tkc * TQ: kj_off(j, kv) + (tkc + 1) * TQ],
                                AR(qj_off(j, gb), TB)[0:HD],
                                start=True, stop=True, skip_group_check=True), s_pe)
                            if ci >= 2:
                                b.wait("scalar", s_pe, f"pv_{gb}_{j}_{ci - 2}")
                            elif gb + j > 0:
                                pj2, pgb2 = (0, gb) if j == 1 else (1, gb - 1)
                                b.wait("scalar", s_pe, f"pvmm_{pgb2}_{pj2}")
                            if kv == gb:
                                b.wait("vector", s_pe, f"smm_{gb}_{j}_{ci}")
                                b.wait("vector", s_gp, "mbuild")
                                if tkc > 0:
                                    b.wait("vector", s_ac, f"exp_{gb}_{j}_{ci - 1}")
                                elif gb + j > 0:
                                    pj2, pgb2 = (0, gb) if j == 1 else (1, gb - 1)
                                    pn2 = (pgb2 + 1) * NTQ
                                    b.wait("vector", s_ac, f"exp_{pgb2}_{pj2}_{pn2 - 1}")
                                M[f"msk_{gb}_{j}_{ci}"] = b.op("vector", lambda e, tkc=tkc, pss=pss: e.tensor_add(
                                    out=smask_ap()[0:TQ, :], in0=pss[0:TQ, 0:TB],
                                    in1=maskT_ap()[0:TQ, tkc * TB:(tkc + 1) * TB]), s_dv)
                                b.wait("scalar", s_dv, f"msk_{gb}_{j}_{ci}")
                                M[f"exp_{gb}_{j}_{ci}"] = b.op("scalar", lambda e, half=half: e.activation(
                                    out=AR(off_p + half * TB, TB)[0:TQ, :],
                                    in_=smask_ap()[0:TQ, 0:TB], func=AF.Exp, bias=EXP_BIAS), s_ac)
                            else:
                                b.wait("scalar", s_pe, f"smm_{gb}_{j}_{ci}")
                                M[f"exp_{gb}_{j}_{ci}"] = b.op("scalar", lambda e, half=half, pss=pss: e.activation(
                                    out=AR(off_p + half * TB, TB)[0:TQ, :],
                                    in_=pss[0:TQ, 0:TB], func=AF.Exp, bias=EXP_BIAS), s_ac)
                            b.wait("tensor", s_ac, f"exp_{gb}_{j}_{ci}")
                            for tqs in range(NTQ):
                                r = b.op("tensor", lambda e, half=half, tqs=tqs, j=j, kv=kv, tkc=tkc, ci=ci, nch_=nch_: e.matmul(
                                    pqo[tqs][0:TQ, 0:VS],
                                    arena[0:TQ, off_p + half * TB + tqs * TQ: off_p + half * TB + (tqs + 1) * TQ],
                                    AR(vj_off(j, kv), NTQ * VS).rearrange("p (s v) -> p s v", v=VS)[0:TQ, tkc, :],
                                    start=(ci == 0), stop=(ci == nch_ - 1),
                                    skip_group_check=True),
                                    s_pe if tqs == NTQ - 1 else None)
                            M[f"pv_{gb}_{j}_{ci}"] = r
                            ci += 1
                    M[f"pvmm_{gb}_{j}"] = r
                    for tqs in range(NTQ):
                        b.wait("vector", s_pe, f"pvmm_{gb}_{j}")
                        if gb + j > 0:
                            pj, pgb = (0, gb) if j == 1 else (1, gb - 1)
                            b.wait("vector", s_ao, f"ao_grp_{pgb}_{pj}")
                        M[f"lc_{gb}_{j}_{tqs}"] = b.op("vector", lambda e, tqs=tqs: e.tensor_copy(
                            out=lrec[0:TQ, 0:1], in_=pqo[tqs][0:TQ, HD:HD + 1]), s_dv)
                        b.wait("vector", s_dv, f"lc_{gb}_{j}_{tqs}")
                        M[f"lr_{gb}_{j}_{tqs}"] = b.op("vector", lambda e: e.reciprocal(
                            out=lrec[0:TQ, 1:2], in_=lrec[0:TQ, 0:1]), s_dv)
                        b.wait("vector", s_dv, f"lr_{gb}_{j}_{tqs}")
                        M[f"aofin_{gb}_{j}_{tqs}"] = b.op("vector", lambda e, tqs=tqs, j=j: e.tensor_scalar(
                            out=AR(off_ao + j * NTQ * HD + tqs * HD, HD)[0:TQ, :],
                            in0=pqo[tqs][0:TQ, 0:HD], scalar1=lrec[0:TQ, 1:2],
                            scalar2=None, op0=ALU.mult), s_dv)
                        sl = tqs % 2
                        ptt = pst1 if sl == 0 else pst2
                        b.wait("tensor", s_dv, f"aofin_{gb}_{j}_{tqs}")
                        if gb + j + tqs > 0:
                            b.wait("tensor", s_dv, "aotc_prev")
                        M["aotr_cur"] = b.op("tensor", lambda e, tqs=tqs, j=j, ptt=ptt: e.transpose(
                            ptt[:, 0:128].bitcast(BF16)[:, 0:TQ],
                            AR(off_ao + j * NTQ * HD + tqs * HD, HD)[0:TQ, :],
                            idbf[0:TQ, 0:TQ]), s_pe)
                        b.wait("vector", s_pe, "aotr_cur")
                        if gb + j + tqs > 0:
                            b.wait("vector", s_ao, "aowr_prev")
                        M["aotc_prev"] = b.op("vector", lambda e, sl=sl, ptt=ptt: e.tensor_copy(
                            out=AR(off_aot + sl * TQ, TQ),
                            in_=ptt[:, 0:128].bitcast(BF16)[:, 0:TQ]), s_dv)
                        b.wait("sync", s_dv, "aotc_prev")
                        b.op("sync", lambda e, gb=gb, j=j, tqs=tqs, sl=sl: e.dma_start(
                            out=AIN[0 if gb < 8 else 1].ap()[
                                128 * j:128 * j + 128,
                                (gb % 8) * TB + tqs * TQ:(gb % 8) * TB + (tqs + 1) * TQ],
                            in_=AR(off_aot + sl * TQ, TQ)), s_ao, 16)
                        M["aowr_prev"] = s_ao.n
                    M[f"ao_grp_{gb}_{j}"] = s_ao.n
                M[f"fl_fin_{gb}"] = M[f"pvmm_{gb}_1"]
                if gb == 7:
                    M["ao_tot_0"] = s_ao.n
                    b.wait("gpsimd", s_ao, "ao_tot_0")
                    M["cc_ao0"] = b.op("gpsimd", lambda e: e.collective_compute(
                        "AllGather", ALU.bypass, replica_groups=RG,
                        ins=[AIN[0].ap()], outs=[AOUT[0].ap()]), s_cc)
            M["ao_tot_1"] = s_ao.n
            b.wait("gpsimd", s_ao, "ao_tot_1")
            M["cc_ao1"] = b.op("gpsimd", lambda e: e.collective_compute(
                "AllGather", ALU.bypass, replica_groups=RG,
                ins=[AIN[1].ap()], outs=[AOUT[1].ap()]), s_cc)

            for bi in range(2):
                ph = 20 + bi
                phase_barrier(ph)
                b.wait("sync", s_cc, "cc_ao0" if bi == 0 else "cc_ao1")
                if bi == 1:
                    b.wait("sync", s_pe, f"op_all_0")
                coeff = TB if bi == 0 else -TB
                cnst = 0 if bi == 0 else 7 * TB
                for he in range(2):
                    b.op("sync", lambda e, bi=bi, he=he, coeff=coeff, cnst=cnst: e.dma_start(
                        out=arena[:, off_att:off_att + 16 * TB]
                            .rearrange("p (ec t) -> p ec t", t=TB)[:, he::2, :],
                        in_=bass.AP(AOUT[bi], b.dynoff(e, coeff, cnst + 128 * he),
                                    [[8 * TB, 128], [256 * 8 * TB, 8], [1, TB]])), s_q0, 16)
                M[f"atld_{bi}"] = s_q0.n
                for wb in range(4):
                    half = wb % 2
                    if wb >= 2:
                        b.wait("sync", s_pe, f"op_fin_{bi}_{wb - 2}")
                    elif bi == 0:
                        b.wait("sync", s_pe, "aw_fin_1_10" if wb == 0 else "aw_fin_1_11")
                    else:
                        b.wait("sync", s_pe, f"op_fin_0_{2 + wb}")
                    M[f"ow_ld_{bi}_{wb}"] = b.op("sync", lambda e, wb=wb, half=half: e.dma_start(
                        out=at_view(half), in_=WATkc[:, :, 512 * (12 + wb):512 * (12 + wb) + 512]), sS[half], 16)
                    b.wait("tensor", sS[half], f"ow_ld_{bi}_{wb}")
                    b.wait("tensor", s_q0, f"atld_{bi}")
                    for sub in range(4):
                        dc = wb * 4 + sub
                        par = dc % 2
                        pg = psE if dc % 2 == 0 else psF
                        if dc >= 2:
                            b.wait("sync", s_dv, f"resfree_{ph}_{dc - 2}")
                        elif bi == 0:
                            b.wait("sync", s_dv, "nm1_11_%d" % (14 + dc))
                        else:
                            b.wait("sync", s_dv, "resfree_20_%d" % (14 + dc))
                        M[f"res_ld_{ph}_{dc}"] = b.op("sync", lambda e, dc=dc, par=par, bi=bi: e.dma_start(
                            out=rbuf[par][:, :], in_=R1[bi][128 * dc:128 * dc + 128, :]), sR[par], 16)
                        if dc >= 2:
                            b.wait("tensor", s_dv, f"opadd_{bi}_{dc - 2}")
                        for ec in range(16):
                            r = b.op("tensor", lambda e, half=half, ec=ec, sub=sub, pg=pg: e.matmul(
                                pg[:, 0:TB], at_view(half)[:, ec, sub * 128:(sub + 1) * 128],
                                arena[:, off_att + ec * TB: off_att + (ec + 1) * TB],
                                start=(ec == 0), stop=(ec == 15), skip_group_check=True),
                                s_pe if ec == 15 else None)
                        M[f"opmm_{bi}_{dc}"] = r
                        b.wait("vector", s_pe, f"opmm_{bi}_{dc}")
                        b.wait("vector", sR[par], f"res_ld_{ph}_{dc}")
                        if dc >= 2:
                            b.wait("vector", sW[par], f"r2wr_{bi}_{dc - 2}")
                        elif bi == 0:
                            b.wait("vector", sW[par], f"rwr_1_{14 + dc}")
                        else:
                            b.wait("vector", sW[par], f"r2wr_0_{14 + dc}")
                        M[f"opadd_{bi}_{dc}"] = b.op("vector", lambda e, par=par, dc=dc, pg=pg: e.scalar_tensor_tensor(
                            out=outb[par][:, :], in0=pg[:, 0:TB],
                            scalar=bo_sb[:, dc:dc + 1], in1=rbuf[par][:, :],
                            op0=ALU.add, op1=ALU.add), s_dv)
                        M[f"resfree_{ph}_{dc}"] = M[f"opadd_{bi}_{dc}"]
                        b.wait("gpsimd", s_dv, f"opadd_{bi}_{dc}")
                        M[f"r2wr_{bi}_{dc}"] = b.op("gpsimd", lambda e, bi=bi, dc=dc, par=par: e.dma_start(
                            out=R2[bi][128 * dc:128 * dc + 128, :], in_=outb[par][:, :]), sW[par], 16)
                    M[f"op_fin_{bi}_{wb}"] = M[f"opmm_{bi}_{4 * wb + 3}"]
                M[f"op_all_{bi}"] = M[f"opmm_{bi}_15"]
                M[f"op_wr_tot0_{bi}"] = s_w0.n
                M[f"op_wr_tot1_{bi}"] = s_w1.n

            ffn(2, R2[0], ob[0],
                prev_gu=[[(s_pe, "op_fin_1_2")], [(s_pe, "op_fin_1_3")]],
                prev_dn=[[(s_pe, "gu_fin_2_%d" % (NSP - 2))],
                         [(s_pe, "gu_fin_2_%d" % (NSP - 1))]],
                prev_xn="op_fin_1_3",
                prev_r=[[(s_dv, "resfree_21_14"), (s_w0, "op_wr_tot0_0"), (s_w1, "op_wr_tot1_0")],
                        [(s_dv, "resfree_21_15")]],
                prev_wr=[[(s_w0, "r2wr_1_14")], [(s_w1, "r2wr_1_15")]])
            ffn(3, R2[1], ob[1],
                prev_gu=[[(s_pe, "dn_fin_2_126")], [(s_pe, "dn_fin_2_127")]],
                prev_dn=[[(s_pe, "gu_fin_3_%d" % (NSP - 2))],
                         [(s_pe, "gu_fin_3_%d" % (NSP - 1))]],
                prev_xn="gu_fin_2_%d" % (NSP - 1),
                prev_r=[[(s_dv, "resadd_2_14"), (s_w0, "op_wr_tot0_1"), (s_w1, "op_wr_tot1_1")],
                        [(s_dv, "resadd_2_15")]],
                prev_wr=[[(s_w0, "rwr_2_14")], [(s_w1, "rwr_2_15")]])

            b.wait("sync", s_w0, "ffn_wr_tot0_3")
            b.wait("sync", s_w1, "ffn_wr_tot1_3")

        b.begin(None, None)
        program()
        dry_counts = [s.n for s in sems]
        dry_M = dict(b.M)
        with nc.Block() as blk:
            for name, deco in (("sync", blk.sync), ("gpsimd", blk.gpsimd),
                               ("vector", blk.vector), ("scalar", blk.scalar),
                               ("tensor", blk.tensor)):
                def mk(name):
                    def f(eng):
                        b.begin(name, eng)
                        program()
                        assert [s.n for s in sems] == dry_counts, name
                        assert b.M == dry_M, name
                    return f
                deco(mk(name))

    return nc


_PI = pi_perm()


def _shard_inputs(L0_attn_x, L0_attn_w_qkv, L0_attn_b_qkv, L0_attn_w_out,
                  L0_attn_b_out, L0_ffn_gate_W_packed, L0_ffn_gate_scales,
                  L0_ffn_up_W_packed, L0_ffn_up_scales, L0_ffn_down_W_packed,
                  L0_ffn_down_scales, MLP, S):
    TB = S // NBLK
    AN = MLP // 2048
    x = np.ascontiguousarray(np.asarray(L0_attn_x, dtype=np.float32)[0])
    xT = np.ascontiguousarray(x.T[_PI])
    wq = np.asarray(L0_attn_w_qkv, dtype=np.float32)
    wo = np.asarray(L0_attn_w_out, dtype=np.float32)
    bq = np.asarray(L0_attn_b_qkv, dtype=np.float32)
    bo_ = np.asarray(L0_attn_b_out, dtype=np.float32)
    wqT = np.ascontiguousarray(wq.T[_PI])
    woT = np.ascontiguousarray(wo.T[:, _PI])

    def u8(a):
        return np.ascontiguousarray(np.asarray(a).view(np.uint8)[:, 0::4])

    g8, u8_, d8 = u8(L0_ffn_gate_W_packed), u8(L0_ffn_up_W_packed), u8(L0_ffn_down_W_packed)
    gs = np.asarray(L0_ffn_gate_scales, dtype=np.float32)
    us = np.asarray(L0_ffn_up_scales, dtype=np.float32)
    ds = np.asarray(L0_ffn_down_scales, dtype=np.float32)
    d8p = d8[_PI]
    dsp_ = ds[_PI]

    awc = np.concatenate([wqT, woT], axis=1)
    in_maps = []
    for c in range(8):
        m = {}
        m["xb0"] = np.ascontiguousarray(xT[:, TB * c:TB * (c + 1)])
        m["xb1"] = np.ascontiguousarray(xT[:, TB * (15 - c):TB * (16 - c)])
        m["gpk"] = np.ascontiguousarray(g8[:, 128 * c:128 * (c + 1)].T)
        m["upk"] = np.ascontiguousarray(u8_[:, 128 * c:128 * (c + 1)].T)
        pidx = 8 * c + np.arange(128) // 16
        m["gsc"] = np.ascontiguousarray(gs[:, pidx].T)
        m["usc"] = np.ascontiguousarray(us[:, pidx].T)
        nbw = MLP // 16
        dslice = d8p[:, nbw * c: nbw * (c + 1)]
        m["dpk"] = np.ascontiguousarray(
            dslice.T.reshape(AN, 128, 2048).transpose(1, 0, 2).reshape(128, AN * 2048))
        a_i = np.arange(AN)
        p_i = np.arange(128)
        col = (MLP // 256) * c + (a_i[None, :] * 8 + (p_i[:, None] // 16))
        m["dsc"] = np.ascontiguousarray(
            dsp_[:, col.reshape(-1)].reshape(2048, 128, AN).transpose(1, 2, 0).reshape(128, AN * 2048))
        m["aw"] = np.ascontiguousarray(awc[256 * c:256 * (c + 1), :])
        m["bqk"] = np.ascontiguousarray(bq[0:4096].reshape(32, 128).T)
        m["bvr"] = np.ascontiguousarray(bq[4096:6144].reshape(1, 2048))
        m["bo"] = np.ascontiguousarray(bo_[_PI].reshape(16, 128).T)
        in_maps.append(m)
    return in_maps


def _assemble(results, MLP, S):
    TB = S // NBLK
    inv = np.empty_like(_PI)
    inv[_PI] = np.arange(D)
    out = np.empty((1, S, D), dtype=np.float32)
    for c in range(8):
        out[0, TB * c:TB * (c + 1), :] = results[c]["ob0"].T[:, inv]
        out[0, TB * (15 - c):TB * (16 - c), :] = results[c]["ob1"].T[:, inv]
    return out


def kernel(**inputs):
    MLP, S = 16384, 8192
    in_maps = _shard_inputs(MLP=MLP, S=S, **inputs)
    nc = build_program(MLP, S)
    res = run_bass_kernel_spmd(nc, in_maps, core_ids=list(range(8)))
    return _assemble(res.results, MLP, S)


# version 14
# speedup vs baseline: 1.0727x; 1.0727x over previous
import numpy as np
import ml_dtypes
from contextlib import ExitStack
import concourse.bass as bass
import concourse.mybir as mybir
from concourse.bass_utils import run_bass_kernel_spmd

F32 = mybir.dt.float32
BF16 = mybir.dt.bfloat16
U8 = mybir.dt.uint8
AF = mybir.ActivationFunctionType
ALU = mybir.AluOpType

D = 2048
NH = 16
HD = 128
NBLK = 16
QK_SCALE = float(1.0 / np.sqrt(HD))
EXP_BIAS = -16.0
MASKVAL = -1e9
LN_EPS = 1e-5


def pi_perm():
    idx = []
    for c in range(8):
        idx.extend(range(256 * c, 256 * c + 256, 2))
        idx.extend(range(256 * c + 1, 256 * c + 256, 2))
    return np.array(idx, dtype=np.int64)


class Sem:
    def __init__(self, ctx, nc, name):
        self.h = ctx.enter_context(nc.semaphore(name))
        self.n = 0


class B:

    def __init__(self, sems):
        self.sems = sems
        self.mode = None
        self.eng = None
        self.M = {}

    def begin(self, mode, eng):
        self.mode = mode
        self.eng = eng
        for s in self.sems:
            s.n = 0

    def op(self, name, fn, sem=None, k=1):
        if self.mode == name:
            ins = fn(self.eng)
            if sem is not None:
                ins.then_inc(sem.h, k)
        if sem is not None:
            sem.n += k
            return sem.n
        return None

    def dynoff(self, e, coeff, const):
        if getattr(self, "_dreg_eng", None) is not e:
            self._dreg_eng = e
            self._dreg = e.alloc_register()
            self._pid = e.partition_id()
        e.reg_alu(self._dreg, self._pid, coeff, mybir.AluOpType.mult)
        e.reg_alu(self._dreg, self._dreg, const, mybir.AluOpType.add)
        return self._dreg

    def wait(self, name, sem, key):
        if isinstance(key, str):
            if key not in self.M:
                assert self.mode is None, f"missing milestone {key} in live pass"
                return
            val = self.M[key]
        else:
            val = key
        if self.mode == name:
            self.eng.wait_ge(sem.h, val)


DEC_INLINE = True


def build_program(MLP, S):
    TB = S // NBLK
    TQ = min(128, TB)
    NTQ = TB // TQ
    NG = MLP // 128
    NSP = MLP // 256
    AN = MLP // 2048
    VS = HD + 1
    NCH = min(4096, MLP)
    assert MLP % 2048 == 0 and S % NBLK == 0

    nc = bass.Bass("TRN2", target_bir_lowering=False)

    xb = [nc.dram_tensor(f"xb{b}", [D, TB], F32, kind="ExternalInput").ap() for b in range(2)]
    gpk = nc.dram_tensor("gpk", [128, MLP], U8, kind="ExternalInput").ap()
    gsc = nc.dram_tensor("gsc", [128, MLP], F32, kind="ExternalInput").ap()
    upk = nc.dram_tensor("upk", [128, MLP], U8, kind="ExternalInput").ap()
    usc = nc.dram_tensor("usc", [128, MLP], F32, kind="ExternalInput").ap()
    dpk = nc.dram_tensor("dpk", [128, AN * D], U8, kind="ExternalInput").ap()
    dsc = nc.dram_tensor("dsc", [128, AN * D], F32, kind="ExternalInput").ap()
    aw = nc.dram_tensor("aw", [256, 4 * D], F32, kind="ExternalInput").ap()
    bqk = nc.dram_tensor("bqk", [128, 32], F32, kind="ExternalInput").ap()
    bvr = nc.dram_tensor("bvr", [1, D], F32, kind="ExternalInput").ap()
    bo = nc.dram_tensor("bo", [128, 16], F32, kind="ExternalInput").ap()
    ob = [nc.dram_tensor(f"ob{b}", [D, TB], F32, kind="ExternalOutput").ap() for b in range(2)]

    NCHK = max(MLP // 4096, 1)
    NCHC = MLP // NCHK
    blob_gu_t = [nc.dram_tensor(f"blob_gu{k}", [512, NCHC], BF16) for k in range(NCHK)]
    blob_dn_t = [nc.dram_tensor(f"blob_dn{j}", [2 * 8 * 128, AN * 128], BF16)
                 for j in range(2)]
    blob_at_t = nc.dram_tensor("blob_at", [256, 4 * D], BF16)
    WGU_t = [nc.dram_tensor(f"WGU{k}", [8 * 512, NCHC], BF16, addr_space="Shared") for k in range(NCHK)]
    WDN_t = [nc.dram_tensor(f"WDN{j}", [8 * 2 * 8 * 128, AN * 128], BF16, addr_space="Shared")
             for j in range(2)]
    WAT_t = nc.dram_tensor("WAT", [8 * 256, 4 * D], BF16, addr_space="Shared")
    QKIN = [nc.dram_tensor(f"QKIN{i}", [2 * D, TB], BF16) for i in range(2)]
    VIN = [nc.dram_tensor(f"VIN{i}", [TB, NH * VS], BF16) for i in range(2)]
    QKOUT = [nc.dram_tensor(f"QKOUT{i}", [8 * 2 * D, TB], BF16, addr_space="Shared") for i in range(2)]
    VOUT = [nc.dram_tensor(f"VOUT{i}", [8 * TB, NH * VS], BF16, addr_space="Shared") for i in range(2)]
    AIN = [nc.dram_tensor(f"AIN{i}", [256, 8 * TB], BF16) for i in range(2)]
    AOUT = [nc.dram_tensor(f"AOUT{i}", [8 * 256, 8 * TB], BF16, addr_space="Shared") for i in range(2)]
    R1_t = nc.dram_tensor("R1", [2, D, TB], F32)
    R2_t = nc.dram_tensor("R2", [2, D, TB], F32)

    with ExitStack() as ctx:
        def sbt(name, shape, dt):
            return ctx.enter_context(nc.sbuf_tensor(name, shape, dt))

        off_k = 0
        off_v = off_k + 2 * NBLK * TB
        off_q = off_v + 2 * NBLK * NTQ * VS
        off_p = off_q + 2 * NBLK * TB
        off_ao = off_p + 2 * TB
        off_aot = off_ao + 2 * NTQ * HD
        off_att = off_aot + 2 * TQ
        off_mask = off_att + 16 * TB
        off_smask = off_mask + NTQ * TB
        attn_el = off_smask + 2 * TB
        DW = max(NCH // 2, D)
        N_SETS = 3
        N_OSETS = 3
        dk_pk = [0, NCH]
        _ub = 2 * NCH
        dk_dqu = [[_ub + (2 * s + i) * DW for i in range(2)] for s in range(N_SETS)]
        _fb = (_ub + 2 * N_SETS * DW) // 2
        dk_dqf = [[_fb + (4 * s + i) * DW for i in range(4)] for s in range(N_SETS)]
        dk_dqs = (_fb + 4 * N_SETS * DW) // 2
        dk_dqo = [2 * dk_dqs + 2 * NCH + s * DW for s in range(N_OSETS)]
        dk_awb = dk_dqo[0] + N_OSETS * DW
        decode_el = dk_awb + 4 * D
        AR_EL = max(NG * TB, attn_el, decode_el)
        arena = sbt("arena", [128, AR_EL], BF16)

        def AR(off, width):
            return arena[:, off:off + width]

        arf = arena[:, :].bitcast(F32)
        aru = arena[:, :].bitcast(U8)
        off_bvrep = off_k

        strm = sbt("strm", [128, 16384], BF16)
        x_n = sbt("x_n", [128, 16 * TB], BF16)
        rbuf = [sbt(f"rbuf{i}", [128, TB], F32) for i in range(2)]
        arep = sbt("arep", [128, TB], F32)
        brep = sbt("brep", [128, TB], F32)
        lnt = [sbt(f"lnt{i}", [128, TB], F32) for i in range(2)]
        sqb = lnt[0]
        gelu_b = [sbt(f"gelu{i}", [128, TB], BF16) for i in range(2)]
        outb = [sbt(f"outb{i}", [128, TB], F32) for i in range(2)]
        evb = [sbt(f"evb{i}", [128, 512], BF16) for i in range(2)]
        stat = sbt("stat", [1, 4 * TB], F32)
        ones_col = sbt("ones_col", [128, 1], F32)
        ones_row = sbt("ones_row", [1, 128], F32)
        bqk_sb = sbt("bqk_sb", [128, 32], F32)
        bo_sb = sbt("bo_sb", [128, 16], F32)
        onesv = sbt("onesv", [128, max(NH, 1)], BF16)
        lrec = sbt("lrec", [128, 2], F32)
        cb = sbt("cb", [128, 8], F32)
        idf = sbt("idf", [128, 128], F32)
        idbf = sbt("idbf", [128, 128], BF16)

        for i, val in enumerate([EXP_BIAS, LN_EPS, 1.0, 0.5, -2.0, -6.0, -1.0, -3.0]):
            nc.const_aps.register(F32, val, cb[:, i:i + 1])

        def dqu_ap(s, i, w):
            return aru[:, dk_dqu[s][i]:dk_dqu[s][i] + w]

        def dqf_ap(s, i, w):
            return arena[:, dk_dqf[s][i]:dk_dqf[s][i] + w]

        def dqs_ap(off, w):
            return arf[:, dk_dqs + off:dk_dqs + off + w]

        def dqo_ap(s, w):
            return arena[:, dk_dqo[s]:dk_dqo[s] + w]

        def pk_ap(i, off, w):
            return aru[:, dk_pk[i] + off:dk_pk[i] + off + w]

        awb_ap = lambda: arena[:, dk_awb:dk_awb + 4 * D]
        bv_row_ap = lambda: arf[0:1, off_v // 2:off_v // 2 + D]
        bv_rep_ap = lambda: arena[:, off_bvrep:off_bvrep + D]
        maskT_ap = lambda: arena[:, off_mask:off_mask + NTQ * TB]
        smask_ap = lambda: arf[:, off_smask // 2:off_smask // 2 + TB]

        psA = ctx.enter_context(nc.psum_tensor("psA", [128, 512], F32))
        psB = ctx.enter_context(nc.psum_tensor("psB", [128, 512], F32))
        psC = ctx.enter_context(nc.psum_tensor("psC", [128, 512], F32))
        psD = ctx.enter_context(nc.psum_tensor("psD", [128, 512], F32))
        psE = ctx.enter_context(nc.psum_tensor("psE", [128, 512], F32))
        psF = ctx.enter_context(nc.psum_tensor("psF", [128, 512], F32))
        pst1 = ctx.enter_context(nc.psum_tensor("pst1", [128, 512], F32))
        pst2 = ctx.enter_context(nc.psum_tensor("pst2", [128, 512], F32))
        pqo = [psC, psD, psE, psF][:NTQ]

        s_pk = Sem(ctx, nc, "s_pk")
        s_in = Sem(ctx, nc, "s_in")
        s_pq = Sem(ctx, nc, "s_pq")
        s_vo = Sem(ctx, nc, "s_vo")
        s_k1 = Sem(ctx, nc, "s_k1")
        s_b1 = Sem(ctx, nc, "s_b1")
        s_b2 = Sem(ctx, nc, "s_b2")
        s_s0 = Sem(ctx, nc, "s_s0")
        s_s1 = Sem(ctx, nc, "s_s1")
        s_r0 = Sem(ctx, nc, "s_r0")
        s_r1 = Sem(ctx, nc, "s_r1")
        s_w0 = Sem(ctx, nc, "s_w0")
        s_w1 = Sem(ctx, nc, "s_w1")
        s_q0 = Sem(ctx, nc, "s_q0")
        s_q1 = Sem(ctx, nc, "s_q1")
        s_kv = Sem(ctx, nc, "s_kv")
        s_ao = Sem(ctx, nc, "s_ao")
        s_bw = Sem(ctx, nc, "s_bw")
        s_aw = Sem(ctx, nc, "s_aw")
        s_cc = Sem(ctx, nc, "s_cc")
        s_pe = Sem(ctx, nc, "s_pe")
        s_ac = Sem(ctx, nc, "s_ac")
        s_dv = Sem(ctx, nc, "s_dv")
        s_gp = Sem(ctx, nc, "s_gp")
        sems = [s_pk, s_in, s_pq, s_vo, s_k1, s_b1, s_b2, s_s0, s_s1, s_r0, s_r1, s_w0, s_w1, s_q0, s_q1,
                s_kv, s_ao, s_bw, s_aw, s_cc, s_pe, s_ac, s_dv, s_gp]
        sS = [s_s0, s_s1]
        sR = [s_r0, s_r1]
        sW = [s_w0, s_w1]
        sQ = [s_q0, s_q1]

        b = B(sems)
        M = b.M

        WGU_v = [t.ap().rearrange("(c q p) n -> p c q n", q=4, p=128) for t in WGU_t]
        WDN_v = [WDN_t[j].ap().rearrange("(c pl hc p) m -> p c pl hc m", pl=2, hc=8, p=128)
                 for j in range(2)]
        WATkc = WAT_t.ap().rearrange("(kc p) e -> p kc e", p=128)

        def gu_view(half):
            return strm[:, half * 8192:half * 8192 + 8192].rearrange(
                "p (c q n) -> p c q n", c=8, q=4)

        def dn_view(half):
            return strm[:, half * 8192:half * 8192 + 2 * AN * 128].rearrange(
                "p (pl m) -> p pl m", pl=2)

        def at_view(half):
            return strm[:, half * 8192:half * 8192 + 16 * 512].rearrange(
                "p (kc e) -> p kc e", kc=16)

        def inter_ap(g):
            return arena[:, g * TB:(g + 1) * TB]

        RG = [list(range(8))]

        def program():
            for i, val in enumerate([EXP_BIAS, LN_EPS, 1.0, 0.5, -2.0, -6.0, -1.0, -3.0]):
                b.op("gpsimd", lambda e, i=i, val=val: e.memset(cb[:, i:i + 1], val), s_gp)
            b.op("gpsimd", lambda e: e.memset(ones_col[:, :], 1.0), s_gp)
            b.op("gpsimd", lambda e: e.memset(ones_row[:, :], 1.0), s_gp)
            b.op("gpsimd", lambda e: e.memset(onesv[:, :], 1.0), s_gp)
            M["idz"] = b.op("gpsimd", lambda e: e.memset(idf[:, :], 0.0), s_gp)
            b.wait("gpsimd", s_gp, "idz")
            M["idsel"] = b.op("gpsimd", lambda e: e.affine_select(
                out=idf[:, :], in_=idf[:, :], compare_op=ALU.not_equal, fill=1.0,
                base=0, pattern=[[-1, 128]], channel_multiplier=1), s_gp)
            M["consts"] = s_gp.n
            for en in ("tensor", "vector", "scalar", "sync"):
                b.wait(en, s_gp, "consts")
            M["idbf"] = b.op("vector", lambda e: e.tensor_copy(out=idbf[:, :], in_=idf[:, :]), s_dv)

            b.op("sync", lambda e: e.dma_start(out=bqk_sb[:, :], in_=bqk), s_in, 16)
            b.op("sync", lambda e: e.dma_start(out=bo_sb[:, :], in_=bo), s_in, 16)
            M["bias_ld"] = s_in.n
            b.wait("vector", s_in, "bias_ld")
            M["bqs"] = b.op("vector", lambda e: e.tensor_scalar(
                out=bqk_sb[:, 0:16], in0=bqk_sb[:, 0:16], scalar1=QK_SCALE,
                scalar2=None, op0=ALU.mult), s_dv)

            blob_at = blob_at_t.ap()

            st = {"hist": [], "pkuse": [None, None], "ss": 0, "so": 0}

            def dq_pk_load(mat, bi_, pk_src, c0, w):
                if st["pkuse"][bi_] is not None:
                    b.wait("scalar", s_dv, st["pkuse"][bi_])
                M[f"pk_{mat}_{c0}"] = b.op("scalar", lambda e, pk_src=pk_src, bi_=bi_, c0=c0, w=w: e.dma_start(
                    out=pk_ap(bi_, 0, w), in_=pk_src[:, c0:c0 + w]), [s_pk, s_pq][bi_], 16)

            def dq_sc_load(mat, sc_src, c0, w, first=False):
                if st["hist"]:
                    b.wait("scalar", s_gp, f"v4_{st['hist'][-1][0]}")
                elif first:
                    b.wait("scalar", s_dv, "bqs")
                M[f"sc_{mat}_{c0}"] = b.op("scalar", lambda e, sc_src=sc_src, c0=c0, w=w: e.dma_start(
                    out=dqs_ap(0, w), in_=sc_src[:, c0:c0 + w]), s_in, 16)

            def dq_chunk(mat, bi_, c0, w, out_fn):
                for coff in range(0, w, DW):
                    sw = min(DW, w - coff)
                    for pl in range(2):
                        tag = f"{mat}_{c0 + coff}_{pl}"
                        ss = st["ss"]
                        st["ss"] = (ss + 1) % N_SETS
                        so = st["so"]
                        st["so"] = (so + 1) % N_OSETS
                        sbs = [s_bw, s_b1, s_b2]
                        hist = st["hist"]
                        ptN = hist[-N_SETS][0] if len(hist) >= N_SETS else None
                        ptO = hist[-N_OSETS][0] if len(hist) >= N_OSETS else None
                        scm = 1.0 if pl == 0 else (1.0 / 16.0)
                        sa = 0.5 if mat == "dn" else 1.0
                        mm_, sm_ = (7, 8) if pl == 0 else (0x70, 0x80)
                        b.wait("vector", [s_pk, s_pq][bi_], f"pk_{mat}_{c0}")
                        if ptN is not None:
                            b.wait("vector", s_ac, f"dqa_{ptN}")
                        M[f"nib_{tag}"] = b.op("vector", lambda e, bi_=bi_, coff=coff, w=sw, ss=ss, mm_=mm_, sm_=sm_: (
                            e.tensor_scalar(out=dqu_ap(ss, 0, w), in0=pk_ap(bi_, coff, w),
                                            scalar1=mm_, scalar2=None, op0=ALU.bitwise_and),
                            e.tensor_scalar(out=dqu_ap(ss, 1, w), in0=pk_ap(bi_, coff, w),
                                            scalar1=sm_, scalar2=None, op0=ALU.bitwise_and))[-1], s_dv)
                        b.wait("scalar", s_dv, f"nib_{tag}")
                        b.wait("scalar", s_gp, "consts")
                        if ptN is not None:
                            b.wait("scalar", s_gp, f"v4_{ptN}")
                            b.wait("scalar", s_dv, f"v3_{ptN}")
                        M[f"dqa_{tag}"] = b.op("scalar", lambda e, w=sw, ss=ss, scm=scm, sa=sa: (
                            e.activation(out=dqf_ap(ss, 0, w), in_=dqu_ap(ss, 0, w), func=AF.Copy, scale=0.5 * scm * sa),
                            e.activation(out=dqf_ap(ss, 2, w), in_=dqu_ap(ss, 0, w), func=AF.Relu, scale=0.5 * scm * sa, bias=-2.0 * sa),
                            e.activation(out=dqf_ap(ss, 3, w), in_=dqu_ap(ss, 0, w), func=AF.Relu, scale=scm * sa, bias=-6.0 * sa),
                            e.activation(out=dqf_ap(ss, 1, w), in_=dqu_ap(ss, 1, w), func=AF.Copy,
                                         scale=-0.25 * scm, bias=1.0))[-1], s_ac)
                        b.wait("vector", s_ac, f"dqa_{tag}")
                        M[f"v1_{tag}"] = b.op("vector", lambda e, w=sw, ss=ss: e.tensor_add(
                            out=dqf_ap(ss, 0, w), in0=dqf_ap(ss, 0, w), in1=dqf_ap(ss, 2, w)), s_dv)
                        b.wait("vector", s_dv, f"v1_{tag}")
                        M[f"v2_{tag}"] = b.op("vector", lambda e, w=sw, ss=ss: e.tensor_add(
                            out=dqf_ap(ss, 0, w), in0=dqf_ap(ss, 0, w), in1=dqf_ap(ss, 3, w)), s_dv)
                        b.wait("vector", s_dv, f"v2_{tag}")
                        M[f"v3_{tag}"] = b.op("vector", lambda e, w=sw, ss=ss: e.tensor_mul(
                            out=dqf_ap(ss, 0, w), in0=dqf_ap(ss, 0, w), in1=dqf_ap(ss, 1, w)), s_dv)
                        b.wait("gpsimd", s_dv, f"v3_{tag}")
                        b.wait("gpsimd", s_in, f"sc_{mat}_{c0}")
                        if ptO is not None:
                            b.wait("gpsimd", sbs[so], f"dq_wr_{ptO}")
                        M[f"v4_{tag}"] = b.op("gpsimd", lambda e, w=sw, ss=ss, so=so, coff=coff: e.tensor_mul(
                            out=dqo_ap(so, w), in0=dqf_ap(ss, 0, w), in1=dqs_ap(coff, w)), s_gp)
                        b.wait("gpsimd", s_gp, f"v4_{tag}")
                        fns = out_fn(pl, c0 + coff, sw, so)
                        if not isinstance(fns, list):
                            fns = [fns]
                        for f_ in fns:
                            M[f"dq_wr_{tag}"] = b.op("gpsimd", f_, sbs[so], 16)
                        hist.append((tag, ss, so))
                st["pkuse"][bi_] = f"nib_{mat}_{c0 + w - DW}_1"

            def gu_out(ch, rowbase):
                def f(pl, c0, w, so):
                    cr = c0 - ch * NCHC
                    return lambda e: e.dma_start(
                        out=blob_gu_t[ch].ap()[rowbase + 128 * pl: rowbase + 128 * pl + 128,
                                               cr:cr + w],
                        in_=dqo_ap(so, w))
                return f

            def dn_out(pl, c0, w, so):
                a0, na = c0 // D, w // D
                dviews = [blob_dn_t[j].ap().rearrange("(pl hc p) m -> pl hc p m", pl=2, hc=8)
                          for j in range(2)]
                return [
                    (lambda e, ai=ai, j=j: e.dma_start(
                        out=dviews[j][pl, :, :, (a0 + ai) * 128:(a0 + ai + 1) * 128]
                            .rearrange("hc p hh -> p hc hh"),
                        in_=dqo_ap(so, w).rearrange("p (a hc hh) -> p a hc hh", a=na, hc=16)[:, ai, 8 * j:8 * j + 8, :]))
                    for ai in range(na) for j in range(2)]

            def dec_gu(ch):
                for mi, (mat, pk_src, sc_src, rowb) in enumerate(
                        [("g", gpk, gsc, 0), ("u", upk, usc, 256)]):
                    c0 = ch * NCHC
                    dq_pk_load(mat, mi, pk_src, c0, NCHC)
                    dq_sc_load(mat, sc_src, c0, NCHC, first=(ch == 0 and mi == 0))
                    dq_chunk(mat, mi, c0, NCHC, gu_out(ch, rowb))
                M[f"gu_dec0_{ch}"] = s_bw.n
                M[f"gu_dec1_{ch}"] = s_b1.n
                M[f"gu_dec2_{ch}"] = s_b2.n
                b.wait("gpsimd", s_bw, f"gu_dec0_{ch}")
                b.wait("gpsimd", s_b1, f"gu_dec1_{ch}")
                b.wait("gpsimd", s_b2, f"gu_dec2_{ch}")
                M[f"cc_gu_{ch}"] = b.op("gpsimd", lambda e, ch=ch: e.collective_compute(
                    "AllGather", ALU.bypass, replica_groups=RG,
                    ins=[blob_gu_t[ch].ap()], outs=[WGU_t[ch].ap()]), s_cc)

            ND = (AN * D + NCH - 1) // NCH

            def dec_dn(ch):
                c0, w = ch * NCH, min(NCH, AN * D - ch * NCH)
                dq_pk_load("dn", ch % 2, dpk, c0, w)
                dq_sc_load("dn", dsc, c0, w)
                dq_chunk("dn", ch % 2, c0, w, dn_out)
                if ch == ND - 1:
                    M["dn_dec0"] = s_bw.n
                    M["dn_dec1"] = s_b1.n
                    M["dn_dec2"] = s_b2.n
                    b.wait("gpsimd", s_bw, "dn_dec0")
                    b.wait("gpsimd", s_b1, "dn_dec1")
                    b.wait("gpsimd", s_b2, "dn_dec2")
                    for j in range(2):
                        M[f"cc_dn_{j}"] = b.op("gpsimd", lambda e, j=j: e.collective_compute(
                            "AllGather", ALU.bypass, replica_groups=RG,
                            ins=[blob_dn_t[j].ap()], outs=[WDN_t[j].ap()]), s_cc)

            def dec_aw():
                for hl in range(2):
                    if hl == 0:
                        b.wait("gpsimd", s_dv, st["pkuse"][0])
                        b.wait("gpsimd", s_dv, st["pkuse"][1])
                        M["aw_pre0"] = s_bw.n
                        M["aw_pre1"] = s_b1.n
                        M["aw_pre2"] = s_b2.n
                        b.wait("gpsimd", s_bw, "aw_pre0")
                        b.wait("gpsimd", s_b1, "aw_pre1")
                        b.wait("gpsimd", s_b2, "aw_pre2")
                    M[f"aw_ld{hl}"] = b.op("gpsimd", lambda e, hl=hl: e.dma_start(
                        out=awb_ap(), in_=aw[128 * hl:128 * hl + 128, :]), s_aw, 16)
                    b.wait("gpsimd", s_aw, f"aw_ld{hl}")
                    M[f"aw_wr{hl}"] = b.op("gpsimd", lambda e, hl=hl: e.dma_start(
                        out=blob_at[128 * hl:128 * hl + 128, :], in_=awb_ap()), s_bw, 16)
                    b.wait("gpsimd", s_bw, f"aw_wr{hl}")
                M["cc_at"] = b.op("gpsimd", lambda e: e.collective_compute(
                    "AllGather", ALU.bypass, replica_groups=RG,
                    ins=[blob_at_t.ap()], outs=[WAT_t.ap()]), s_cc)

            def ffn0_hook(sp):
                if DEC_INLINE:
                    return
                ch = sp // 2
                if sp % 2 == 0 and ch < ND:
                    dec_dn(ch)
                elif sp == 2 * ND:
                    dec_aw()

            def emit_phase0():
                emit_R_loads(0, xb[0], [[], []])
                layer_norm(0, None, None)
                for ch in range(NCHK):
                    dec_gu(ch)
                if DEC_INLINE:
                    for ch in range(ND):
                        dec_dn(ch)
                    dec_aw()

            def phase_barrier(ph):
                if ph == 0:
                    return
                M[f"bar_dv_{ph}"] = s_dv.n
                M[f"bar_ac_{ph}"] = s_ac.n
                b.wait("tensor", s_dv, f"bar_dv_{ph}")
                b.wait("tensor", s_ac, f"bar_ac_{ph}")

            def emit_R_loads(ph, src, prev_keys):
                for idx in range(32):
                    j = idx % 16
                    par = idx % 2
                    if idx >= 2:
                        if idx - 2 < 16:
                            b.wait("sync", s_pe, f"sx2_{ph}_{idx - 2}")
                            b.wait("sync", s_ac, f"sq_{ph}_{idx - 2}")
                        else:
                            b.wait("sync", s_dv, f"nm1_{ph}_{idx - 18}")
                    else:
                        for (sem, key) in prev_keys[idx]:
                            b.wait("sync", sem, key)
                    M[f"r_ld_{ph}_{idx}"] = b.op("sync", lambda e, j=j, par=par, src=src: e.dma_start(
                        out=rbuf[par][:, :], in_=src[128 * j:128 * j + 128, :]), sR[par], 16)

            def emit_res_loads(ph, src, prev_keys):
                for hc in range(16):
                    par = hc % 2
                    if hc >= 2:
                        b.wait("sync", s_dv, f"resfree_{ph}_{hc - 2}")
                    else:
                        for (sem, key) in prev_keys[hc]:
                            b.wait("sync", sem, key)
                    M[f"res_ld_{ph}_{hc}"] = b.op("sync", lambda e, hc=hc, par=par, src=src: e.dma_start(
                        out=rbuf[par][:, :], in_=src[128 * hc:128 * hc + 128, :]), sR[par], 16)

            def layer_norm(ph, prev_xn_key, prev_lnt_keys):
                phase_barrier(ph)
                for j in range(16):
                    par = j % 2
                    b.wait("scalar", sR[par], f"r_ld_{ph}_{j}")
                    if j >= 1:
                        b.wait("scalar", s_pe, f"sx2_{ph}_{j - 1}")
                    M[f"sq_{ph}_{j}"] = b.op("scalar", lambda e, par=par: e.activation(
                        out=sqb[:, :], in_=rbuf[par][:, :], func=AF.Square), s_ac)
                    b.wait("tensor", sR[par], f"r_ld_{ph}_{j}")
                    M[f"sx_{ph}_{j}"] = b.op("tensor", lambda e, j=j, par=par: e.matmul(
                        pst1[0:1, 0:TB], ones_col[:, :], rbuf[par][:, :],
                        start=(j == 0), stop=(j == 15), skip_group_check=True), s_pe)
                    b.wait("tensor", s_ac, f"sq_{ph}_{j}")
                    M[f"sx2_{ph}_{j}"] = b.op("tensor", lambda e, j=j: e.matmul(
                        pst2[0:1, 0:TB], ones_col[:, :], sqb[:, :],
                        start=(j == 0), stop=(j == 15), skip_group_check=True), s_pe)
                st_ = stat
                b.wait("vector", s_pe, f"sx2_{ph}_15")
                M[f"st1_{ph}"] = b.op("vector", lambda e: e.tensor_scalar(
                    out=st_[:, 0:TB], in0=pst1[0:1, 0:TB], scalar1=1.0 / D,
                    scalar2=None, op0=ALU.mult), s_dv)
                M[f"st2_{ph}"] = b.op("vector", lambda e: e.tensor_scalar(
                    out=st_[:, TB:2 * TB], in0=pst2[0:1, 0:TB], scalar1=1.0 / D,
                    scalar2=None, op0=ALU.mult), s_dv)
                b.wait("vector", s_dv, f"st2_{ph}")
                M[f"st3_{ph}"] = b.op("vector", lambda e: e.tensor_mul(
                    out=st_[:, 2 * TB:3 * TB], in0=st_[:, 0:TB], in1=st_[:, 0:TB]), s_dv)
                b.wait("vector", s_dv, f"st3_{ph}")
                M[f"st4_{ph}"] = b.op("vector", lambda e: e.tensor_sub(
                    out=st_[:, TB:2 * TB], in0=st_[:, TB:2 * TB], in1=st_[:, 2 * TB:3 * TB]), s_dv)
                b.wait("scalar", s_dv, f"st4_{ph}")
                M[f"st5_{ph}"] = b.op("scalar", lambda e: e.activation(
                    out=st_[:, 2 * TB:3 * TB], in_=st_[:, TB:2 * TB], func=AF.Ln,
                    bias=LN_EPS), s_ac)
                b.wait("scalar", s_ac, f"st5_{ph}")
                M[f"st6_{ph}"] = b.op("scalar", lambda e: e.activation(
                    out=st_[:, 3 * TB:4 * TB], in_=st_[:, 2 * TB:3 * TB], func=AF.Exp,
                    scale=-0.5), s_ac)
                b.wait("vector", s_ac, f"st6_{ph}")
                M[f"st7_{ph}"] = b.op("vector", lambda e: e.tensor_mul(
                    out=st_[:, 0:TB], in0=st_[:, 0:TB], in1=st_[:, 3 * TB:4 * TB]), s_dv)
                b.wait("vector", s_dv, f"st7_{ph}")
                M[f"st8_{ph}"] = b.op("vector", lambda e: e.tensor_scalar(
                    out=st_[:, 0:TB], in0=st_[:, 0:TB], scalar1=-1.0,
                    scalar2=None, op0=ALU.mult), s_dv)
                b.wait("tensor", s_ac, f"st6_{ph}")
                b.wait("tensor", s_dv, f"st8_{ph}")
                M[f"rep1_{ph}"] = b.op("tensor", lambda e: e.matmul(
                    psA[:, 0:TB], ones_row[:, :], stat[0:1, 3 * TB:4 * TB],
                    start=True, stop=True, skip_group_check=True), s_pe)
                M[f"rep2_{ph}"] = b.op("tensor", lambda e: e.matmul(
                    psB[:, 0:TB], ones_row[:, :], stat[0:1, 0:TB],
                    start=True, stop=True, skip_group_check=True), s_pe)
                b.wait("vector", s_pe, f"rep2_{ph}")
                M[f"arep_{ph}"] = b.op("vector", lambda e: e.tensor_copy(
                    out=arep[:, :], in_=psA[:, 0:TB]), s_dv)
                b.wait("vector", s_dv, f"arep_{ph}")
                M[f"brepc_{ph}"] = b.op("vector", lambda e: e.tensor_copy(
                    out=brep[:, :], in_=psB[:, 0:TB]), s_dv)
                for j in range(16):
                    jj = 16 + j
                    par = jj % 2
                    b.wait("vector", sR[par], f"r_ld_{ph}_{jj}")
                    b.wait("vector", s_dv, f"brepc_{ph}")
                    if j < 2 and prev_lnt_keys is not None:
                        for (sem, key) in prev_lnt_keys[j]:
                            b.wait("vector", sem, key)
                    if j >= 2:
                        b.wait("vector", s_dv, f"nm2_{ph}_{j - 2}")
                    M[f"nm1_{ph}_{j}"] = b.op("vector", lambda e, j=j, par=par: e.tensor_mul(
                        out=lnt[j % 2][:, :], in0=rbuf[par][:, :], in1=arep[:, :]), s_dv)
                    b.wait("vector", s_dv, f"nm1_{ph}_{j}")
                    if j == 0 and prev_xn_key is not None:
                        b.wait("vector", s_pe, prev_xn_key)
                    M[f"nm2_{ph}_{j}"] = b.op("vector", lambda e, j=j: e.tensor_add(
                        out=x_n[:, j * TB:(j + 1) * TB], in0=lnt[j % 2][:, :],
                        in1=brep[:, :]), s_dv)
                M[f"xn_{ph}"] = s_dv.n

            def ffn(ph, src, dst, prev_gu, prev_dn, prev_xn, prev_r, prev_wr, hook=None,
                    pre_ln=False):
                if not pre_ln:
                    emit_R_loads(ph, src, prev_r)
                    layer_norm(ph, prev_xn, None)
                spc = NCHC // 256
                for sp in range(NSP):
                    if hook is not None:
                        hook(sp)
                    half = sp % 2
                    kch = sp // spc
                    if sp >= 2:
                        b.wait("sync", s_pe, f"gu_fin_{ph}_{sp - 2}")
                    else:
                        for (sem, key) in prev_gu[sp]:
                            b.wait("sync", sem, key)
                    if ph == 0 and (sp % spc) < 2:
                        b.wait("sync", s_cc, f"cc_gu_{kch}")
                    for ph_ in range(2):
                        M[f"gu_ld_{ph}_{sp}"] = b.op("sync", lambda e, sp=sp, half=half, kch=kch, ph_=ph_: e.dma_start(
                            out=gu_view(half)[:, 4 * ph_:4 * ph_ + 4, :, :],
                            in_=WGU_v[kch][:, 4 * ph_:4 * ph_ + 4, :,
                                           256 * sp - kch * NCHC: 256 * sp - kch * NCHC + 256]), sS[half], 16)
                    b.wait("tensor", sS[half], f"gu_ld_{ph}_{sp}")
                    b.wait("tensor", s_dv, f"xn_{ph}")
                    for par2 in range(2):
                        g = 2 * sp + par2
                        pg, pu = (psA, psB) if par2 == 0 else (psE, psF)
                        if g >= 2:
                            b.wait("tensor", s_ac, f"gel_{ph}_{g - 2}")
                            b.wait("tensor", s_dv, f"inter_{ph}_{g - 2}")
                        else:
                            b.wait("tensor", s_dv, f"arep_{ph}")
                            b.wait("tensor", s_dv, f"brepc_{ph}")
                        for kc in range(16):
                            c0, pl = kc // 2, kc % 2
                            r = b.op("tensor", lambda e, half=half, c0=c0, pl=pl, par2=par2, kc=kc, pg=pg: e.matmul(
                                pg[:, 0:TB], gu_view(half)[:, c0, pl, par2:256:2],
                                x_n[:, (2 * c0 + pl) * TB:(2 * c0 + pl + 1) * TB],
                                start=(kc == 0), stop=(kc == 15), skip_group_check=True),
                                s_pe if kc == 15 else None)
                        M[f"gmm_{ph}_{g}"] = r
                        for kc in range(16):
                            c0, pl = kc // 2, kc % 2
                            r = b.op("tensor", lambda e, half=half, c0=c0, pl=pl, par2=par2, kc=kc, pu=pu: e.matmul(
                                pu[:, 0:TB], gu_view(half)[:, c0, 2 + pl, par2:256:2],
                                x_n[:, (2 * c0 + pl) * TB:(2 * c0 + pl + 1) * TB],
                                start=(kc == 0), stop=(kc == 15), skip_group_check=True),
                                s_pe if kc == 15 else None)
                        M[f"umm_{ph}_{g}"] = r
                        b.wait("scalar", s_pe, f"gmm_{ph}_{g}")
                        if g >= 2:
                            b.wait("scalar", s_dv, f"inter_{ph}_{g - 2}")
                        M[f"gel_{ph}_{g}"] = b.op("scalar", lambda e, g=g, pg=pg: e.activation(
                            out=gelu_b[g % 2][:, :], in_=pg[:, 0:TB],
                            func=AF.Gelu_apprx_tanh), s_ac)
                        b.wait("vector", s_ac, f"gel_{ph}_{g}")
                        b.wait("vector", s_pe, f"umm_{ph}_{g}")
                        M[f"inter_{ph}_{g}"] = b.op("vector", lambda e, g=g, pu=pu: e.tensor_tensor(
                            out=inter_ap(g), in0=pu[:, 0:TB], in1=gelu_b[g % 2][:, :],
                            op=ALU.mult), s_dv)
                    M[f"gu_fin_{ph}_{sp}"] = M[f"umm_{ph}_{2 * sp + 1}"]
                for hc in range(16):
                    par = hc % 2
                    if hc >= 2:
                        b.wait("sync", s_dv, f"resfree_{ph}_{hc - 2}")
                    else:
                        b.wait("sync", s_dv, f"nm1_{ph}_{14 + hc}")
                    M[f"res_ld_{ph}_{hc}"] = b.op("sync", lambda e, hc=hc, par=par, src=src: e.dma_start(
                        out=rbuf[par][:, :], in_=src[128 * hc:128 * hc + 128, :]), sR[par], 16)
                    pdn = psA if hc % 2 == 0 else psE
                    for dsp in range(8):
                        idx = hc * 8 + dsp
                        half = idx % 2
                        if idx >= 2:
                            b.wait("sync", s_pe, f"dn_fin_{ph}_{idx - 2}")
                        else:
                            for (sem, key) in prev_dn[idx]:
                                b.wait("sync", sem, key)
                        if ph == 0 and hc == 8 and dsp < 2:
                            b.wait("sync", s_cc, "cc_dn_1")
                        M[f"dn_ld_{ph}_{idx}"] = b.op("sync", lambda e, hc=hc, dsp=dsp, half=half: e.dma_start(
                            out=dn_view(half), in_=WDN_v[hc // 8][:, dsp, :, hc % 8, :]), sS[half], 16)
                    if hc == 0:
                        b.wait("tensor", s_dv, f"inter_{ph}_{NG - 1}")
                        b.wait("tensor", s_dv, f"inter_{ph}_{NG - 2}")
                        b.wait("tensor", s_ac, f"gel_{ph}_{NG - 1}")
                        b.wait("tensor", s_ac, f"gel_{ph}_{NG - 2}")
                    if hc >= 2:
                        b.wait("tensor", s_dv, f"resadd_{ph}_{hc - 2}")
                    first = True
                    for dsp in range(8):
                        idx = hc * 8 + dsp
                        half = idx % 2
                        b.wait("tensor", sS[half], f"dn_ld_{ph}_{idx}")
                        for pl in range(2):
                            for a in range(AN):
                                g = 2 * (dsp * AN + a) + pl
                                stop_ = (dsp == 7 and pl == 1 and a == AN - 1)
                                mark = (pl == 1 and a == AN - 1)
                                r = b.op("tensor", lambda e, half=half, pl=pl, a=a, g=g, pdn=pdn, first=first, stop_=stop_: e.matmul(
                                    pdn[:, 0:TB], dn_view(half)[:, pl, a * 128:(a + 1) * 128],
                                    inter_ap(g), start=first, stop=stop_,
                                    skip_group_check=True), s_pe if mark else None)
                                first = False
                        M[f"dn_fin_{ph}_{idx}"] = r
                    M[f"dn_mm_{ph}_{hc}"] = r
                    par = hc % 2
                    b.wait("vector", s_pe, f"dn_mm_{ph}_{hc}")
                    b.wait("vector", sR[par], f"res_ld_{ph}_{hc}")
                    if hc >= 2:
                        b.wait("vector", sW[par], f"rwr_{ph}_{hc - 2}")
                    else:
                        for (sem, key) in prev_wr[hc]:
                            b.wait("vector", sem, key)
                    M[f"resadd_{ph}_{hc}"] = b.op("vector", lambda e, par=par, pdn=pdn: e.tensor_add(
                        out=outb[par][:, :], in0=pdn[:, 0:TB], in1=rbuf[par][:, :]), s_dv)
                    M[f"resfree_{ph}_{hc}"] = M[f"resadd_{ph}_{hc}"]
                    b.wait("gpsimd", s_dv, f"resadd_{ph}_{hc}")
                    M[f"rwr_{ph}_{hc}"] = b.op("gpsimd", lambda e, hc=hc, par=par, dst=dst: e.dma_start(
                        out=dst[128 * hc:128 * hc + 128, :], in_=outb[par][:, :]), sW[par], 16)
                M[f"ffn_wr_tot0_{ph}"] = s_w0.n
                M[f"ffn_wr_tot1_{ph}"] = s_w1.n

            R1 = R1_t.ap()
            R2 = R2_t.ap()

            emit_phase0()
            ffn(0, xb[0], R1[0],
                prev_gu=[[], []],
                prev_dn=[[(s_cc, "cc_dn_0"), (s_pe, "gu_fin_0_%d" % (NSP - 2))],
                         [(s_cc, "cc_dn_0"), (s_pe, "gu_fin_0_%d" % (NSP - 1))]],
                prev_xn=None,
                prev_r=[[], []],
                prev_wr=[[], []],
                hook=ffn0_hook,
                pre_ln=True)
            ffn(1, xb[1], R1[1],
                prev_gu=[[(s_pe, "dn_fin_0_126")], [(s_pe, "dn_fin_0_127")]],
                prev_dn=[[(s_pe, "gu_fin_1_%d" % (NSP - 2))],
                         [(s_pe, "gu_fin_1_%d" % (NSP - 1))]],
                prev_xn="gu_fin_0_%d" % (NSP - 1),
                prev_r=[[(s_dv, "resadd_0_14")], [(s_dv, "resadd_0_15")]],
                prev_wr=[[(s_w0, "rwr_0_14")], [(s_w1, "rwr_0_15")]])

            for bi in range(2):
                ph = 10 + bi
                b.wait("sync", sW[0], f"ffn_wr_tot0_{bi}")
                b.wait("sync", sW[1], f"ffn_wr_tot1_{bi}")
                if bi == 0:
                    b.wait("sync", s_dv, "resadd_1_15")
                    b.wait("sync", s_pe, "dn_mm_1_15")
                    M["bvr_ld"] = b.op("sync", lambda e: e.dma_start(
                        out=bv_row_ap(), in_=bvr), s_in, 16)
                    b.wait("tensor", s_in, "bvr_ld")
                    for i in range(D // 512):
                        pg2 = psA if i % 2 == 0 else psB
                        if i >= 2:
                            b.wait("tensor", s_dv, f"bvcp{i-2}")
                        M[f"bvrep{i}"] = b.op("tensor", lambda e, i=i, pg2=pg2: e.matmul(
                            pg2[:, :], ones_row[:, :], bv_row_ap()[:, 512 * i:512 * i + 512],
                            start=True, stop=True, skip_group_check=True), s_pe)
                        b.wait("vector", s_pe, f"bvrep{i}")
                        M[f"bvcp{i}"] = b.op("vector", lambda e, i=i, pg2=pg2: e.tensor_copy(
                            out=bv_rep_ap()[:, 512 * i:512 * i + 512], in_=pg2[:, :]), s_dv)
                    M["bv_done"] = s_dv.n
                emit_R_loads(ph, R1[bi],
                             [[(s_dv, "resadd_1_14")], [(s_dv, "resadd_1_15")]] if bi == 0
                             else [[(s_dv, "nm1_10_14")], [(s_dv, "nm1_10_15")]])
                layer_norm(ph, prev_xn_key=("gu_fin_1_%d" % (NSP - 1)) if bi == 0 else "aw_fin_0_11",
                           prev_lnt_keys=None)
                for eb in range(12):
                    half = eb % 2
                    if eb >= 2:
                        b.wait("sync", s_pe, f"aw_fin_{bi}_{eb - 2}")
                    elif bi == 0:
                        b.wait("sync", s_cc, "cc_at")
                        b.wait("sync", s_pe, f"dn_fin_1_{126 + eb}")
                    else:
                        b.wait("sync", s_pe, f"aw_fin_0_{10 + eb}")
                    M[f"aw_ld_{bi}_{eb}"] = b.op("sync", lambda e, eb=eb, half=half: e.dma_start(
                        out=at_view(half), in_=WATkc[:, :, 512 * eb:512 * eb + 512]), sS[half], 16)
                    b.wait("tensor", sS[half], f"aw_ld_{bi}_{eb}")
                    b.wait("tensor", s_dv, f"xn_{ph}")
                    if eb < 8:
                        for sub in range(4):
                            ec = eb * 4 + sub
                            pg = psA if ec % 2 == 0 else psB
                            if ec >= 2:
                                b.wait("tensor", s_dv, f"qkev_{bi}_{ec - 2}")
                            else:
                                b.wait("tensor", s_dv, f"arep_{ph}")
                                b.wait("tensor", s_dv, f"brepc_{ph}")
                            for kc in range(16):
                                r = b.op("tensor", lambda e, half=half, kc=kc, sub=sub, pg=pg: e.matmul(
                                    pg[:, 0:TB], at_view(half)[:, kc, sub * 128:(sub + 1) * 128],
                                    x_n[:, kc * TB:(kc + 1) * TB],
                                    start=(kc == 0), stop=(kc == 15), skip_group_check=True),
                                    s_pe if kc == 15 else None)
                            M[f"qkmm_{bi}_{ec}"] = r
                            b.wait("vector", s_pe, f"qkmm_{bi}_{ec}")
                            if ec >= 2:
                                b.wait("vector", sQ[ec % 2], f"qkwr_{bi}_{ec - 2}")
                            elif bi == 1:
                                b.wait("vector", sQ[ec % 2], f"vwr_0_{4 * NTQ - 2 + ec}")
                            isq = ec < 16
                            M[f"qkev_{bi}_{ec}"] = b.op("vector", lambda e, ec=ec, pg=pg, isq=isq: e.tensor_scalar(
                                out=evb[ec % 2][:, 0:TB], in0=pg[:, 0:TB],
                                scalar1=QK_SCALE if isq else 1.0,
                                scalar2=bqk_sb[:, ec:ec + 1],
                                op0=ALU.mult, op1=ALU.add), s_dv)
                            row = (0 if isq else D) + (ec % 16) * 128
                            b.wait("sync", s_dv, f"qkev_{bi}_{ec}")
                            M[f"qkwr_{bi}_{ec}"] = b.op("sync", lambda e, bi=bi, row=row, ec=ec: e.dma_start(
                                out=QKIN[bi].ap()[row:row + 128, :], in_=evb[ec % 2][:, 0:TB]), sQ[ec % 2], 16)
                        M[f"aw_fin_{bi}_{eb}"] = M[f"qkmm_{bi}_{eb * 4 + 3}"]
                    else:
                        evs = eb - 8
                        for ts in range(NTQ):
                            vi = evs * NTQ + ts
                            pg = psA if vi % 2 == 0 else psB
                            if vi >= 2:
                                b.wait("tensor", s_dv, f"vev_{bi}_{vi - 2}")
                            elif evs == 0:
                                b.wait("tensor", s_dv, f"qkev_{bi}_30")
                                b.wait("tensor", s_dv, f"qkev_{bi}_31")
                            for kc in range(16):
                                r = b.op("tensor", lambda e, half=half, kc=kc, ts=ts, pg=pg: e.matmul(
                                    pg[0:TQ, 0:512],
                                    x_n[:, kc * TB + ts * TQ: kc * TB + (ts + 1) * TQ],
                                    at_view(half)[:, kc, :],
                                    start=(kc == 0), stop=(kc == 15), skip_group_check=True),
                                    s_pe if kc == 15 else None)
                            M[f"vmm_{bi}_{vi}"] = r
                            b.wait("vector", s_pe, f"vmm_{bi}_{vi}")
                            if vi >= 2:
                                b.wait("vector", sQ[vi % 2], f"vwr_{bi}_{vi - 2}")
                            else:
                                b.wait("vector", sQ[vi % 2], f"qkwr_{bi}_{30 + vi}")
                            M[f"vev_{bi}_{vi}"] = b.op("vector", lambda e, vi=vi, evs=evs, pg=pg: e.tensor_tensor(
                                out=evb[vi % 2][0:TQ, 0:512], in0=pg[0:TQ, 0:512],
                                in1=bv_rep_ap()[0:TQ, 512 * evs:512 * evs + 512], op=ALU.add), s_dv)
                            b.wait("sync", s_dv, f"vev_{bi}_{vi}")
                            M[f"vwr_{bi}_{vi}"] = b.op("sync", lambda e, bi=bi, vi=vi, evs=evs, ts=ts: e.dma_start(
                                out=VIN[bi].ap().rearrange("t (h s) -> t h s", s=VS)[
                                    ts * TQ:(ts + 1) * TQ, 4 * evs:4 * evs + 4, 0:HD],
                                in_=evb[vi % 2][0:TQ, 0:512].rearrange("t (h s) -> t h s", s=HD)), sQ[vi % 2], 16)
                        M[f"aw_fin_{bi}_{eb}"] = M[f"vmm_{bi}_{(eb - 8 + 1) * NTQ - 1}"]
                if bi == 1:
                    b.wait("sync", s_cc, "cc_v0")
                for ts in range(NTQ):
                    M[f"vones_{bi}"] = b.op("sync", lambda e, bi=bi, ts=ts: e.dma_start(
                        out=VIN[bi].ap().rearrange("(s p) (h v) -> p s h v", p=TQ, v=VS)[:, ts, :, HD:VS],
                        in_=onesv[0:TQ, 0:NH].unsqueeze(2),
                        allow_slow_non_contiguous=True), s_vo, 16)
                M[f"q0_tot_{bi}"] = sQ[0].n
                M[f"q1_tot_{bi}"] = sQ[1].n
                b.wait("gpsimd", sQ[0], f"q0_tot_{bi}")
                b.wait("gpsimd", sQ[1], f"q1_tot_{bi}")
                b.wait("gpsimd", s_vo, f"vones_{bi}")
                M[f"cc_qk{bi}"] = b.op("gpsimd", lambda e, bi=bi: e.collective_compute(
                    "AllGather", ALU.bypass, replica_groups=RG,
                    ins=[QKIN[bi].ap()], outs=[QKOUT[bi].ap()]), s_cc)
                M[f"cc_v{bi}"] = b.op("gpsimd", lambda e, bi=bi: e.collective_compute(
                    "AllGather", ALU.bypass, replica_groups=RG,
                    ins=[VIN[bi].ap()], outs=[VOUT[bi].ap()]), s_cc)

            def kv_rank(gb):
                return gb if gb < 8 else 15 - gb

            def slot(gb):
                return (0 if gb < 8 else 8) + kv_rank(gb)

            def kj_off(j, gb):
                return off_k + (j * NBLK + slot(gb)) * TB

            def vj_off(j, gb):
                return off_v + (j * NBLK + slot(gb)) * NTQ * VS

            def qj_off(j, gb):
                return off_q + (j * NBLK + slot(gb)) * TB

            for hf in range(2):
                skv_h = s_kv if hf == 0 else s_k1
                b.wait("sync", s_cc, "cc_v0" if hf == 0 else "cc_v1")
                for j in range(2):
                    b.op("sync", lambda e, hf=hf, j=j: e.dma_start(
                        out=AR(off_k + (j * NBLK + hf * 8) * TB, 8 * TB)
                            .rearrange("p (r t) -> p r t", r=8),
                        in_=bass.AP(QKOUT[hf],
                                    b.dynoff(e, 256 * TB, D * TB + (128 * TB) * j),
                                    [[TB, HD], [2 * D * TB, 8], [1, TB]])), skv_h, 16)
                    b.op("sync", lambda e, hf=hf, j=j: e.dma_start(
                        out=AR(off_v + (j * NBLK + hf * 8) * NTQ * VS, 8 * NTQ * VS)
                            .rearrange("p (rs v) -> p rs v", v=VS)[0:TQ],
                        in_=bass.AP(VOUT[hf],
                                    b.dynoff(e, 2 * VS, VS * j),
                                    [[NH * VS, TQ], [TQ * NH * VS, 8 * NTQ], [1, VS]])), skv_h, 16)
                for j in range(2):
                    b.op("sync", lambda e, hf=hf, j=j: e.dma_start(
                        out=AR(off_q + (j * NBLK + hf * 8) * TB, 8 * TB)
                            .rearrange("p (r t) -> p r t", r=8)[0:HD],
                        in_=bass.AP(QKOUT[hf],
                                    b.dynoff(e, 256 * TB, (128 * TB) * j),
                                    [[TB, HD], [2 * D * TB, 8], [1, TB]])), skv_h, 16)
                if hf == 0:
                    M["kv_half0"] = s_kv.n
            M["kv_all"] = s_k1.n

            phase_barrier(99)
            b.wait("gpsimd", s_dv, "resadd_1_15")
            b.wait("gpsimd", s_pe, "dn_mm_1_15")
            M["mz"] = b.op("gpsimd", lambda e: e.memset(maskT_ap(), 0.0), s_gp)
            b.wait("gpsimd", s_gp, "mz")
            for tkc in range(NTQ):
                M["mbuild"] = b.op("gpsimd", lambda e, tkc=tkc: e.affine_select(
                    out=maskT_ap()[:, tkc * TB:(tkc + 1) * TB],
                    in_=maskT_ap()[:, tkc * TB:(tkc + 1) * TB],
                    compare_op=ALU.is_ge, fill=MASKVAL, base=-tkc * TQ,
                    pattern=[[1, TB]], channel_multiplier=-1), s_gp)
            b.wait("tensor", s_kv, "kv_half0")
            for gb in range(NBLK):
                if gb == 8:
                    b.wait("tensor", s_k1, "kv_all")
                for j in range(2):
                    nch_ = (gb + 1) * NTQ
                    ci = 0
                    if gb + j > 0:
                        pj, pgb = (0, gb) if j == 1 else (1, gb - 1)
                        for tqs in range(NTQ):
                            b.wait("tensor", s_dv, f"aofin_{pgb}_{pj}_{tqs}")
                    for kv in range(gb + 1):
                        for tkc in range(NTQ):
                            pss = psA if ci % 2 == 0 else psB
                            half = ci % 2
                            if ci >= 2:
                                b.wait("tensor", s_ac, f"exp_{gb}_{j}_{ci - 2}")
                            elif gb + j > 0:
                                pj2, pgb2 = (0, gb) if j == 1 else (1, gb - 1)
                                pn2 = (pgb2 + 1) * NTQ
                                li = pn2 - 1 - ((pn2 - 1 - ci) % 2)
                                if li >= 0:
                                    b.wait("tensor", s_ac, f"exp_{pgb2}_{pj2}_{li}")
                            M[f"smm_{gb}_{j}_{ci}"] = b.op("tensor", lambda e, j=j, kv=kv, tkc=tkc, gb=gb, pss=pss: e.matmul(
                                pss[0:TQ, 0:TB],
                                arena[0:HD, kj_off(j, kv) + tkc * TQ: kj_off(j, kv) + (tkc + 1) * TQ],
                                AR(qj_off(j, gb), TB)[0:HD],
                                start=True, stop=True, skip_group_check=True), s_pe)
                            if ci >= 2:
                                b.wait("scalar", s_pe, f"pv_{gb}_{j}_{ci - 2}")
                            elif gb + j > 0:
                                pj2, pgb2 = (0, gb) if j == 1 else (1, gb - 1)
                                b.wait("scalar", s_pe, f"pvmm_{pgb2}_{pj2}")
                            if kv == gb:
                                b.wait("vector", s_pe, f"smm_{gb}_{j}_{ci}")
                                b.wait("vector", s_gp, "mbuild")
                                if tkc > 0:
                                    b.wait("vector", s_ac, f"exp_{gb}_{j}_{ci - 1}")
                                elif gb + j > 0:
                                    pj2, pgb2 = (0, gb) if j == 1 else (1, gb - 1)
                                    pn2 = (pgb2 + 1) * NTQ
                                    b.wait("vector", s_ac, f"exp_{pgb2}_{pj2}_{pn2 - 1}")
                                M[f"msk_{gb}_{j}_{ci}"] = b.op("vector", lambda e, tkc=tkc, pss=pss: e.tensor_add(
                                    out=smask_ap()[0:TQ, :], in0=pss[0:TQ, 0:TB],
                                    in1=maskT_ap()[0:TQ, tkc * TB:(tkc + 1) * TB]), s_dv)
                                b.wait("scalar", s_dv, f"msk_{gb}_{j}_{ci}")
                                M[f"exp_{gb}_{j}_{ci}"] = b.op("scalar", lambda e, half=half: e.activation(
                                    out=AR(off_p + half * TB, TB)[0:TQ, :],
                                    in_=smask_ap()[0:TQ, 0:TB], func=AF.Exp, bias=EXP_BIAS), s_ac)
                            else:
                                b.wait("scalar", s_pe, f"smm_{gb}_{j}_{ci}")
                                M[f"exp_{gb}_{j}_{ci}"] = b.op("scalar", lambda e, half=half, pss=pss: e.activation(
                                    out=AR(off_p + half * TB, TB)[0:TQ, :],
                                    in_=pss[0:TQ, 0:TB], func=AF.Exp, bias=EXP_BIAS), s_ac)
                            b.wait("tensor", s_ac, f"exp_{gb}_{j}_{ci}")
                            for tqs in range(NTQ):
                                r = b.op("tensor", lambda e, half=half, tqs=tqs, j=j, kv=kv, tkc=tkc, ci=ci, nch_=nch_: e.matmul(
                                    pqo[tqs][0:TQ, 0:VS],
                                    arena[0:TQ, off_p + half * TB + tqs * TQ: off_p + half * TB + (tqs + 1) * TQ],
                                    AR(vj_off(j, kv), NTQ * VS).rearrange("p (s v) -> p s v", v=VS)[0:TQ, tkc, :],
                                    start=(ci == 0), stop=(ci == nch_ - 1),
                                    skip_group_check=True),
                                    s_pe if tqs == NTQ - 1 else None)
                            M[f"pv_{gb}_{j}_{ci}"] = r
                            ci += 1
                    M[f"pvmm_{gb}_{j}"] = r
                    for tqs in range(NTQ):
                        b.wait("vector", s_pe, f"pvmm_{gb}_{j}")
                        if gb + j > 0:
                            pj, pgb = (0, gb) if j == 1 else (1, gb - 1)
                            b.wait("vector", s_ao, f"ao_grp_{pgb}_{pj}")
                        M[f"lc_{gb}_{j}_{tqs}"] = b.op("vector", lambda e, tqs=tqs: e.tensor_copy(
                            out=lrec[0:TQ, 0:1], in_=pqo[tqs][0:TQ, HD:HD + 1]), s_dv)
                        b.wait("vector", s_dv, f"lc_{gb}_{j}_{tqs}")
                        M[f"lr_{gb}_{j}_{tqs}"] = b.op("vector", lambda e: e.reciprocal(
                            out=lrec[0:TQ, 1:2], in_=lrec[0:TQ, 0:1]), s_dv)
                        b.wait("vector", s_dv, f"lr_{gb}_{j}_{tqs}")
                        M[f"aofin_{gb}_{j}_{tqs}"] = b.op("vector", lambda e, tqs=tqs, j=j: e.tensor_scalar(
                            out=AR(off_ao + j * NTQ * HD + tqs * HD, HD)[0:TQ, :],
                            in0=pqo[tqs][0:TQ, 0:HD], scalar1=lrec[0:TQ, 1:2],
                            scalar2=None, op0=ALU.mult), s_dv)
                        sl = tqs % 2
                        ptt = pst1 if sl == 0 else pst2
                        b.wait("tensor", s_dv, f"aofin_{gb}_{j}_{tqs}")
                        if gb + j + tqs > 0:
                            b.wait("tensor", s_dv, "aotc_prev")
                        M["aotr_cur"] = b.op("tensor", lambda e, tqs=tqs, j=j, ptt=ptt: e.transpose(
                            ptt[:, 0:128].bitcast(BF16)[:, 0:TQ],
                            AR(off_ao + j * NTQ * HD + tqs * HD, HD)[0:TQ, :],
                            idbf[0:TQ, 0:TQ]), s_pe)
                        b.wait("vector", s_pe, "aotr_cur")
                        if gb + j + tqs > 0:
                            b.wait("vector", s_ao, "aowr_prev")
                        M["aotc_prev"] = b.op("vector", lambda e, sl=sl, ptt=ptt: e.tensor_copy(
                            out=AR(off_aot + sl * TQ, TQ),
                            in_=ptt[:, 0:128].bitcast(BF16)[:, 0:TQ]), s_dv)
                        b.wait("sync", s_dv, "aotc_prev")
                        b.op("sync", lambda e, gb=gb, j=j, tqs=tqs, sl=sl: e.dma_start(
                            out=AIN[0 if gb < 8 else 1].ap()[
                                128 * j:128 * j + 128,
                                (gb % 8) * TB + tqs * TQ:(gb % 8) * TB + (tqs + 1) * TQ],
                            in_=AR(off_aot + sl * TQ, TQ)), s_ao, 16)
                        M["aowr_prev"] = s_ao.n
                    M[f"ao_grp_{gb}_{j}"] = s_ao.n
                M[f"fl_fin_{gb}"] = M[f"pvmm_{gb}_1"]
                if gb == 7:
                    M["ao_tot_0"] = s_ao.n
                    b.wait("gpsimd", s_ao, "ao_tot_0")
                    M["cc_ao0"] = b.op("gpsimd", lambda e: e.collective_compute(
                        "AllGather", ALU.bypass, replica_groups=RG,
                        ins=[AIN[0].ap()], outs=[AOUT[0].ap()]), s_cc)
            M["ao_tot_1"] = s_ao.n
            b.wait("gpsimd", s_ao, "ao_tot_1")
            M["cc_ao1"] = b.op("gpsimd", lambda e: e.collective_compute(
                "AllGather", ALU.bypass, replica_groups=RG,
                ins=[AIN[1].ap()], outs=[AOUT[1].ap()]), s_cc)

            for bi in range(2):
                ph = 20 + bi
                phase_barrier(ph)
                b.wait("sync", s_cc, "cc_ao0" if bi == 0 else "cc_ao1")
                if bi == 1:
                    b.wait("sync", s_pe, f"op_all_0")
                coeff = TB if bi == 0 else -TB
                cnst = 0 if bi == 0 else 7 * TB
                for he in range(2):
                    b.op("sync", lambda e, bi=bi, he=he, coeff=coeff, cnst=cnst: e.dma_start(
                        out=arena[:, off_att:off_att + 16 * TB]
                            .rearrange("p (ec t) -> p ec t", t=TB)[:, he::2, :],
                        in_=bass.AP(AOUT[bi], b.dynoff(e, coeff, cnst + 128 * he),
                                    [[8 * TB, 128], [256 * 8 * TB, 8], [1, TB]])), s_q0, 16)
                M[f"atld_{bi}"] = s_q0.n
                for wb in range(4):
                    half = wb % 2
                    if wb >= 2:
                        b.wait("sync", s_pe, f"op_fin_{bi}_{wb - 2}")
                    elif bi == 0:
                        b.wait("sync", s_pe, "aw_fin_1_10" if wb == 0 else "aw_fin_1_11")
                    else:
                        b.wait("sync", s_pe, f"op_fin_0_{2 + wb}")
                    M[f"ow_ld_{bi}_{wb}"] = b.op("sync", lambda e, wb=wb, half=half: e.dma_start(
                        out=at_view(half), in_=WATkc[:, :, 512 * (12 + wb):512 * (12 + wb) + 512]), sS[half], 16)
                    b.wait("tensor", sS[half], f"ow_ld_{bi}_{wb}")
                    b.wait("tensor", s_q0, f"atld_{bi}")
                    for sub in range(4):
                        dc = wb * 4 + sub
                        par = dc % 2
                        pg = psE if dc % 2 == 0 else psF
                        if dc >= 2:
                            b.wait("sync", s_dv, f"resfree_{ph}_{dc - 2}")
                        elif bi == 0:
                            b.wait("sync", s_dv, "nm1_11_%d" % (14 + dc))
                        else:
                            b.wait("sync", s_dv, "resfree_20_%d" % (14 + dc))
                        M[f"res_ld_{ph}_{dc}"] = b.op("sync", lambda e, dc=dc, par=par, bi=bi: e.dma_start(
                            out=rbuf[par][:, :], in_=R1[bi][128 * dc:128 * dc + 128, :]), sR[par], 16)
                        if dc >= 2:
                            b.wait("tensor", s_dv, f"opadd_{bi}_{dc - 2}")
                        for ec in range(16):
                            r = b.op("tensor", lambda e, half=half, ec=ec, sub=sub, pg=pg: e.matmul(
                                pg[:, 0:TB], at_view(half)[:, ec, sub * 128:(sub + 1) * 128],
                                arena[:, off_att + ec * TB: off_att + (ec + 1) * TB],
                                start=(ec == 0), stop=(ec == 15), skip_group_check=True),
                                s_pe if ec == 15 else None)
                        M[f"opmm_{bi}_{dc}"] = r
                        b.wait("vector", s_pe, f"opmm_{bi}_{dc}")
                        b.wait("vector", sR[par], f"res_ld_{ph}_{dc}")
                        if dc >= 2:
                            b.wait("vector", sW[par], f"r2wr_{bi}_{dc - 2}")
                        elif bi == 0:
                            b.wait("vector", sW[par], f"rwr_1_{14 + dc}")
                        else:
                            b.wait("vector", sW[par], f"r2wr_0_{14 + dc}")
                        M[f"opadd_{bi}_{dc}"] = b.op("vector", lambda e, par=par, dc=dc, pg=pg: e.scalar_tensor_tensor(
                            out=outb[par][:, :], in0=pg[:, 0:TB],
                            scalar=bo_sb[:, dc:dc + 1], in1=rbuf[par][:, :],
                            op0=ALU.add, op1=ALU.add), s_dv)
                        M[f"resfree_{ph}_{dc}"] = M[f"opadd_{bi}_{dc}"]
                        b.wait("gpsimd", s_dv, f"opadd_{bi}_{dc}")
                        M[f"r2wr_{bi}_{dc}"] = b.op("gpsimd", lambda e, bi=bi, dc=dc, par=par: e.dma_start(
                            out=R2[bi][128 * dc:128 * dc + 128, :], in_=outb[par][:, :]), sW[par], 16)
                    M[f"op_fin_{bi}_{wb}"] = M[f"opmm_{bi}_{4 * wb + 3}"]
                M[f"op_all_{bi}"] = M[f"opmm_{bi}_15"]
                M[f"op_wr_tot0_{bi}"] = s_w0.n
                M[f"op_wr_tot1_{bi}"] = s_w1.n

            ffn(2, R2[0], ob[0],
                prev_gu=[[(s_pe, "op_fin_1_2")], [(s_pe, "op_fin_1_3")]],
                prev_dn=[[(s_pe, "gu_fin_2_%d" % (NSP - 2))],
                         [(s_pe, "gu_fin_2_%d" % (NSP - 1))]],
                prev_xn="op_fin_1_3",
                prev_r=[[(s_dv, "resfree_21_14"), (s_w0, "op_wr_tot0_0"), (s_w1, "op_wr_tot1_0")],
                        [(s_dv, "resfree_21_15")]],
                prev_wr=[[(s_w0, "r2wr_1_14")], [(s_w1, "r2wr_1_15")]])
            ffn(3, R2[1], ob[1],
                prev_gu=[[(s_pe, "dn_fin_2_126")], [(s_pe, "dn_fin_2_127")]],
                prev_dn=[[(s_pe, "gu_fin_3_%d" % (NSP - 2))],
                         [(s_pe, "gu_fin_3_%d" % (NSP - 1))]],
                prev_xn="gu_fin_2_%d" % (NSP - 1),
                prev_r=[[(s_dv, "resadd_2_14"), (s_w0, "op_wr_tot0_1"), (s_w1, "op_wr_tot1_1")],
                        [(s_dv, "resadd_2_15")]],
                prev_wr=[[(s_w0, "rwr_2_14")], [(s_w1, "rwr_2_15")]])

            b.wait("sync", s_w0, "ffn_wr_tot0_3")
            b.wait("sync", s_w1, "ffn_wr_tot1_3")

        b.begin(None, None)
        program()
        dry_counts = [s.n for s in sems]
        dry_M = dict(b.M)
        with nc.Block() as blk:
            for name, deco in (("sync", blk.sync), ("gpsimd", blk.gpsimd),
                               ("vector", blk.vector), ("scalar", blk.scalar),
                               ("tensor", blk.tensor)):
                def mk(name):
                    def f(eng):
                        b.begin(name, eng)
                        program()
                        assert [s.n for s in sems] == dry_counts, name
                        assert b.M == dry_M, name
                    return f
                deco(mk(name))

    return nc


_PI = pi_perm()


def _shard_inputs(L0_attn_x, L0_attn_w_qkv, L0_attn_b_qkv, L0_attn_w_out,
                  L0_attn_b_out, L0_ffn_gate_W_packed, L0_ffn_gate_scales,
                  L0_ffn_up_W_packed, L0_ffn_up_scales, L0_ffn_down_W_packed,
                  L0_ffn_down_scales, MLP, S):
    TB = S // NBLK
    AN = MLP // 2048
    x = np.ascontiguousarray(np.asarray(L0_attn_x, dtype=np.float32)[0])
    xT = np.ascontiguousarray(x.T[_PI])
    wq = np.asarray(L0_attn_w_qkv, dtype=np.float32)
    wo = np.asarray(L0_attn_w_out, dtype=np.float32)
    bq = np.asarray(L0_attn_b_qkv, dtype=np.float32)
    bo_ = np.asarray(L0_attn_b_out, dtype=np.float32)
    wqT = np.ascontiguousarray(wq.T[_PI])
    woT = np.ascontiguousarray(wo.T[:, _PI])

    def u8(a):
        return np.ascontiguousarray(np.asarray(a).view(np.uint8)[:, 0::4])

    g8, u8_, d8 = u8(L0_ffn_gate_W_packed), u8(L0_ffn_up_W_packed), u8(L0_ffn_down_W_packed)
    gs = np.asarray(L0_ffn_gate_scales, dtype=np.float32)
    us = np.asarray(L0_ffn_up_scales, dtype=np.float32)
    ds = np.asarray(L0_ffn_down_scales, dtype=np.float32)
    d8p = d8[_PI]
    dsp_ = ds[_PI]

    awc = np.concatenate([wqT, woT], axis=1)
    in_maps = []
    for c in range(8):
        m = {}
        m["xb0"] = np.ascontiguousarray(xT[:, TB * c:TB * (c + 1)])
        m["xb1"] = np.ascontiguousarray(xT[:, TB * (15 - c):TB * (16 - c)])
        m["gpk"] = np.ascontiguousarray(g8[:, 128 * c:128 * (c + 1)].T)
        m["upk"] = np.ascontiguousarray(u8_[:, 128 * c:128 * (c + 1)].T)
        pidx = 8 * c + np.arange(128) // 16
        m["gsc"] = np.ascontiguousarray(gs[:, pidx].T)
        m["usc"] = np.ascontiguousarray(us[:, pidx].T)
        nbw = MLP // 16
        dslice = d8p[:, nbw * c: nbw * (c + 1)]
        m["dpk"] = np.ascontiguousarray(
            dslice.T.reshape(AN, 128, 2048).transpose(1, 0, 2).reshape(128, AN * 2048))
        a_i = np.arange(AN)
        p_i = np.arange(128)
        col = (MLP // 256) * c + (a_i[None, :] * 8 + (p_i[:, None] // 16))
        m["dsc"] = np.ascontiguousarray(
            dsp_[:, col.reshape(-1)].reshape(2048, 128, AN).transpose(1, 2, 0).reshape(128, AN * 2048))
        m["aw"] = np.ascontiguousarray(awc[256 * c:256 * (c + 1), :])
        m["bqk"] = np.ascontiguousarray(bq[0:4096].reshape(32, 128).T)
        m["bvr"] = np.ascontiguousarray(bq[4096:6144].reshape(1, 2048))
        m["bo"] = np.ascontiguousarray(bo_[_PI].reshape(16, 128).T)
        in_maps.append(m)
    return in_maps


def _assemble(results, MLP, S):
    TB = S // NBLK
    inv = np.empty_like(_PI)
    inv[_PI] = np.arange(D)
    out = np.empty((1, S, D), dtype=np.float32)
    for c in range(8):
        out[0, TB * c:TB * (c + 1), :] = results[c]["ob0"].T[:, inv]
        out[0, TB * (15 - c):TB * (16 - c), :] = results[c]["ob1"].T[:, inv]
    return out


def kernel(**inputs):
    MLP, S = 16384, 8192
    in_maps = _shard_inputs(MLP=MLP, S=S, **inputs)
    nc = build_program(MLP, S)
    res = run_bass_kernel_spmd(nc, in_maps, core_ids=list(range(8)))
    return _assemble(res.results, MLP, S)


# version 15
# speedup vs baseline: 1.0734x; 1.0006x over previous
import numpy as np
import ml_dtypes
from contextlib import ExitStack
import concourse.bass as bass
import concourse.mybir as mybir
from concourse.bass_utils import run_bass_kernel_spmd

F32 = mybir.dt.float32
BF16 = mybir.dt.bfloat16
U8 = mybir.dt.uint8
AF = mybir.ActivationFunctionType
ALU = mybir.AluOpType

D = 2048
NH = 16
HD = 128
NBLK = 16
QK_SCALE = float(1.0 / np.sqrt(HD))
EXP_BIAS = -16.0
MASKVAL = -1e9
LN_EPS = 1e-5


def pi_perm():
    idx = []
    for c in range(8):
        idx.extend(range(256 * c, 256 * c + 256, 2))
        idx.extend(range(256 * c + 1, 256 * c + 256, 2))
    return np.array(idx, dtype=np.int64)


class Sem:
    def __init__(self, ctx, nc, name):
        self.h = ctx.enter_context(nc.semaphore(name))
        self.n = 0


class B:

    def __init__(self, sems):
        self.sems = sems
        self.mode = None
        self.eng = None
        self.M = {}

    def begin(self, mode, eng):
        self.mode = mode
        self.eng = eng
        for s in self.sems:
            s.n = 0

    def op(self, name, fn, sem=None, k=1):
        if self.mode == name:
            ins = fn(self.eng)
            if sem is not None:
                ins.then_inc(sem.h, k)
        if sem is not None:
            sem.n += k
            return sem.n
        return None

    def dynoff(self, e, coeff, const):
        if getattr(self, "_dreg_eng", None) is not e:
            self._dreg_eng = e
            self._pid = e.partition_id()
            self._dbase = e.alloc_register()
            self._dreg = e.alloc_register()
            self._dcoeff = None
        if self._dcoeff != coeff:
            self._dcoeff = coeff
            e.reg_alu(self._dbase, self._pid, coeff, mybir.AluOpType.mult)
        e.reg_alu(self._dreg, self._dbase, const, mybir.AluOpType.add)
        return self._dreg

    def wait(self, name, sem, key):
        if isinstance(key, str):
            if key not in self.M:
                assert self.mode is None, f"missing milestone {key} in live pass"
                return
            val = self.M[key]
        else:
            val = key
        if self.mode == name:
            self.eng.wait_ge(sem.h, val)


DEC_INLINE = True


def build_program(MLP, S):
    TB = S // NBLK
    TQ = min(128, TB)
    NTQ = TB // TQ
    NG = MLP // 128
    NSP = MLP // 256
    AN = MLP // 2048
    VS = HD + 1
    NCH = min(4096, MLP)
    assert MLP % 2048 == 0 and S % NBLK == 0

    nc = bass.Bass("TRN2", target_bir_lowering=False)

    xb = [nc.dram_tensor(f"xb{b}", [D, TB], F32, kind="ExternalInput").ap() for b in range(2)]
    gpk = nc.dram_tensor("gpk", [128, MLP], U8, kind="ExternalInput").ap()
    gsc = nc.dram_tensor("gsc", [128, MLP], F32, kind="ExternalInput").ap()
    upk = nc.dram_tensor("upk", [128, MLP], U8, kind="ExternalInput").ap()
    usc = nc.dram_tensor("usc", [128, MLP], F32, kind="ExternalInput").ap()
    dpk = nc.dram_tensor("dpk", [128, AN * D], U8, kind="ExternalInput").ap()
    dsc = nc.dram_tensor("dsc", [128, AN * D], F32, kind="ExternalInput").ap()
    aw = nc.dram_tensor("aw", [256, 4 * D], F32, kind="ExternalInput").ap()
    bqk = nc.dram_tensor("bqk", [128, 32], F32, kind="ExternalInput").ap()
    bvr = nc.dram_tensor("bvr", [1, D], F32, kind="ExternalInput").ap()
    bo = nc.dram_tensor("bo", [128, 16], F32, kind="ExternalInput").ap()
    ob = [nc.dram_tensor(f"ob{b}", [D, TB], F32, kind="ExternalOutput").ap() for b in range(2)]

    NCHK = max(MLP // 4096, 1)
    NCHC = MLP // NCHK
    blob_gu_t = [nc.dram_tensor(f"blob_gu{k}", [512, NCHC], BF16) for k in range(NCHK)]
    blob_dn_t = [nc.dram_tensor(f"blob_dn{j}", [2 * 8 * 128, AN * 128], BF16)
                 for j in range(2)]
    blob_at_t = nc.dram_tensor("blob_at", [256, 4 * D], BF16)
    WGU_t = [nc.dram_tensor(f"WGU{k}", [8 * 512, NCHC], BF16, addr_space="Shared") for k in range(NCHK)]
    WDN_t = [nc.dram_tensor(f"WDN{j}", [8 * 2 * 8 * 128, AN * 128], BF16, addr_space="Shared")
             for j in range(2)]
    WAT_t = nc.dram_tensor("WAT", [8 * 256, 4 * D], BF16, addr_space="Shared")
    QKIN = [nc.dram_tensor(f"QKIN{i}", [2 * D, TB], BF16) for i in range(2)]
    VIN = [nc.dram_tensor(f"VIN{i}", [TB, NH * VS], BF16) for i in range(2)]
    QKOUT = [nc.dram_tensor(f"QKOUT{i}", [8 * 2 * D, TB], BF16, addr_space="Shared") for i in range(2)]
    VOUT = [nc.dram_tensor(f"VOUT{i}", [8 * TB, NH * VS], BF16, addr_space="Shared") for i in range(2)]
    AIN = [nc.dram_tensor(f"AIN{i}", [256, 8 * TB], BF16) for i in range(2)]
    AOUT = [nc.dram_tensor(f"AOUT{i}", [8 * 256, 8 * TB], BF16, addr_space="Shared") for i in range(2)]
    R1_t = nc.dram_tensor("R1", [2, D, TB], F32)
    R2_t = nc.dram_tensor("R2", [2, D, TB], F32)

    with ExitStack() as ctx:
        def sbt(name, shape, dt):
            return ctx.enter_context(nc.sbuf_tensor(name, shape, dt))

        off_k = 0
        off_v = off_k + 2 * NBLK * TB
        off_q = off_v + 2 * NBLK * NTQ * VS
        off_p = off_q + 2 * NBLK * TB
        off_ao = off_p + 2 * TB
        off_aot = off_ao + 2 * NTQ * HD
        off_att = off_aot + 2 * TQ
        off_mask = off_att + 16 * TB
        off_smask = off_mask + NTQ * TB
        attn_el = off_smask + 2 * TB
        DW = max(NCH // 2, D)
        N_SETS = 3
        N_OSETS = 3
        dk_pk = [0, NCH]
        _ub = 2 * NCH
        dk_dqu = [[_ub + (2 * s + i) * DW for i in range(2)] for s in range(N_SETS)]
        _fb = (_ub + 2 * N_SETS * DW) // 2
        dk_dqf = [[_fb + (4 * s + i) * DW for i in range(4)] for s in range(N_SETS)]
        dk_dqs = (_fb + 4 * N_SETS * DW) // 2
        dk_dqo = [2 * dk_dqs + 2 * NCH + s * DW for s in range(N_OSETS)]
        dk_awb = dk_dqo[0] + N_OSETS * DW
        decode_el = dk_awb + 4 * D
        AR_EL = max(NG * TB, attn_el, decode_el)
        arena = sbt("arena", [128, AR_EL], BF16)

        def AR(off, width):
            return arena[:, off:off + width]

        arf = arena[:, :].bitcast(F32)
        aru = arena[:, :].bitcast(U8)
        off_bvrep = off_k

        strm = sbt("strm", [128, 16384], BF16)
        x_n = sbt("x_n", [128, 16 * TB], BF16)
        rbuf = [sbt(f"rbuf{i}", [128, TB], F32) for i in range(2)]
        arep = sbt("arep", [128, TB], F32)
        brep = sbt("brep", [128, TB], F32)
        lnt = [sbt(f"lnt{i}", [128, TB], F32) for i in range(2)]
        sqb = lnt[0]
        gelu_b = [sbt(f"gelu{i}", [128, TB], BF16) for i in range(2)]
        outb = [sbt(f"outb{i}", [128, TB], F32) for i in range(2)]
        evb = [sbt(f"evb{i}", [128, 512], BF16) for i in range(4)]
        stat = sbt("stat", [1, 4 * TB], F32)
        ones_col = sbt("ones_col", [128, 1], F32)
        ones_row = sbt("ones_row", [1, 128], F32)
        bqk_sb = sbt("bqk_sb", [128, 32], F32)
        bo_sb = sbt("bo_sb", [128, 16], F32)
        onesv = sbt("onesv", [128, max(NH, 1)], BF16)
        lrec = sbt("lrec", [128, 2], F32)
        cb = sbt("cb", [128, 8], F32)
        idf = sbt("idf", [128, 128], F32)
        idbf = sbt("idbf", [128, 128], BF16)

        for i, val in enumerate([EXP_BIAS, LN_EPS, 1.0, 0.5, -2.0, -6.0, -1.0, -3.0]):
            nc.const_aps.register(F32, val, cb[:, i:i + 1])

        def dqu_ap(s, i, w):
            return aru[:, dk_dqu[s][i]:dk_dqu[s][i] + w]

        def dqf_ap(s, i, w):
            return arena[:, dk_dqf[s][i]:dk_dqf[s][i] + w]

        def dqs_ap(off, w):
            return arf[:, dk_dqs + off:dk_dqs + off + w]

        def dqo_ap(s, w):
            return arena[:, dk_dqo[s]:dk_dqo[s] + w]

        def pk_ap(i, off, w):
            return aru[:, dk_pk[i] + off:dk_pk[i] + off + w]

        awb_ap = lambda: arena[:, dk_awb:dk_awb + 4 * D]
        bv_row_ap = lambda: arf[0:1, off_v // 2:off_v // 2 + D]
        bv_rep_ap = lambda: arena[:, off_bvrep:off_bvrep + D]
        maskT_ap = lambda: arena[:, off_mask:off_mask + NTQ * TB]
        smask_ap = lambda: arf[:, off_smask // 2:off_smask // 2 + TB]

        psA = ctx.enter_context(nc.psum_tensor("psA", [128, 512], F32))
        psB = ctx.enter_context(nc.psum_tensor("psB", [128, 512], F32))
        psC = ctx.enter_context(nc.psum_tensor("psC", [128, 512], F32))
        psD = ctx.enter_context(nc.psum_tensor("psD", [128, 512], F32))
        psE = ctx.enter_context(nc.psum_tensor("psE", [128, 512], F32))
        psF = ctx.enter_context(nc.psum_tensor("psF", [128, 512], F32))
        pst1 = ctx.enter_context(nc.psum_tensor("pst1", [128, 512], F32))
        pst2 = ctx.enter_context(nc.psum_tensor("pst2", [128, 512], F32))
        pqo = [psC, psD, psE, psF][:NTQ]

        s_pk = Sem(ctx, nc, "s_pk")
        s_in = Sem(ctx, nc, "s_in")
        s_pq = Sem(ctx, nc, "s_pq")
        s_vo = Sem(ctx, nc, "s_vo")
        s_k1 = Sem(ctx, nc, "s_k1")
        s_b1 = Sem(ctx, nc, "s_b1")
        s_b2 = Sem(ctx, nc, "s_b2")
        s_s0 = Sem(ctx, nc, "s_s0")
        s_s1 = Sem(ctx, nc, "s_s1")
        s_r0 = Sem(ctx, nc, "s_r0")
        s_r1 = Sem(ctx, nc, "s_r1")
        s_w0 = Sem(ctx, nc, "s_w0")
        s_w1 = Sem(ctx, nc, "s_w1")
        s_q0 = Sem(ctx, nc, "s_q0")
        s_q1 = Sem(ctx, nc, "s_q1")
        s_q2 = Sem(ctx, nc, "s_q2")
        s_q3 = Sem(ctx, nc, "s_q3")
        s_kv = Sem(ctx, nc, "s_kv")
        s_ao = Sem(ctx, nc, "s_ao")
        s_bw = Sem(ctx, nc, "s_bw")
        s_aw = Sem(ctx, nc, "s_aw")
        s_cc = Sem(ctx, nc, "s_cc")
        s_pe = Sem(ctx, nc, "s_pe")
        s_ac = Sem(ctx, nc, "s_ac")
        s_dv = Sem(ctx, nc, "s_dv")
        s_gp = Sem(ctx, nc, "s_gp")
        sems = [s_pk, s_in, s_pq, s_vo, s_k1, s_b1, s_b2, s_s0, s_s1, s_r0, s_r1, s_w0, s_w1, s_q0, s_q1, s_q2, s_q3,
                s_kv, s_ao, s_bw, s_aw, s_cc, s_pe, s_ac, s_dv, s_gp]
        sS = [s_s0, s_s1]
        sR = [s_r0, s_r1]
        sW = [s_w0, s_w1]
        sQ = [s_q0, s_q1]
        sQ4 = [s_q0, s_q1, s_q2, s_q3]

        b = B(sems)
        M = b.M

        WGU_v = [t.ap().rearrange("(c q p) n -> p c q n", q=4, p=128) for t in WGU_t]
        WDN_v = [WDN_t[j].ap().rearrange("(c pl hc p) m -> p c pl hc m", pl=2, hc=8, p=128)
                 for j in range(2)]
        WATkc = WAT_t.ap().rearrange("(kc p) e -> p kc e", p=128)

        def gu_view(half):
            return strm[:, half * 8192:half * 8192 + 8192].rearrange(
                "p (c q n) -> p c q n", c=8, q=4)

        def dn_view(half):
            return strm[:, half * 8192:half * 8192 + 2 * AN * 128].rearrange(
                "p (pl m) -> p pl m", pl=2)

        def at_view(half):
            return strm[:, half * 8192:half * 8192 + 16 * 512].rearrange(
                "p (kc e) -> p kc e", kc=16)

        def inter_ap(g):
            return arena[:, g * TB:(g + 1) * TB]

        RG = [list(range(8))]

        def program():
            for i, val in enumerate([EXP_BIAS, LN_EPS, 1.0, 0.5, -2.0, -6.0, -1.0, -3.0]):
                b.op("gpsimd", lambda e, i=i, val=val: e.memset(cb[:, i:i + 1], val), s_gp)
            b.op("gpsimd", lambda e: e.memset(ones_col[:, :], 1.0), s_gp)
            b.op("gpsimd", lambda e: e.memset(ones_row[:, :], 1.0), s_gp)
            b.op("gpsimd", lambda e: e.memset(onesv[:, :], 1.0), s_gp)
            M["idz"] = b.op("gpsimd", lambda e: e.memset(idf[:, :], 0.0), s_gp)
            b.wait("gpsimd", s_gp, "idz")
            M["idsel"] = b.op("gpsimd", lambda e: e.affine_select(
                out=idf[:, :], in_=idf[:, :], compare_op=ALU.not_equal, fill=1.0,
                base=0, pattern=[[-1, 128]], channel_multiplier=1), s_gp)
            M["consts"] = s_gp.n
            for en in ("tensor", "vector", "scalar", "sync"):
                b.wait(en, s_gp, "consts")
            M["idbf"] = b.op("vector", lambda e: e.tensor_copy(out=idbf[:, :], in_=idf[:, :]), s_dv)

            b.op("sync", lambda e: e.dma_start(out=bqk_sb[:, :], in_=bqk), s_in, 16)
            b.op("sync", lambda e: e.dma_start(out=bo_sb[:, :], in_=bo), s_in, 16)
            M["bias_ld"] = s_in.n
            b.wait("vector", s_in, "bias_ld")
            M["bqs"] = b.op("vector", lambda e: e.tensor_scalar(
                out=bqk_sb[:, 0:16], in0=bqk_sb[:, 0:16], scalar1=QK_SCALE,
                scalar2=None, op0=ALU.mult), s_dv)

            blob_at = blob_at_t.ap()

            st = {"hist": [], "pkuse": [None, None], "ss": 0, "so": 0}

            def dq_pk_load(mat, bi_, pk_src, c0, w):
                if st["pkuse"][bi_] is not None:
                    b.wait("scalar", s_dv, st["pkuse"][bi_])
                M[f"pk_{mat}_{c0}"] = b.op("scalar", lambda e, pk_src=pk_src, bi_=bi_, c0=c0, w=w: e.dma_start(
                    out=pk_ap(bi_, 0, w), in_=pk_src[:, c0:c0 + w]), [s_pk, s_pq][bi_], 16)

            def dq_sc_load(mat, sc_src, c0, w, first=False):
                if st["hist"]:
                    b.wait("scalar", s_gp, f"v4_{st['hist'][-1][0]}")
                elif first:
                    b.wait("scalar", s_dv, "bqs")
                M[f"sc_{mat}_{c0}"] = b.op("scalar", lambda e, sc_src=sc_src, c0=c0, w=w: e.dma_start(
                    out=dqs_ap(0, w), in_=sc_src[:, c0:c0 + w]), s_in, 16)

            def dq_chunk(mat, bi_, c0, w, out_fn):
                for coff in range(0, w, DW):
                    sw = min(DW, w - coff)
                    for pl in range(2):
                        tag = f"{mat}_{c0 + coff}_{pl}"
                        ss = st["ss"]
                        st["ss"] = (ss + 1) % N_SETS
                        so = st["so"]
                        st["so"] = (so + 1) % N_OSETS
                        sbs = [s_bw, s_b1, s_b2]
                        hist = st["hist"]
                        ptN = hist[-N_SETS][0] if len(hist) >= N_SETS else None
                        ptO = hist[-N_OSETS][0] if len(hist) >= N_OSETS else None
                        scm = 1.0 if pl == 0 else (1.0 / 16.0)
                        sa = 0.5 if mat == "dn" else 1.0
                        mm_, sm_ = (7, 8) if pl == 0 else (0x70, 0x80)
                        b.wait("vector", [s_pk, s_pq][bi_], f"pk_{mat}_{c0}")
                        if ptN is not None:
                            b.wait("vector", s_ac, f"dqa_{ptN}")
                        M[f"nib_{tag}"] = b.op("vector", lambda e, bi_=bi_, coff=coff, w=sw, ss=ss, mm_=mm_, sm_=sm_: (
                            e.tensor_scalar(out=dqu_ap(ss, 0, w), in0=pk_ap(bi_, coff, w),
                                            scalar1=mm_, scalar2=None, op0=ALU.bitwise_and),
                            e.tensor_scalar(out=dqu_ap(ss, 1, w), in0=pk_ap(bi_, coff, w),
                                            scalar1=sm_, scalar2=None, op0=ALU.bitwise_and))[-1], s_dv)
                        b.wait("scalar", s_dv, f"nib_{tag}")
                        b.wait("scalar", s_gp, "consts")
                        if ptN is not None:
                            b.wait("scalar", s_gp, f"v4_{ptN}")
                            b.wait("scalar", s_dv, f"v3_{ptN}")
                        M[f"dqa_{tag}"] = b.op("scalar", lambda e, w=sw, ss=ss, scm=scm, sa=sa: (
                            e.activation(out=dqf_ap(ss, 0, w), in_=dqu_ap(ss, 0, w), func=AF.Copy, scale=0.5 * scm * sa),
                            e.activation(out=dqf_ap(ss, 2, w), in_=dqu_ap(ss, 0, w), func=AF.Relu, scale=0.5 * scm * sa, bias=-2.0 * sa),
                            e.activation(out=dqf_ap(ss, 3, w), in_=dqu_ap(ss, 0, w), func=AF.Relu, scale=scm * sa, bias=-6.0 * sa),
                            e.activation(out=dqf_ap(ss, 1, w), in_=dqu_ap(ss, 1, w), func=AF.Copy,
                                         scale=-0.25 * scm, bias=1.0))[-1], s_ac)
                        b.wait("vector", s_ac, f"dqa_{tag}")
                        M[f"v1_{tag}"] = b.op("vector", lambda e, w=sw, ss=ss: e.tensor_add(
                            out=dqf_ap(ss, 0, w), in0=dqf_ap(ss, 0, w), in1=dqf_ap(ss, 2, w)), s_dv)
                        b.wait("vector", s_dv, f"v1_{tag}")
                        M[f"v2_{tag}"] = b.op("vector", lambda e, w=sw, ss=ss: e.tensor_add(
                            out=dqf_ap(ss, 0, w), in0=dqf_ap(ss, 0, w), in1=dqf_ap(ss, 3, w)), s_dv)
                        b.wait("vector", s_dv, f"v2_{tag}")
                        M[f"v3_{tag}"] = b.op("vector", lambda e, w=sw, ss=ss: e.tensor_mul(
                            out=dqf_ap(ss, 0, w), in0=dqf_ap(ss, 0, w), in1=dqf_ap(ss, 1, w)), s_dv)
                        b.wait("gpsimd", s_dv, f"v3_{tag}")
                        b.wait("gpsimd", s_in, f"sc_{mat}_{c0}")
                        if ptO is not None:
                            b.wait("gpsimd", sbs[so], f"dq_wr_{ptO}")
                        M[f"v4_{tag}"] = b.op("gpsimd", lambda e, w=sw, ss=ss, so=so, coff=coff: e.tensor_mul(
                            out=dqo_ap(so, w), in0=dqf_ap(ss, 0, w), in1=dqs_ap(coff, w)), s_gp)
                        b.wait("gpsimd", s_gp, f"v4_{tag}")
                        fns = out_fn(pl, c0 + coff, sw, so)
                        if not isinstance(fns, list):
                            fns = [fns]
                        for f_ in fns:
                            M[f"dq_wr_{tag}"] = b.op("gpsimd", f_, sbs[so], 16)
                        hist.append((tag, ss, so))
                st["pkuse"][bi_] = f"nib_{mat}_{c0 + w - DW}_1"

            def gu_out(ch, rowbase):
                def f(pl, c0, w, so):
                    cr = c0 - ch * NCHC
                    return lambda e: e.dma_start(
                        out=blob_gu_t[ch].ap()[rowbase + 128 * pl: rowbase + 128 * pl + 128,
                                               cr:cr + w],
                        in_=dqo_ap(so, w))
                return f

            def dn_out(pl, c0, w, so):
                a0, na = c0 // D, w // D
                dviews = [blob_dn_t[j].ap().rearrange("(pl hc p) m -> pl hc p m", pl=2, hc=8)
                          for j in range(2)]
                return [
                    (lambda e, ai=ai, j=j: e.dma_start(
                        out=dviews[j][pl, :, :, (a0 + ai) * 128:(a0 + ai + 1) * 128]
                            .rearrange("hc p hh -> p hc hh"),
                        in_=dqo_ap(so, w).rearrange("p (a hc hh) -> p a hc hh", a=na, hc=16)[:, ai, 8 * j:8 * j + 8, :]))
                    for ai in range(na) for j in range(2)]

            def dec_gu(ch):
                for mi, (mat, pk_src, sc_src, rowb) in enumerate(
                        [("g", gpk, gsc, 0), ("u", upk, usc, 256)]):
                    c0 = ch * NCHC
                    dq_pk_load(mat, mi, pk_src, c0, NCHC)
                    dq_sc_load(mat, sc_src, c0, NCHC, first=(ch == 0 and mi == 0))
                    dq_chunk(mat, mi, c0, NCHC, gu_out(ch, rowb))
                M[f"gu_dec0_{ch}"] = s_bw.n
                M[f"gu_dec1_{ch}"] = s_b1.n
                M[f"gu_dec2_{ch}"] = s_b2.n
                b.wait("gpsimd", s_bw, f"gu_dec0_{ch}")
                b.wait("gpsimd", s_b1, f"gu_dec1_{ch}")
                b.wait("gpsimd", s_b2, f"gu_dec2_{ch}")
                M[f"cc_gu_{ch}"] = b.op("gpsimd", lambda e, ch=ch: e.collective_compute(
                    "AllGather", ALU.bypass, replica_groups=RG,
                    ins=[blob_gu_t[ch].ap()], outs=[WGU_t[ch].ap()]), s_cc)

            ND = (AN * D + NCH - 1) // NCH

            def dec_dn(ch):
                c0, w = ch * NCH, min(NCH, AN * D - ch * NCH)
                dq_pk_load("dn", ch % 2, dpk, c0, w)
                dq_sc_load("dn", dsc, c0, w)
                dq_chunk("dn", ch % 2, c0, w, dn_out)
                if ch == ND - 1:
                    M["dn_dec0"] = s_bw.n
                    M["dn_dec1"] = s_b1.n
                    M["dn_dec2"] = s_b2.n
                    b.wait("gpsimd", s_bw, "dn_dec0")
                    b.wait("gpsimd", s_b1, "dn_dec1")
                    b.wait("gpsimd", s_b2, "dn_dec2")
                    for j in range(2):
                        M[f"cc_dn_{j}"] = b.op("gpsimd", lambda e, j=j: e.collective_compute(
                            "AllGather", ALU.bypass, replica_groups=RG,
                            ins=[blob_dn_t[j].ap()], outs=[WDN_t[j].ap()]), s_cc)

            def dec_aw():
                for hl in range(2):
                    if hl == 0:
                        b.wait("gpsimd", s_dv, st["pkuse"][0])
                        b.wait("gpsimd", s_dv, st["pkuse"][1])
                        M["aw_pre0"] = s_bw.n
                        M["aw_pre1"] = s_b1.n
                        M["aw_pre2"] = s_b2.n
                        b.wait("gpsimd", s_bw, "aw_pre0")
                        b.wait("gpsimd", s_b1, "aw_pre1")
                        b.wait("gpsimd", s_b2, "aw_pre2")
                    M[f"aw_ld{hl}"] = b.op("gpsimd", lambda e, hl=hl: e.dma_start(
                        out=awb_ap(), in_=aw[128 * hl:128 * hl + 128, :]), s_aw, 16)
                    b.wait("gpsimd", s_aw, f"aw_ld{hl}")
                    M[f"aw_wr{hl}"] = b.op("gpsimd", lambda e, hl=hl: e.dma_start(
                        out=blob_at[128 * hl:128 * hl + 128, :], in_=awb_ap()), s_bw, 16)
                    b.wait("gpsimd", s_bw, f"aw_wr{hl}")
                M["cc_at"] = b.op("gpsimd", lambda e: e.collective_compute(
                    "AllGather", ALU.bypass, replica_groups=RG,
                    ins=[blob_at_t.ap()], outs=[WAT_t.ap()]), s_cc)

            def ffn0_hook(sp):
                if DEC_INLINE:
                    return
                ch = sp // 2
                if sp % 2 == 0 and ch < ND:
                    dec_dn(ch)
                elif sp == 2 * ND:
                    dec_aw()

            def emit_phase0():
                emit_R_loads(0, xb[0], [[], []])
                layer_norm(0, None, None)
                for ch in range(NCHK):
                    dec_gu(ch)
                if DEC_INLINE:
                    for ch in range(ND):
                        dec_dn(ch)
                    dec_aw()

            def phase_barrier(ph):
                if ph == 0:
                    return
                M[f"bar_dv_{ph}"] = s_dv.n
                M[f"bar_ac_{ph}"] = s_ac.n
                b.wait("tensor", s_dv, f"bar_dv_{ph}")
                b.wait("tensor", s_ac, f"bar_ac_{ph}")

            def emit_R_loads(ph, src, prev_keys):
                for idx in range(32):
                    j = idx % 16
                    par = idx % 2
                    if idx >= 2:
                        if idx - 2 < 16:
                            b.wait("sync", s_pe, f"sx2_{ph}_{idx - 2}")
                            b.wait("sync", s_ac, f"sq_{ph}_{idx - 2}")
                        else:
                            b.wait("sync", s_dv, f"nm1_{ph}_{idx - 18}")
                    else:
                        for (sem, key) in prev_keys[idx]:
                            b.wait("sync", sem, key)
                    M[f"r_ld_{ph}_{idx}"] = b.op("sync", lambda e, j=j, par=par, src=src: e.dma_start(
                        out=rbuf[par][:, :], in_=src[128 * j:128 * j + 128, :]), sR[par], 16)

            def emit_res_loads(ph, src, prev_keys):
                for hc in range(16):
                    par = hc % 2
                    if hc >= 2:
                        b.wait("sync", s_dv, f"resfree_{ph}_{hc - 2}")
                    else:
                        for (sem, key) in prev_keys[hc]:
                            b.wait("sync", sem, key)
                    M[f"res_ld_{ph}_{hc}"] = b.op("sync", lambda e, hc=hc, par=par, src=src: e.dma_start(
                        out=rbuf[par][:, :], in_=src[128 * hc:128 * hc + 128, :]), sR[par], 16)

            def layer_norm(ph, prev_xn_key, prev_lnt_keys):
                phase_barrier(ph)
                for j in range(16):
                    par = j % 2
                    b.wait("scalar", sR[par], f"r_ld_{ph}_{j}")
                    if j >= 1:
                        b.wait("scalar", s_pe, f"sx2_{ph}_{j - 1}")
                    M[f"sq_{ph}_{j}"] = b.op("scalar", lambda e, par=par: e.activation(
                        out=sqb[:, :], in_=rbuf[par][:, :], func=AF.Square), s_ac)
                    b.wait("tensor", sR[par], f"r_ld_{ph}_{j}")
                    M[f"sx_{ph}_{j}"] = b.op("tensor", lambda e, j=j, par=par: e.matmul(
                        pst1[0:1, 0:TB], ones_col[:, :], rbuf[par][:, :],
                        start=(j == 0), stop=(j == 15), skip_group_check=True), s_pe)
                    b.wait("tensor", s_ac, f"sq_{ph}_{j}")
                    M[f"sx2_{ph}_{j}"] = b.op("tensor", lambda e, j=j: e.matmul(
                        pst2[0:1, 0:TB], ones_col[:, :], sqb[:, :],
                        start=(j == 0), stop=(j == 15), skip_group_check=True), s_pe)
                st_ = stat
                b.wait("vector", s_pe, f"sx2_{ph}_15")
                M[f"st1_{ph}"] = b.op("vector", lambda e: e.tensor_scalar(
                    out=st_[:, 0:TB], in0=pst1[0:1, 0:TB], scalar1=1.0 / D,
                    scalar2=None, op0=ALU.mult), s_dv)
                M[f"st2_{ph}"] = b.op("vector", lambda e: e.tensor_scalar(
                    out=st_[:, TB:2 * TB], in0=pst2[0:1, 0:TB], scalar1=1.0 / D,
                    scalar2=None, op0=ALU.mult), s_dv)
                b.wait("vector", s_dv, f"st2_{ph}")
                M[f"st3_{ph}"] = b.op("vector", lambda e: e.tensor_mul(
                    out=st_[:, 2 * TB:3 * TB], in0=st_[:, 0:TB], in1=st_[:, 0:TB]), s_dv)
                b.wait("vector", s_dv, f"st3_{ph}")
                M[f"st4_{ph}"] = b.op("vector", lambda e: e.tensor_sub(
                    out=st_[:, TB:2 * TB], in0=st_[:, TB:2 * TB], in1=st_[:, 2 * TB:3 * TB]), s_dv)
                b.wait("scalar", s_dv, f"st4_{ph}")
                M[f"st5_{ph}"] = b.op("scalar", lambda e: e.activation(
                    out=st_[:, 2 * TB:3 * TB], in_=st_[:, TB:2 * TB], func=AF.Ln,
                    bias=LN_EPS), s_ac)
                b.wait("scalar", s_ac, f"st5_{ph}")
                M[f"st6_{ph}"] = b.op("scalar", lambda e: e.activation(
                    out=st_[:, 3 * TB:4 * TB], in_=st_[:, 2 * TB:3 * TB], func=AF.Exp,
                    scale=-0.5), s_ac)
                b.wait("vector", s_ac, f"st6_{ph}")
                M[f"st7_{ph}"] = b.op("vector", lambda e: e.tensor_mul(
                    out=st_[:, 0:TB], in0=st_[:, 0:TB], in1=st_[:, 3 * TB:4 * TB]), s_dv)
                b.wait("vector", s_dv, f"st7_{ph}")
                M[f"st8_{ph}"] = b.op("vector", lambda e: e.tensor_scalar(
                    out=st_[:, 0:TB], in0=st_[:, 0:TB], scalar1=-1.0,
                    scalar2=None, op0=ALU.mult), s_dv)
                b.wait("tensor", s_ac, f"st6_{ph}")
                b.wait("tensor", s_dv, f"st8_{ph}")
                M[f"rep1_{ph}"] = b.op("tensor", lambda e: e.matmul(
                    psA[:, 0:TB], ones_row[:, :], stat[0:1, 3 * TB:4 * TB],
                    start=True, stop=True, skip_group_check=True), s_pe)
                M[f"rep2_{ph}"] = b.op("tensor", lambda e: e.matmul(
                    psB[:, 0:TB], ones_row[:, :], stat[0:1, 0:TB],
                    start=True, stop=True, skip_group_check=True), s_pe)
                b.wait("vector", s_pe, f"rep2_{ph}")
                M[f"arep_{ph}"] = b.op("vector", lambda e: e.tensor_copy(
                    out=arep[:, :], in_=psA[:, 0:TB]), s_dv)
                b.wait("vector", s_dv, f"arep_{ph}")
                M[f"brepc_{ph}"] = b.op("vector", lambda e: e.tensor_copy(
                    out=brep[:, :], in_=psB[:, 0:TB]), s_dv)
                for j in range(16):
                    jj = 16 + j
                    par = jj % 2
                    b.wait("vector", sR[par], f"r_ld_{ph}_{jj}")
                    b.wait("vector", s_dv, f"brepc_{ph}")
                    if j < 2 and prev_lnt_keys is not None:
                        for (sem, key) in prev_lnt_keys[j]:
                            b.wait("vector", sem, key)
                    if j >= 2:
                        b.wait("vector", s_dv, f"nm2_{ph}_{j - 2}")
                    M[f"nm1_{ph}_{j}"] = b.op("vector", lambda e, j=j, par=par: e.tensor_mul(
                        out=lnt[j % 2][:, :], in0=rbuf[par][:, :], in1=arep[:, :]), s_dv)
                    b.wait("vector", s_dv, f"nm1_{ph}_{j}")
                    if j == 0 and prev_xn_key is not None:
                        b.wait("vector", s_pe, prev_xn_key)
                    M[f"nm2_{ph}_{j}"] = b.op("vector", lambda e, j=j: e.tensor_add(
                        out=x_n[:, j * TB:(j + 1) * TB], in0=lnt[j % 2][:, :],
                        in1=brep[:, :]), s_dv)
                M[f"xn_{ph}"] = s_dv.n

            def ffn(ph, src, dst, prev_gu, prev_dn, prev_xn, prev_r, prev_wr, hook=None,
                    pre_ln=False):
                if not pre_ln:
                    emit_R_loads(ph, src, prev_r)
                    layer_norm(ph, prev_xn, None)
                spc = NCHC // 256
                for sp in range(NSP):
                    if hook is not None:
                        hook(sp)
                    half = sp % 2
                    kch = sp // spc
                    if sp >= 2:
                        b.wait("sync", s_pe, f"gu_fin_{ph}_{sp - 2}")
                    else:
                        for (sem, key) in prev_gu[sp]:
                            b.wait("sync", sem, key)
                    if ph == 0 and (sp % spc) < 2:
                        b.wait("sync", s_cc, f"cc_gu_{kch}")
                    for ph_ in range(2):
                        M[f"gu_ld_{ph}_{sp}"] = b.op("sync", lambda e, sp=sp, half=half, kch=kch, ph_=ph_: e.dma_start(
                            out=gu_view(half)[:, 4 * ph_:4 * ph_ + 4, :, :],
                            in_=WGU_v[kch][:, 4 * ph_:4 * ph_ + 4, :,
                                           256 * sp - kch * NCHC: 256 * sp - kch * NCHC + 256]), sS[half], 16)
                    b.wait("tensor", sS[half], f"gu_ld_{ph}_{sp}")
                    b.wait("tensor", s_dv, f"xn_{ph}")
                    for par2 in range(2):
                        g = 2 * sp + par2
                        pg, pu = (psA, psB) if par2 == 0 else (psE, psF)
                        if g >= 2:
                            b.wait("tensor", s_ac, f"gel_{ph}_{g - 2}")
                            b.wait("tensor", s_dv, f"inter_{ph}_{g - 2}")
                        else:
                            b.wait("tensor", s_dv, f"arep_{ph}")
                            b.wait("tensor", s_dv, f"brepc_{ph}")
                        for kc in range(16):
                            c0, pl = kc // 2, kc % 2
                            r = b.op("tensor", lambda e, half=half, c0=c0, pl=pl, par2=par2, kc=kc, pg=pg: e.matmul(
                                pg[:, 0:TB], gu_view(half)[:, c0, pl, par2:256:2],
                                x_n[:, (2 * c0 + pl) * TB:(2 * c0 + pl + 1) * TB],
                                start=(kc == 0), stop=(kc == 15), skip_group_check=True),
                                s_pe if kc == 15 else None)
                        M[f"gmm_{ph}_{g}"] = r
                        for kc in range(16):
                            c0, pl = kc // 2, kc % 2
                            r = b.op("tensor", lambda e, half=half, c0=c0, pl=pl, par2=par2, kc=kc, pu=pu: e.matmul(
                                pu[:, 0:TB], gu_view(half)[:, c0, 2 + pl, par2:256:2],
                                x_n[:, (2 * c0 + pl) * TB:(2 * c0 + pl + 1) * TB],
                                start=(kc == 0), stop=(kc == 15), skip_group_check=True),
                                s_pe if kc == 15 else None)
                        M[f"umm_{ph}_{g}"] = r
                        b.wait("scalar", s_pe, f"gmm_{ph}_{g}")
                        if g >= 2:
                            b.wait("scalar", s_dv, f"inter_{ph}_{g - 2}")
                        M[f"gel_{ph}_{g}"] = b.op("scalar", lambda e, g=g, pg=pg: e.activation(
                            out=gelu_b[g % 2][:, :], in_=pg[:, 0:TB],
                            func=AF.Gelu_apprx_tanh), s_ac)
                        b.wait("vector", s_ac, f"gel_{ph}_{g}")
                        b.wait("vector", s_pe, f"umm_{ph}_{g}")
                        M[f"inter_{ph}_{g}"] = b.op("vector", lambda e, g=g, pu=pu: e.tensor_tensor(
                            out=inter_ap(g), in0=pu[:, 0:TB], in1=gelu_b[g % 2][:, :],
                            op=ALU.mult), s_dv)
                    M[f"gu_fin_{ph}_{sp}"] = M[f"umm_{ph}_{2 * sp + 1}"]
                for hc in range(16):
                    par = hc % 2
                    if hc >= 2:
                        b.wait("sync", s_dv, f"resfree_{ph}_{hc - 2}")
                    else:
                        b.wait("sync", s_dv, f"nm1_{ph}_{14 + hc}")
                    M[f"res_ld_{ph}_{hc}"] = b.op("sync", lambda e, hc=hc, par=par, src=src: e.dma_start(
                        out=rbuf[par][:, :], in_=src[128 * hc:128 * hc + 128, :]), sR[par], 16)
                    pdn = psA if hc % 2 == 0 else psE
                    for dsp in range(8):
                        idx = hc * 8 + dsp
                        half = idx % 2
                        if idx >= 2:
                            b.wait("sync", s_pe, f"dn_fin_{ph}_{idx - 2}")
                        else:
                            for (sem, key) in prev_dn[idx]:
                                b.wait("sync", sem, key)
                        if ph == 0 and hc == 8 and dsp < 2:
                            b.wait("sync", s_cc, "cc_dn_1")
                        M[f"dn_ld_{ph}_{idx}"] = b.op("sync", lambda e, hc=hc, dsp=dsp, half=half: e.dma_start(
                            out=dn_view(half), in_=WDN_v[hc // 8][:, dsp, :, hc % 8, :]), sS[half], 16)
                    if hc == 0:
                        b.wait("tensor", s_dv, f"inter_{ph}_{NG - 1}")
                        b.wait("tensor", s_dv, f"inter_{ph}_{NG - 2}")
                        b.wait("tensor", s_ac, f"gel_{ph}_{NG - 1}")
                        b.wait("tensor", s_ac, f"gel_{ph}_{NG - 2}")
                    if hc >= 2:
                        b.wait("tensor", s_dv, f"resadd_{ph}_{hc - 2}")
                    first = True
                    for dsp in range(8):
                        idx = hc * 8 + dsp
                        half = idx % 2
                        b.wait("tensor", sS[half], f"dn_ld_{ph}_{idx}")
                        for pl in range(2):
                            for a in range(AN):
                                g = 2 * (dsp * AN + a) + pl
                                stop_ = (dsp == 7 and pl == 1 and a == AN - 1)
                                mark = (pl == 1 and a == AN - 1)
                                r = b.op("tensor", lambda e, half=half, pl=pl, a=a, g=g, pdn=pdn, first=first, stop_=stop_: e.matmul(
                                    pdn[:, 0:TB], dn_view(half)[:, pl, a * 128:(a + 1) * 128],
                                    inter_ap(g), start=first, stop=stop_,
                                    skip_group_check=True), s_pe if mark else None)
                                first = False
                        M[f"dn_fin_{ph}_{idx}"] = r
                    M[f"dn_mm_{ph}_{hc}"] = r
                    par = hc % 2
                    b.wait("vector", s_pe, f"dn_mm_{ph}_{hc}")
                    b.wait("vector", sR[par], f"res_ld_{ph}_{hc}")
                    if hc >= 2:
                        b.wait("vector", sW[par], f"rwr_{ph}_{hc - 2}")
                    else:
                        for (sem, key) in prev_wr[hc]:
                            b.wait("vector", sem, key)
                    M[f"resadd_{ph}_{hc}"] = b.op("vector", lambda e, par=par, pdn=pdn: e.tensor_add(
                        out=outb[par][:, :], in0=pdn[:, 0:TB], in1=rbuf[par][:, :]), s_dv)
                    M[f"resfree_{ph}_{hc}"] = M[f"resadd_{ph}_{hc}"]
                    b.wait("gpsimd", s_dv, f"resadd_{ph}_{hc}")
                    M[f"rwr_{ph}_{hc}"] = b.op("gpsimd", lambda e, hc=hc, par=par, dst=dst: e.dma_start(
                        out=dst[128 * hc:128 * hc + 128, :], in_=outb[par][:, :]), sW[par], 16)
                M[f"ffn_wr_tot0_{ph}"] = s_w0.n
                M[f"ffn_wr_tot1_{ph}"] = s_w1.n

            R1 = R1_t.ap()
            R2 = R2_t.ap()

            emit_phase0()
            ffn(0, xb[0], R1[0],
                prev_gu=[[], []],
                prev_dn=[[(s_cc, "cc_dn_0"), (s_pe, "gu_fin_0_%d" % (NSP - 2))],
                         [(s_cc, "cc_dn_0"), (s_pe, "gu_fin_0_%d" % (NSP - 1))]],
                prev_xn=None,
                prev_r=[[], []],
                prev_wr=[[], []],
                hook=ffn0_hook,
                pre_ln=True)
            ffn(1, xb[1], R1[1],
                prev_gu=[[(s_pe, "dn_fin_0_126")], [(s_pe, "dn_fin_0_127")]],
                prev_dn=[[(s_pe, "gu_fin_1_%d" % (NSP - 2))],
                         [(s_pe, "gu_fin_1_%d" % (NSP - 1))]],
                prev_xn="gu_fin_0_%d" % (NSP - 1),
                prev_r=[[(s_dv, "resadd_0_14")], [(s_dv, "resadd_0_15")]],
                prev_wr=[[(s_w0, "rwr_0_14")], [(s_w1, "rwr_0_15")]])

            for bi in range(2):
                ph = 10 + bi
                b.wait("sync", sW[0], f"ffn_wr_tot0_{bi}")
                b.wait("sync", sW[1], f"ffn_wr_tot1_{bi}")
                if bi == 0:
                    b.wait("sync", s_dv, "resadd_1_15")
                    b.wait("sync", s_pe, "dn_mm_1_15")
                    M["bvr_ld"] = b.op("sync", lambda e: e.dma_start(
                        out=bv_row_ap(), in_=bvr), s_in, 16)
                    b.wait("tensor", s_in, "bvr_ld")
                    for i in range(D // 512):
                        pg2 = psA if i % 2 == 0 else psB
                        if i >= 2:
                            b.wait("tensor", s_dv, f"bvcp{i-2}")
                        M[f"bvrep{i}"] = b.op("tensor", lambda e, i=i, pg2=pg2: e.matmul(
                            pg2[:, :], ones_row[:, :], bv_row_ap()[:, 512 * i:512 * i + 512],
                            start=True, stop=True, skip_group_check=True), s_pe)
                        b.wait("vector", s_pe, f"bvrep{i}")
                        M[f"bvcp{i}"] = b.op("vector", lambda e, i=i, pg2=pg2: e.tensor_copy(
                            out=bv_rep_ap()[:, 512 * i:512 * i + 512], in_=pg2[:, :]), s_dv)
                    M["bv_done"] = s_dv.n
                emit_R_loads(ph, R1[bi],
                             [[(s_dv, "resadd_1_14")], [(s_dv, "resadd_1_15")]] if bi == 0
                             else [[(s_dv, "nm1_10_14")], [(s_dv, "nm1_10_15")]])
                layer_norm(ph, prev_xn_key=("gu_fin_1_%d" % (NSP - 1)) if bi == 0 else "aw_fin_0_11",
                           prev_lnt_keys=None)
                for eb in range(12):
                    half = eb % 2
                    if eb >= 2:
                        b.wait("sync", s_pe, f"aw_fin_{bi}_{eb - 2}")
                    elif bi == 0:
                        b.wait("sync", s_cc, "cc_at")
                        b.wait("sync", s_pe, f"dn_fin_1_{126 + eb}")
                    else:
                        b.wait("sync", s_pe, f"aw_fin_0_{10 + eb}")
                    M[f"aw_ld_{bi}_{eb}"] = b.op("sync", lambda e, eb=eb, half=half: e.dma_start(
                        out=at_view(half), in_=WATkc[:, :, 512 * eb:512 * eb + 512]), sS[half], 16)
                    b.wait("tensor", sS[half], f"aw_ld_{bi}_{eb}")
                    b.wait("tensor", s_dv, f"xn_{ph}")
                    if eb < 8:
                        for sub in range(4):
                            ec = eb * 4 + sub
                            pg = psA if ec % 2 == 0 else psB
                            if ec >= 2:
                                b.wait("tensor", s_dv, f"qkev_{bi}_{ec - 2}")
                            else:
                                b.wait("tensor", s_dv, f"arep_{ph}")
                                b.wait("tensor", s_dv, f"brepc_{ph}")
                            for kc in range(16):
                                r = b.op("tensor", lambda e, half=half, kc=kc, sub=sub, pg=pg: e.matmul(
                                    pg[:, 0:TB], at_view(half)[:, kc, sub * 128:(sub + 1) * 128],
                                    x_n[:, kc * TB:(kc + 1) * TB],
                                    start=(kc == 0), stop=(kc == 15), skip_group_check=True),
                                    s_pe if kc == 15 else None)
                            M[f"qkmm_{bi}_{ec}"] = r
                            b.wait("vector", s_pe, f"qkmm_{bi}_{ec}")
                            if ec >= 4:
                                b.wait("vector", sQ4[ec % 4], f"qkwr_{bi}_{ec - 4}")
                            elif bi == 1:
                                b.wait("vector", sQ4[ec % 4], f"vwr_0_{4 * NTQ - 4 + ec}")
                            isq = ec < 16
                            M[f"qkev_{bi}_{ec}"] = b.op("vector", lambda e, ec=ec, pg=pg, isq=isq: e.tensor_scalar(
                                out=evb[ec % 4][:, 0:TB], in0=pg[:, 0:TB],
                                scalar1=QK_SCALE if isq else 1.0,
                                scalar2=bqk_sb[:, ec:ec + 1],
                                op0=ALU.mult, op1=ALU.add), s_dv)
                            row = (0 if isq else D) + (ec % 16) * 128
                            b.wait("sync", s_dv, f"qkev_{bi}_{ec}")
                            M[f"qkwr_{bi}_{ec}"] = b.op("sync", lambda e, bi=bi, row=row, ec=ec: e.dma_start(
                                out=QKIN[bi].ap()[row:row + 128, :], in_=evb[ec % 4][:, 0:TB]), sQ4[ec % 4], 16)
                        M[f"aw_fin_{bi}_{eb}"] = M[f"qkmm_{bi}_{eb * 4 + 3}"]
                    else:
                        evs = eb - 8
                        for ts in range(NTQ):
                            vi = evs * NTQ + ts
                            pg = psA if vi % 2 == 0 else psB
                            if vi >= 2:
                                b.wait("tensor", s_dv, f"vev_{bi}_{vi - 2}")
                            elif evs == 0:
                                b.wait("tensor", s_dv, f"qkev_{bi}_30")
                                b.wait("tensor", s_dv, f"qkev_{bi}_31")
                            for kc in range(16):
                                r = b.op("tensor", lambda e, half=half, kc=kc, ts=ts, pg=pg: e.matmul(
                                    pg[0:TQ, 0:512],
                                    x_n[:, kc * TB + ts * TQ: kc * TB + (ts + 1) * TQ],
                                    at_view(half)[:, kc, :],
                                    start=(kc == 0), stop=(kc == 15), skip_group_check=True),
                                    s_pe if kc == 15 else None)
                            M[f"vmm_{bi}_{vi}"] = r
                            b.wait("vector", s_pe, f"vmm_{bi}_{vi}")
                            if vi >= 4:
                                b.wait("vector", sQ4[vi % 4], f"vwr_{bi}_{vi - 4}")
                            else:
                                b.wait("vector", sQ4[vi % 4], f"qkwr_{bi}_{28 + vi}")
                            M[f"vev_{bi}_{vi}"] = b.op("vector", lambda e, vi=vi, evs=evs, pg=pg: e.tensor_tensor(
                                out=evb[vi % 4][0:TQ, 0:512], in0=pg[0:TQ, 0:512],
                                in1=bv_rep_ap()[0:TQ, 512 * evs:512 * evs + 512], op=ALU.add), s_dv)
                            b.wait("sync", s_dv, f"vev_{bi}_{vi}")
                            M[f"vwr_{bi}_{vi}"] = b.op("sync", lambda e, bi=bi, vi=vi, evs=evs, ts=ts: e.dma_start(
                                out=VIN[bi].ap().rearrange("t (h s) -> t h s", s=VS)[
                                    ts * TQ:(ts + 1) * TQ, 4 * evs:4 * evs + 4, 0:HD],
                                in_=evb[vi % 4][0:TQ, 0:512].rearrange("t (h s) -> t h s", s=HD)), sQ4[vi % 4], 16)
                        M[f"aw_fin_{bi}_{eb}"] = M[f"vmm_{bi}_{(eb - 8 + 1) * NTQ - 1}"]
                if bi == 1:
                    b.wait("sync", s_cc, "cc_v0")
                for ts in range(NTQ):
                    M[f"vones_{bi}"] = b.op("sync", lambda e, bi=bi, ts=ts: e.dma_start(
                        out=VIN[bi].ap().rearrange("(s p) (h v) -> p s h v", p=TQ, v=VS)[:, ts, :, HD:VS],
                        in_=onesv[0:TQ, 0:NH].unsqueeze(2),
                        allow_slow_non_contiguous=True), s_vo, 16)
                for qi in range(4):
                    M[f"q{qi}_tot_{bi}"] = sQ4[qi].n
                    b.wait("gpsimd", sQ4[qi], f"q{qi}_tot_{bi}")
                b.wait("gpsimd", s_vo, f"vones_{bi}")
                M[f"cc_qk{bi}"] = b.op("gpsimd", lambda e, bi=bi: e.collective_compute(
                    "AllGather", ALU.bypass, replica_groups=RG,
                    ins=[QKIN[bi].ap()], outs=[QKOUT[bi].ap()]), s_cc)
                M[f"cc_v{bi}"] = b.op("gpsimd", lambda e, bi=bi: e.collective_compute(
                    "AllGather", ALU.bypass, replica_groups=RG,
                    ins=[VIN[bi].ap()], outs=[VOUT[bi].ap()]), s_cc)

            def kv_rank(gb):
                return gb if gb < 8 else 15 - gb

            def slot(gb):
                return (0 if gb < 8 else 8) + kv_rank(gb)

            def kj_off(j, gb):
                return off_k + (j * NBLK + slot(gb)) * TB

            def vj_off(j, gb):
                return off_v + (j * NBLK + slot(gb)) * NTQ * VS

            def qj_off(j, gb):
                return off_q + (j * NBLK + slot(gb)) * TB

            for hf in range(2):
                skv_h = s_kv if hf == 0 else s_k1
                b.wait("sync", s_cc, "cc_v0" if hf == 0 else "cc_v1")
                for j in range(2):
                    b.op("sync", lambda e, hf=hf, j=j: e.dma_start(
                        out=AR(off_k + (j * NBLK + hf * 8) * TB, 8 * TB)
                            .rearrange("p (r t) -> p r t", r=8),
                        in_=bass.AP(QKOUT[hf],
                                    b.dynoff(e, 256 * TB, D * TB + (128 * TB) * j),
                                    [[TB, HD], [2 * D * TB, 8], [1, TB]])), skv_h, 16)
                    b.op("sync", lambda e, hf=hf, j=j: e.dma_start(
                        out=AR(off_v + (j * NBLK + hf * 8) * NTQ * VS, 8 * NTQ * VS)
                            .rearrange("p (rs v) -> p rs v", v=VS)[0:TQ],
                        in_=bass.AP(VOUT[hf],
                                    b.dynoff(e, 2 * VS, VS * j),
                                    [[NH * VS, TQ], [TQ * NH * VS, 8 * NTQ], [1, VS]])), skv_h, 16)
                for j in range(2):
                    b.op("sync", lambda e, hf=hf, j=j: e.dma_start(
                        out=AR(off_q + (j * NBLK + hf * 8) * TB, 8 * TB)
                            .rearrange("p (r t) -> p r t", r=8)[0:HD],
                        in_=bass.AP(QKOUT[hf],
                                    b.dynoff(e, 256 * TB, (128 * TB) * j),
                                    [[TB, HD], [2 * D * TB, 8], [1, TB]])), skv_h, 16)
                if hf == 0:
                    M["kv_half0"] = s_kv.n
            M["kv_all"] = s_k1.n

            phase_barrier(99)
            b.wait("gpsimd", s_dv, "resadd_1_15")
            b.wait("gpsimd", s_pe, "dn_mm_1_15")
            M["mz"] = b.op("gpsimd", lambda e: e.memset(maskT_ap(), 0.0), s_gp)
            b.wait("gpsimd", s_gp, "mz")
            for tkc in range(NTQ):
                M["mbuild"] = b.op("gpsimd", lambda e, tkc=tkc: e.affine_select(
                    out=maskT_ap()[:, tkc * TB:(tkc + 1) * TB],
                    in_=maskT_ap()[:, tkc * TB:(tkc + 1) * TB],
                    compare_op=ALU.is_ge, fill=MASKVAL, base=-tkc * TQ,
                    pattern=[[1, TB]], channel_multiplier=-1), s_gp)
            b.wait("tensor", s_kv, "kv_half0")
            for gb in range(NBLK):
                if gb == 8:
                    b.wait("tensor", s_k1, "kv_all")
                for j in range(2):
                    nch_ = (gb + 1) * NTQ
                    ci = 0
                    if gb + j > 0:
                        pj, pgb = (0, gb) if j == 1 else (1, gb - 1)
                        for tqs in range(NTQ):
                            b.wait("tensor", s_dv, f"aofin_{pgb}_{pj}_{tqs}")
                    for kv in range(gb + 1):
                        for tkc in range(NTQ):
                            pss = psA if ci % 2 == 0 else psB
                            half = ci % 2
                            if ci >= 2:
                                b.wait("tensor", s_ac, f"exp_{gb}_{j}_{ci - 2}")
                            elif gb + j > 0:
                                pj2, pgb2 = (0, gb) if j == 1 else (1, gb - 1)
                                pn2 = (pgb2 + 1) * NTQ
                                li = pn2 - 1 - ((pn2 - 1 - ci) % 2)
                                if li >= 0:
                                    b.wait("tensor", s_ac, f"exp_{pgb2}_{pj2}_{li}")
                            M[f"smm_{gb}_{j}_{ci}"] = b.op("tensor", lambda e, j=j, kv=kv, tkc=tkc, gb=gb, pss=pss: e.matmul(
                                pss[0:TQ, 0:TB],
                                arena[0:HD, kj_off(j, kv) + tkc * TQ: kj_off(j, kv) + (tkc + 1) * TQ],
                                AR(qj_off(j, gb), TB)[0:HD],
                                start=True, stop=True, skip_group_check=True), s_pe)
                            if ci >= 2:
                                b.wait("scalar", s_pe, f"pv_{gb}_{j}_{ci - 2}")
                            elif gb + j > 0:
                                pj2, pgb2 = (0, gb) if j == 1 else (1, gb - 1)
                                b.wait("scalar", s_pe, f"pvmm_{pgb2}_{pj2}")
                            if kv == gb:
                                b.wait("vector", s_pe, f"smm_{gb}_{j}_{ci}")
                                b.wait("vector", s_gp, "mbuild")
                                if tkc > 0:
                                    b.wait("vector", s_ac, f"exp_{gb}_{j}_{ci - 1}")
                                elif gb + j > 0:
                                    pj2, pgb2 = (0, gb) if j == 1 else (1, gb - 1)
                                    pn2 = (pgb2 + 1) * NTQ
                                    b.wait("vector", s_ac, f"exp_{pgb2}_{pj2}_{pn2 - 1}")
                                M[f"msk_{gb}_{j}_{ci}"] = b.op("vector", lambda e, tkc=tkc, pss=pss: e.tensor_add(
                                    out=smask_ap()[0:TQ, :], in0=pss[0:TQ, 0:TB],
                                    in1=maskT_ap()[0:TQ, tkc * TB:(tkc + 1) * TB]), s_dv)
                                b.wait("scalar", s_dv, f"msk_{gb}_{j}_{ci}")
                                M[f"exp_{gb}_{j}_{ci}"] = b.op("scalar", lambda e, half=half: e.activation(
                                    out=AR(off_p + half * TB, TB)[0:TQ, :],
                                    in_=smask_ap()[0:TQ, 0:TB], func=AF.Exp, bias=EXP_BIAS), s_ac)
                            else:
                                b.wait("scalar", s_pe, f"smm_{gb}_{j}_{ci}")
                                M[f"exp_{gb}_{j}_{ci}"] = b.op("scalar", lambda e, half=half, pss=pss: e.activation(
                                    out=AR(off_p + half * TB, TB)[0:TQ, :],
                                    in_=pss[0:TQ, 0:TB], func=AF.Exp, bias=EXP_BIAS), s_ac)
                            b.wait("tensor", s_ac, f"exp_{gb}_{j}_{ci}")
                            for tqs in range(NTQ):
                                r = b.op("tensor", lambda e, half=half, tqs=tqs, j=j, kv=kv, tkc=tkc, ci=ci, nch_=nch_: e.matmul(
                                    pqo[tqs][0:TQ, 0:VS],
                                    arena[0:TQ, off_p + half * TB + tqs * TQ: off_p + half * TB + (tqs + 1) * TQ],
                                    AR(vj_off(j, kv), NTQ * VS).rearrange("p (s v) -> p s v", v=VS)[0:TQ, tkc, :],
                                    start=(ci == 0), stop=(ci == nch_ - 1),
                                    skip_group_check=True),
                                    s_pe if tqs == NTQ - 1 else None)
                            M[f"pv_{gb}_{j}_{ci}"] = r
                            ci += 1
                    M[f"pvmm_{gb}_{j}"] = r
                    for tqs in range(NTQ):
                        b.wait("vector", s_pe, f"pvmm_{gb}_{j}")
                        if gb + j > 0:
                            pj, pgb = (0, gb) if j == 1 else (1, gb - 1)
                            b.wait("vector", s_ao, f"ao_grp_{pgb}_{pj}")
                        M[f"lc_{gb}_{j}_{tqs}"] = b.op("vector", lambda e, tqs=tqs: e.tensor_copy(
                            out=lrec[0:TQ, 0:1], in_=pqo[tqs][0:TQ, HD:HD + 1]), s_dv)
                        b.wait("vector", s_dv, f"lc_{gb}_{j}_{tqs}")
                        M[f"lr_{gb}_{j}_{tqs}"] = b.op("vector", lambda e: e.reciprocal(
                            out=lrec[0:TQ, 1:2], in_=lrec[0:TQ, 0:1]), s_dv)
                        b.wait("vector", s_dv, f"lr_{gb}_{j}_{tqs}")
                        M[f"aofin_{gb}_{j}_{tqs}"] = b.op("vector", lambda e, tqs=tqs, j=j: e.tensor_scalar(
                            out=AR(off_ao + j * NTQ * HD + tqs * HD, HD)[0:TQ, :],
                            in0=pqo[tqs][0:TQ, 0:HD], scalar1=lrec[0:TQ, 1:2],
                            scalar2=None, op0=ALU.mult), s_dv)
                        sl = tqs % 2
                        ptt = pst1 if sl == 0 else pst2
                        b.wait("tensor", s_dv, f"aofin_{gb}_{j}_{tqs}")
                        if gb + j + tqs > 0:
                            b.wait("tensor", s_dv, "aotc_prev")
                        M["aotr_cur"] = b.op("tensor", lambda e, tqs=tqs, j=j, ptt=ptt: e.transpose(
                            ptt[:, 0:128].bitcast(BF16)[:, 0:TQ],
                            AR(off_ao + j * NTQ * HD + tqs * HD, HD)[0:TQ, :],
                            idbf[0:TQ, 0:TQ]), s_pe)
                        b.wait("vector", s_pe, "aotr_cur")
                        if gb + j + tqs > 0:
                            b.wait("vector", s_ao, "aowr_prev")
                        M["aotc_prev"] = b.op("vector", lambda e, sl=sl, ptt=ptt: e.tensor_copy(
                            out=AR(off_aot + sl * TQ, TQ),
                            in_=ptt[:, 0:128].bitcast(BF16)[:, 0:TQ]), s_dv)
                        b.wait("sync", s_dv, "aotc_prev")
                        b.op("sync", lambda e, gb=gb, j=j, tqs=tqs, sl=sl: e.dma_start(
                            out=AIN[0 if gb < 8 else 1].ap()[
                                128 * j:128 * j + 128,
                                (gb % 8) * TB + tqs * TQ:(gb % 8) * TB + (tqs + 1) * TQ],
                            in_=AR(off_aot + sl * TQ, TQ)), s_ao, 16)
                        M["aowr_prev"] = s_ao.n
                    M[f"ao_grp_{gb}_{j}"] = s_ao.n
                M[f"fl_fin_{gb}"] = M[f"pvmm_{gb}_1"]
                if gb == 7:
                    M["ao_tot_0"] = s_ao.n
                    b.wait("gpsimd", s_ao, "ao_tot_0")
                    M["cc_ao0"] = b.op("gpsimd", lambda e: e.collective_compute(
                        "AllGather", ALU.bypass, replica_groups=RG,
                        ins=[AIN[0].ap()], outs=[AOUT[0].ap()]), s_cc)
            M["ao_tot_1"] = s_ao.n
            b.wait("gpsimd", s_ao, "ao_tot_1")
            M["cc_ao1"] = b.op("gpsimd", lambda e: e.collective_compute(
                "AllGather", ALU.bypass, replica_groups=RG,
                ins=[AIN[1].ap()], outs=[AOUT[1].ap()]), s_cc)

            for bi in range(2):
                ph = 20 + bi
                phase_barrier(ph)
                b.wait("sync", s_cc, "cc_ao0" if bi == 0 else "cc_ao1")
                if bi == 1:
                    b.wait("sync", s_pe, f"op_all_0")
                coeff = TB if bi == 0 else -TB
                cnst = 0 if bi == 0 else 7 * TB
                for he in range(2):
                    b.op("sync", lambda e, bi=bi, he=he, coeff=coeff, cnst=cnst: e.dma_start(
                        out=arena[:, off_att:off_att + 16 * TB]
                            .rearrange("p (ec t) -> p ec t", t=TB)[:, he::2, :],
                        in_=bass.AP(AOUT[bi], b.dynoff(e, coeff, cnst + 128 * he),
                                    [[8 * TB, 128], [256 * 8 * TB, 8], [1, TB]])), s_q0, 16)
                M[f"atld_{bi}"] = s_q0.n
                for wb in range(4):
                    half = wb % 2
                    if wb >= 2:
                        b.wait("sync", s_pe, f"op_fin_{bi}_{wb - 2}")
                    elif bi == 0:
                        b.wait("sync", s_pe, "aw_fin_1_10" if wb == 0 else "aw_fin_1_11")
                    else:
                        b.wait("sync", s_pe, f"op_fin_0_{2 + wb}")
                    M[f"ow_ld_{bi}_{wb}"] = b.op("sync", lambda e, wb=wb, half=half: e.dma_start(
                        out=at_view(half), in_=WATkc[:, :, 512 * (12 + wb):512 * (12 + wb) + 512]), sS[half], 16)
                    b.wait("tensor", sS[half], f"ow_ld_{bi}_{wb}")
                    b.wait("tensor", s_q0, f"atld_{bi}")
                    for sub in range(4):
                        dc = wb * 4 + sub
                        par = dc % 2
                        pg = psE if dc % 2 == 0 else psF
                        if dc >= 2:
                            b.wait("sync", s_dv, f"resfree_{ph}_{dc - 2}")
                        elif bi == 0:
                            b.wait("sync", s_dv, "nm1_11_%d" % (14 + dc))
                        else:
                            b.wait("sync", s_dv, "resfree_20_%d" % (14 + dc))
                        M[f"res_ld_{ph}_{dc}"] = b.op("sync", lambda e, dc=dc, par=par, bi=bi: e.dma_start(
                            out=rbuf[par][:, :], in_=R1[bi][128 * dc:128 * dc + 128, :]), sR[par], 16)
                        if dc >= 2:
                            b.wait("tensor", s_dv, f"opadd_{bi}_{dc - 2}")
                        for ec in range(16):
                            r = b.op("tensor", lambda e, half=half, ec=ec, sub=sub, pg=pg: e.matmul(
                                pg[:, 0:TB], at_view(half)[:, ec, sub * 128:(sub + 1) * 128],
                                arena[:, off_att + ec * TB: off_att + (ec + 1) * TB],
                                start=(ec == 0), stop=(ec == 15), skip_group_check=True),
                                s_pe if ec == 15 else None)
                        M[f"opmm_{bi}_{dc}"] = r
                        b.wait("vector", s_pe, f"opmm_{bi}_{dc}")
                        b.wait("vector", sR[par], f"res_ld_{ph}_{dc}")
                        if dc >= 2:
                            b.wait("vector", sW[par], f"r2wr_{bi}_{dc - 2}")
                        elif bi == 0:
                            b.wait("vector", sW[par], f"rwr_1_{14 + dc}")
                        else:
                            b.wait("vector", sW[par], f"r2wr_0_{14 + dc}")
                        M[f"opadd_{bi}_{dc}"] = b.op("vector", lambda e, par=par, dc=dc, pg=pg: e.scalar_tensor_tensor(
                            out=outb[par][:, :], in0=pg[:, 0:TB],
                            scalar=bo_sb[:, dc:dc + 1], in1=rbuf[par][:, :],
                            op0=ALU.add, op1=ALU.add), s_dv)
                        M[f"resfree_{ph}_{dc}"] = M[f"opadd_{bi}_{dc}"]
                        b.wait("gpsimd", s_dv, f"opadd_{bi}_{dc}")
                        M[f"r2wr_{bi}_{dc}"] = b.op("gpsimd", lambda e, bi=bi, dc=dc, par=par: e.dma_start(
                            out=R2[bi][128 * dc:128 * dc + 128, :], in_=outb[par][:, :]), sW[par], 16)
                    M[f"op_fin_{bi}_{wb}"] = M[f"opmm_{bi}_{4 * wb + 3}"]
                M[f"op_all_{bi}"] = M[f"opmm_{bi}_15"]
                M[f"op_wr_tot0_{bi}"] = s_w0.n
                M[f"op_wr_tot1_{bi}"] = s_w1.n

            ffn(2, R2[0], ob[0],
                prev_gu=[[(s_pe, "op_fin_1_2")], [(s_pe, "op_fin_1_3")]],
                prev_dn=[[(s_pe, "gu_fin_2_%d" % (NSP - 2))],
                         [(s_pe, "gu_fin_2_%d" % (NSP - 1))]],
                prev_xn="op_fin_1_3",
                prev_r=[[(s_dv, "resfree_21_14"), (s_w0, "op_wr_tot0_0"), (s_w1, "op_wr_tot1_0")],
                        [(s_dv, "resfree_21_15")]],
                prev_wr=[[(s_w0, "r2wr_1_14")], [(s_w1, "r2wr_1_15")]])
            ffn(3, R2[1], ob[1],
                prev_gu=[[(s_pe, "dn_fin_2_126")], [(s_pe, "dn_fin_2_127")]],
                prev_dn=[[(s_pe, "gu_fin_3_%d" % (NSP - 2))],
                         [(s_pe, "gu_fin_3_%d" % (NSP - 1))]],
                prev_xn="gu_fin_2_%d" % (NSP - 1),
                prev_r=[[(s_dv, "resadd_2_14"), (s_w0, "op_wr_tot0_1"), (s_w1, "op_wr_tot1_1")],
                        [(s_dv, "resadd_2_15")]],
                prev_wr=[[(s_w0, "rwr_2_14")], [(s_w1, "rwr_2_15")]])

            b.wait("sync", s_w0, "ffn_wr_tot0_3")
            b.wait("sync", s_w1, "ffn_wr_tot1_3")

        b.begin(None, None)
        program()
        dry_counts = [s.n for s in sems]
        dry_M = dict(b.M)
        with nc.Block() as blk:
            for name, deco in (("sync", blk.sync), ("gpsimd", blk.gpsimd),
                               ("vector", blk.vector), ("scalar", blk.scalar),
                               ("tensor", blk.tensor)):
                def mk(name):
                    def f(eng):
                        b.begin(name, eng)
                        program()
                        assert [s.n for s in sems] == dry_counts, name
                        assert b.M == dry_M, name
                    return f
                deco(mk(name))

    return nc


_PI = pi_perm()


def _shard_inputs(L0_attn_x, L0_attn_w_qkv, L0_attn_b_qkv, L0_attn_w_out,
                  L0_attn_b_out, L0_ffn_gate_W_packed, L0_ffn_gate_scales,
                  L0_ffn_up_W_packed, L0_ffn_up_scales, L0_ffn_down_W_packed,
                  L0_ffn_down_scales, MLP, S):
    TB = S // NBLK
    AN = MLP // 2048
    x = np.ascontiguousarray(np.asarray(L0_attn_x, dtype=np.float32)[0])
    xT = np.ascontiguousarray(x.T[_PI])
    wq = np.asarray(L0_attn_w_qkv, dtype=np.float32)
    wo = np.asarray(L0_attn_w_out, dtype=np.float32)
    bq = np.asarray(L0_attn_b_qkv, dtype=np.float32)
    bo_ = np.asarray(L0_attn_b_out, dtype=np.float32)
    wqT = np.ascontiguousarray(wq.T[_PI])
    woT = np.ascontiguousarray(wo.T[:, _PI])

    def u8(a):
        return np.ascontiguousarray(np.asarray(a).view(np.uint8)[:, 0::4])

    g8, u8_, d8 = u8(L0_ffn_gate_W_packed), u8(L0_ffn_up_W_packed), u8(L0_ffn_down_W_packed)
    gs = np.asarray(L0_ffn_gate_scales, dtype=np.float32)
    us = np.asarray(L0_ffn_up_scales, dtype=np.float32)
    ds = np.asarray(L0_ffn_down_scales, dtype=np.float32)
    d8p = d8[_PI]
    dsp_ = ds[_PI]

    awc = np.concatenate([wqT, woT], axis=1)
    in_maps = []
    for c in range(8):
        m = {}
        m["xb0"] = np.ascontiguousarray(xT[:, TB * c:TB * (c + 1)])
        m["xb1"] = np.ascontiguousarray(xT[:, TB * (15 - c):TB * (16 - c)])
        m["gpk"] = np.ascontiguousarray(g8[:, 128 * c:128 * (c + 1)].T)
        m["upk"] = np.ascontiguousarray(u8_[:, 128 * c:128 * (c + 1)].T)
        pidx = 8 * c + np.arange(128) // 16
        m["gsc"] = np.ascontiguousarray(gs[:, pidx].T)
        m["usc"] = np.ascontiguousarray(us[:, pidx].T)
        nbw = MLP // 16
        dslice = d8p[:, nbw * c: nbw * (c + 1)]
        m["dpk"] = np.ascontiguousarray(
            dslice.T.reshape(AN, 128, 2048).transpose(1, 0, 2).reshape(128, AN * 2048))
        a_i = np.arange(AN)
        p_i = np.arange(128)
        col = (MLP // 256) * c + (a_i[None, :] * 8 + (p_i[:, None] // 16))
        m["dsc"] = np.ascontiguousarray(
            dsp_[:, col.reshape(-1)].reshape(2048, 128, AN).transpose(1, 2, 0).reshape(128, AN * 2048))
        m["aw"] = np.ascontiguousarray(awc[256 * c:256 * (c + 1), :])
        m["bqk"] = np.ascontiguousarray(bq[0:4096].reshape(32, 128).T)
        m["bvr"] = np.ascontiguousarray(bq[4096:6144].reshape(1, 2048))
        m["bo"] = np.ascontiguousarray(bo_[_PI].reshape(16, 128).T)
        in_maps.append(m)
    return in_maps


def _assemble(results, MLP, S):
    TB = S // NBLK
    inv = np.empty_like(_PI)
    inv[_PI] = np.arange(D)
    out = np.empty((1, S, D), dtype=np.float32)
    for c in range(8):
        out[0, TB * c:TB * (c + 1), :] = results[c]["ob0"].T[:, inv]
        out[0, TB * (15 - c):TB * (16 - c), :] = results[c]["ob1"].T[:, inv]
    return out


def kernel(**inputs):
    MLP, S = 16384, 8192
    in_maps = _shard_inputs(MLP=MLP, S=S, **inputs)
    nc = build_program(MLP, S)
    res = run_bass_kernel_spmd(nc, in_maps, core_ids=list(range(8)))
    return _assemble(res.results, MLP, S)


# version 16
# speedup vs baseline: 1.0788x; 1.0050x over previous
import numpy as np
import ml_dtypes
from contextlib import ExitStack
import concourse.bass as bass
import concourse.mybir as mybir
from concourse.bass_utils import run_bass_kernel_spmd

F32 = mybir.dt.float32
BF16 = mybir.dt.bfloat16
U8 = mybir.dt.uint8
AF = mybir.ActivationFunctionType
ALU = mybir.AluOpType

D = 2048
NH = 16
HD = 128
NBLK = 16
QK_SCALE = float(1.0 / np.sqrt(HD))
EXP_BIAS = -16.0
MASKVAL = -1e9
LN_EPS = 1e-5


def pi_perm():
    idx = []
    for c in range(8):
        idx.extend(range(256 * c, 256 * c + 256, 2))
        idx.extend(range(256 * c + 1, 256 * c + 256, 2))
    return np.array(idx, dtype=np.int64)


class Sem:
    def __init__(self, ctx, nc, name):
        self.h = ctx.enter_context(nc.semaphore(name))
        self.n = 0


class B:

    def __init__(self, sems):
        self.sems = sems
        self.mode = None
        self.eng = None
        self.M = {}

    def begin(self, mode, eng):
        self.mode = mode
        self.eng = eng
        for s in self.sems:
            s.n = 0

    def op(self, name, fn, sem=None, k=1):
        if self.mode == name:
            ins = fn(self.eng)
            if sem is not None:
                ins.then_inc(sem.h, k)
        if sem is not None:
            sem.n += k
            return sem.n
        return None

    def dynoff(self, e, coeff, const):
        if getattr(self, "_dreg_eng", None) is not e:
            self._dreg_eng = e
            self._pid = e.partition_id()
            self._dbase = e.alloc_register()
            self._dreg = e.alloc_register()
            self._dcoeff = None
        if self._dcoeff != coeff:
            self._dcoeff = coeff
            e.reg_alu(self._dbase, self._pid, coeff, mybir.AluOpType.mult)
        e.reg_alu(self._dreg, self._dbase, const, mybir.AluOpType.add)
        return self._dreg

    def wait(self, name, sem, key):
        if isinstance(key, str):
            if key not in self.M:
                assert self.mode is None, f"missing milestone {key} in live pass"
                return
            val = self.M[key]
        else:
            val = key
        if self.mode == name:
            self.eng.wait_ge(sem.h, val)


DEC_INLINE = True


def build_program(MLP, S):
    TB = S // NBLK
    TQ = min(128, TB)
    NTQ = TB // TQ
    NG = MLP // 128
    NSP = MLP // 256
    AN = MLP // 2048
    VS = HD + 1
    NCH = min(4096, MLP)
    assert MLP % 2048 == 0 and S % NBLK == 0

    nc = bass.Bass("TRN2", target_bir_lowering=False)

    xb = [nc.dram_tensor(f"xb{b}", [D, TB], F32, kind="ExternalInput").ap() for b in range(2)]
    gpk = nc.dram_tensor("gpk", [128, MLP], U8, kind="ExternalInput").ap()
    gsc = nc.dram_tensor("gsc", [128, MLP], F32, kind="ExternalInput").ap()
    upk = nc.dram_tensor("upk", [128, MLP], U8, kind="ExternalInput").ap()
    usc = nc.dram_tensor("usc", [128, MLP], F32, kind="ExternalInput").ap()
    dpk = nc.dram_tensor("dpk", [128, AN * D], U8, kind="ExternalInput").ap()
    dsc = nc.dram_tensor("dsc", [128, AN * D], F32, kind="ExternalInput").ap()
    aw = nc.dram_tensor("aw", [256, 4 * D], F32, kind="ExternalInput").ap()
    bqk = nc.dram_tensor("bqk", [128, 32], F32, kind="ExternalInput").ap()
    bvr = nc.dram_tensor("bvr", [1, D], F32, kind="ExternalInput").ap()
    bo = nc.dram_tensor("bo", [128, 16], F32, kind="ExternalInput").ap()
    ob = [nc.dram_tensor(f"ob{b}", [D, TB], F32, kind="ExternalOutput").ap() for b in range(2)]

    NCHK = max(MLP // 4096, 1)
    NCHC = MLP // NCHK
    blob_gu_t = [nc.dram_tensor(f"blob_gu{k}", [512, NCHC], BF16) for k in range(NCHK)]
    blob_dn_t = [nc.dram_tensor(f"blob_dn{j}", [2 * 8 * 128, AN * 128], BF16)
                 for j in range(2)]
    blob_at_t = nc.dram_tensor("blob_at", [256, 4 * D], BF16)
    WGU_t = [nc.dram_tensor(f"WGU{k}", [8 * 512, NCHC], BF16, addr_space="Shared") for k in range(NCHK)]
    WDN_t = [nc.dram_tensor(f"WDN{j}", [8 * 2 * 8 * 128, AN * 128], BF16, addr_space="Shared")
             for j in range(2)]
    WAT_t = nc.dram_tensor("WAT", [8 * 256, 4 * D], BF16, addr_space="Shared")
    QKIN = [nc.dram_tensor(f"QKIN{i}", [2 * D, TB], BF16) for i in range(2)]
    VIN = [nc.dram_tensor(f"VIN{i}", [TB, NH * VS], BF16) for i in range(2)]
    QKOUT = [nc.dram_tensor(f"QKOUT{i}", [8 * 2 * D, TB], BF16, addr_space="Shared") for i in range(2)]
    VOUT = [nc.dram_tensor(f"VOUT{i}", [8 * TB, NH * VS], BF16, addr_space="Shared") for i in range(2)]
    AIN = [nc.dram_tensor(f"AIN{i}", [256, 8 * TB], BF16) for i in range(2)]
    AOUT = [nc.dram_tensor(f"AOUT{i}", [8 * 256, 8 * TB], BF16, addr_space="Shared") for i in range(2)]
    R1_t = nc.dram_tensor("R1", [2, D, TB], F32)
    R2_t = nc.dram_tensor("R2", [2, D, TB], F32)

    with ExitStack() as ctx:
        def sbt(name, shape, dt):
            return ctx.enter_context(nc.sbuf_tensor(name, shape, dt))

        off_k = 0
        off_v = off_k + 2 * NBLK * TB
        off_q = off_v + 2 * NBLK * NTQ * VS
        off_p = off_q + 2 * NBLK * TB
        off_ao = off_p + 2 * TB
        off_aot = off_ao + 2 * NTQ * HD
        off_att = off_aot + 2 * TQ
        off_mask = off_att + 16 * TB
        off_smask = off_mask + NTQ * TB
        attn_el = off_smask + 2 * TB
        dk_pk = [0, NCH]
        dk_dqu = [[(2 + 2 * s + i) * NCH for i in range(2)] for s in range(2)]
        dk_dqf = [[(3 + 3 * s + i) * NCH for i in range(3)] for s in range(2)]
        dk_dqs = (9 * NCH) // 2
        dk_dqo = [11 * NCH + s * NCH for s in range(2)]
        dk_awb = 13 * NCH
        decode_el = dk_awb + 4 * D
        AR_EL = max(NG * TB, attn_el, decode_el)
        arena = sbt("arena", [128, AR_EL], BF16)

        def AR(off, width):
            return arena[:, off:off + width]

        arf = arena[:, :].bitcast(F32)
        aru = arena[:, :].bitcast(U8)
        off_bvrep = off_k

        strm = sbt("strm", [128, 16384], BF16)
        x_n = sbt("x_n", [128, 16 * TB], BF16)
        rbuf = [sbt(f"rbuf{i}", [128, TB], F32) for i in range(2)]
        arep = sbt("arep", [128, TB], F32)
        brep = sbt("brep", [128, TB], F32)
        lnt = [sbt(f"lnt{i}", [128, TB], F32) for i in range(2)]
        sqb = lnt[0]
        gelu_b = [sbt(f"gelu{i}", [128, TB], BF16) for i in range(2)]
        outb = [sbt(f"outb{i}", [128, TB], F32) for i in range(2)]
        evb = [sbt(f"evb{i}", [128, 512], BF16) for i in range(2)]
        stat = sbt("stat", [1, 4 * TB], F32)
        ones_col = sbt("ones_col", [128, 1], F32)
        ones_row = sbt("ones_row", [1, 128], F32)
        bqk_sb = sbt("bqk_sb", [128, 32], F32)
        bo_sb = sbt("bo_sb", [128, 16], F32)
        onesv = sbt("onesv", [128, max(NH, 1)], BF16)
        lrec = sbt("lrec", [128, 2], F32)
        cb = sbt("cb", [128, 6], F32)
        idf = sbt("idf", [128, 128], F32)
        idbf = sbt("idbf", [128, 128], BF16)

        for i, val in enumerate([EXP_BIAS, LN_EPS, 1.0, 0.5, -2.0, -6.0]):
            nc.const_aps.register(F32, val, cb[:, i:i + 1])

        def dqu_ap(s, i, w):
            return aru[:, dk_dqu[s][i]:dk_dqu[s][i] + w]

        def dqf_ap(s, i, w):
            return arena[:, dk_dqf[s][i]:dk_dqf[s][i] + w]

        def dqs_ap(w):
            return arf[:, dk_dqs:dk_dqs + w]

        def dqo_ap(s, w):
            return arena[:, dk_dqo[s]:dk_dqo[s] + w]

        def pk_ap(i, w):
            return aru[:, dk_pk[i]:dk_pk[i] + w]

        awb_ap = lambda: arena[:, dk_awb:dk_awb + 4 * D]
        bv_row_ap = lambda: arf[0:1, off_v // 2:off_v // 2 + D]
        bv_rep_ap = lambda: arena[:, off_bvrep:off_bvrep + D]
        maskT_ap = lambda: arena[:, off_mask:off_mask + NTQ * TB]
        smask_ap = lambda: arf[:, off_smask // 2:off_smask // 2 + TB]

        psA = ctx.enter_context(nc.psum_tensor("psA", [128, 512], F32))
        psB = ctx.enter_context(nc.psum_tensor("psB", [128, 512], F32))
        psC = ctx.enter_context(nc.psum_tensor("psC", [128, 512], F32))
        psD = ctx.enter_context(nc.psum_tensor("psD", [128, 512], F32))
        psE = ctx.enter_context(nc.psum_tensor("psE", [128, 512], F32))
        psF = ctx.enter_context(nc.psum_tensor("psF", [128, 512], F32))
        pst1 = ctx.enter_context(nc.psum_tensor("pst1", [128, 512], F32))
        pst2 = ctx.enter_context(nc.psum_tensor("pst2", [128, 512], F32))
        pqo = [psC, psD, psE, psF][:NTQ]

        s_pk = Sem(ctx, nc, "s_pk")
        s_in = Sem(ctx, nc, "s_in")
        s_pq = Sem(ctx, nc, "s_pq")
        s_vo = Sem(ctx, nc, "s_vo")
        s_k1 = Sem(ctx, nc, "s_k1")
        s_b1 = Sem(ctx, nc, "s_b1")
        s_s0 = Sem(ctx, nc, "s_s0")
        s_s1 = Sem(ctx, nc, "s_s1")
        s_r0 = Sem(ctx, nc, "s_r0")
        s_r1 = Sem(ctx, nc, "s_r1")
        s_w0 = Sem(ctx, nc, "s_w0")
        s_w1 = Sem(ctx, nc, "s_w1")
        s_q0 = Sem(ctx, nc, "s_q0")
        s_q1 = Sem(ctx, nc, "s_q1")
        s_kv = Sem(ctx, nc, "s_kv")
        s_ao = Sem(ctx, nc, "s_ao")
        s_bw = Sem(ctx, nc, "s_bw")
        s_aw = Sem(ctx, nc, "s_aw")
        s_cc = Sem(ctx, nc, "s_cc")
        s_pe = Sem(ctx, nc, "s_pe")
        s_ac = Sem(ctx, nc, "s_ac")
        s_dv = Sem(ctx, nc, "s_dv")
        s_gp = Sem(ctx, nc, "s_gp")
        sems = [s_pk, s_in, s_pq, s_vo, s_k1, s_b1, s_s0, s_s1, s_r0, s_r1, s_w0, s_w1, s_q0, s_q1,
                s_kv, s_ao, s_bw, s_aw, s_cc, s_pe, s_ac, s_dv, s_gp]
        sS = [s_s0, s_s1]
        sR = [s_r0, s_r1]
        sW = [s_w0, s_w1]
        sQ = [s_q0, s_q1]

        b = B(sems)
        M = b.M

        WGU_v = [t.ap().rearrange("(c q p) n -> p c q n", q=4, p=128) for t in WGU_t]
        WDN_v = [WDN_t[j].ap().rearrange("(c pl hc p) m -> p c pl hc m", pl=2, hc=8, p=128)
                 for j in range(2)]
        WATkc = WAT_t.ap().rearrange("(kc p) e -> p kc e", p=128)

        def gu_view(half):
            return strm[:, half * 8192:half * 8192 + 8192].rearrange(
                "p (c q n) -> p c q n", c=8, q=4)

        def dn_view(half):
            return strm[:, half * 8192:half * 8192 + 2 * AN * 128].rearrange(
                "p (pl m) -> p pl m", pl=2)

        def at_view(half):
            return strm[:, half * 8192:half * 8192 + 16 * 512].rearrange(
                "p (kc e) -> p kc e", kc=16)

        def inter_ap(g):
            return arena[:, g * TB:(g + 1) * TB]

        RG = [list(range(8))]

        def program():
            for i, val in enumerate([EXP_BIAS, LN_EPS, 1.0, 0.5, -2.0, -6.0]):
                b.op("gpsimd", lambda e, i=i, val=val: e.memset(cb[:, i:i + 1], val), s_gp)
            b.op("gpsimd", lambda e: e.memset(ones_col[:, :], 1.0), s_gp)
            b.op("gpsimd", lambda e: e.memset(ones_row[:, :], 1.0), s_gp)
            b.op("gpsimd", lambda e: e.memset(onesv[:, :], 1.0), s_gp)
            M["idz"] = b.op("gpsimd", lambda e: e.memset(idf[:, :], 0.0), s_gp)
            b.wait("gpsimd", s_gp, "idz")
            M["idsel"] = b.op("gpsimd", lambda e: e.affine_select(
                out=idf[:, :], in_=idf[:, :], compare_op=ALU.not_equal, fill=1.0,
                base=0, pattern=[[-1, 128]], channel_multiplier=1), s_gp)
            M["consts"] = s_gp.n
            for en in ("tensor", "vector", "scalar", "sync"):
                b.wait(en, s_gp, "consts")
            M["idbf"] = b.op("vector", lambda e: e.tensor_copy(out=idbf[:, :], in_=idf[:, :]), s_dv)

            b.op("sync", lambda e: e.dma_start(out=bqk_sb[:, :], in_=bqk), s_in, 16)
            b.op("sync", lambda e: e.dma_start(out=bo_sb[:, :], in_=bo), s_in, 16)
            M["bias_ld"] = s_in.n
            b.wait("vector", s_in, "bias_ld")
            M["bqs"] = b.op("vector", lambda e: e.tensor_scalar(
                out=bqk_sb[:, 0:16], in0=bqk_sb[:, 0:16], scalar1=QK_SCALE,
                scalar2=None, op0=ALU.mult), s_dv)

            blob_at = blob_at_t.ap()

            st = {"hist": [], "pkuse": [None, None], "ss": 0}

            def dq_pk_load(mat, bi_, pk_src, c0, w):
                if st["pkuse"][bi_] is not None:
                    b.wait("scalar", s_dv, st["pkuse"][bi_])
                M[f"pk_{mat}_{c0}"] = b.op("scalar", lambda e, pk_src=pk_src, bi_=bi_, c0=c0, w=w: e.dma_start(
                    out=pk_ap(bi_, w), in_=pk_src[:, c0:c0 + w]), [s_pk, s_pq][bi_], 16)

            def dq_sc_load(mat, sc_src, c0, w, first=False):
                if st["hist"]:
                    b.wait("scalar", s_gp, f"v4_{st['hist'][-1]}")
                elif first:
                    b.wait("scalar", s_dv, "bqs")
                M[f"sc_{mat}_{c0}"] = b.op("scalar", lambda e, sc_src=sc_src, c0=c0, w=w: e.dma_start(
                    out=dqs_ap(w), in_=sc_src[:, c0:c0 + w]), s_in, 16)

            def dq_chunk(mat, bi_, c0, w, out_fn):
                for pl in range(2):
                    tag = f"{mat}_{c0}_{pl}"
                    ss = st["ss"]
                    st["ss"] ^= 1
                    hist = st["hist"]
                    pt2 = hist[-2] if len(hist) >= 2 else None
                    scm = 1.0 if pl == 0 else (1.0 / 16.0)
                    sa = 0.5 if mat == "dn" else 1.0
                    mm_ = 7 if pl == 0 else 0x70
                    b.wait("vector", [s_pk, s_pq][bi_], f"pk_{mat}_{c0}")
                    if pt2 is not None:
                        b.wait("vector", s_ac, f"sgn_{pt2}")
                    if pl == 0:
                        M[f"nib_{tag}"] = b.op("vector", lambda e, bi_=bi_, w=w, ss=ss: (
                            e.tensor_scalar(out=dqu_ap(ss, 0, w), in0=pk_ap(bi_, w),
                                            scalar1=7, scalar2=None, op0=ALU.bitwise_and),
                            e.tensor_scalar(out=dqu_ap(ss, 1, w), in0=pk_ap(bi_, w),
                                            scalar1=8, scalar2=None, op0=ALU.bitwise_and))[-1], s_dv)
                    else:
                        M[f"nib_{tag}"] = b.op("vector", lambda e, bi_=bi_, w=w, ss=ss: (
                            e.tensor_scalar(out=dqu_ap(ss, 0, w), in0=pk_ap(bi_, w),
                                            scalar1=0x70, scalar2=None, op0=ALU.bitwise_and),
                            e.tensor_scalar(out=dqu_ap(ss, 1, w), in0=pk_ap(bi_, w),
                                            scalar1=128, scalar2=None, op0=ALU.is_ge))[-1], s_dv)
                    b.wait("scalar", s_dv, f"nib_{tag}")
                    b.wait("scalar", s_gp, "consts")
                    if pt2 is not None:
                        b.wait("scalar", s_gp, f"v4_{pt2}")
                    M[f"dqa_{tag}"] = b.op("scalar", lambda e, w=w, ss=ss, scm=scm: (
                        e.activation(out=dqf_ap(ss, 0, w), in_=dqu_ap(ss, 0, w), func=AF.Copy, scale=0.5 * scm),
                        e.activation(out=dqf_ap(ss, 1, w), in_=dqu_ap(ss, 0, w), func=AF.Relu, scale=0.5 * scm, bias=-2.0),
                        e.activation(out=dqf_ap(ss, 2, w), in_=dqu_ap(ss, 0, w), func=AF.Relu, scale=scm, bias=-6.0))[-1], s_ac)
                    b.wait("vector", s_ac, f"dqa_{tag}")
                    M[f"v1_{tag}"] = b.op("vector", lambda e, w=w, ss=ss: e.tensor_add(
                        out=dqf_ap(ss, 0, w), in0=dqf_ap(ss, 0, w), in1=dqf_ap(ss, 1, w)), s_dv)
                    sgs = (-0.25 * sa) if pl == 0 else (-2.0 * sa)
                    b.wait("scalar", s_dv, f"v1_{tag}")
                    M[f"sgn_{tag}"] = b.op("scalar", lambda e, w=w, ss=ss, sgs=sgs, sa=sa: e.activation(
                        out=dqf_ap(ss, 1, w), in_=dqu_ap(ss, 1, w), func=AF.Copy,
                        scale=sgs, bias=sa), s_ac)
                    b.wait("vector", s_dv, f"v1_{tag}")
                    M[f"v2_{tag}"] = b.op("vector", lambda e, w=w, ss=ss: e.tensor_add(
                        out=dqf_ap(ss, 0, w), in0=dqf_ap(ss, 0, w), in1=dqf_ap(ss, 2, w)), s_dv)
                    b.wait("gpsimd", s_dv, f"v2_{tag}")
                    b.wait("gpsimd", s_ac, f"sgn_{tag}")
                    M[f"v3_{tag}"] = b.op("gpsimd", lambda e, w=w, ss=ss: e.tensor_mul(
                        out=dqf_ap(ss, 0, w), in0=dqf_ap(ss, 0, w), in1=dqf_ap(ss, 1, w)), s_gp)
                    b.wait("gpsimd", s_gp, f"v3_{tag}")
                    b.wait("gpsimd", s_in, f"sc_{mat}_{c0}")
                    if pt2 is not None:
                        b.wait("gpsimd", [s_bw, s_b1][ss], f"dq_wr_{pt2}")
                    M[f"v4_{tag}"] = b.op("gpsimd", lambda e, w=w, ss=ss: e.tensor_mul(
                        out=dqo_ap(ss, w), in0=dqf_ap(ss, 0, w), in1=dqs_ap(w)), s_gp)
                    b.wait("gpsimd", s_gp, f"v4_{tag}")
                    fns = out_fn(pl, c0, w, ss)
                    if not isinstance(fns, list):
                        fns = [fns]
                    for f_ in fns:
                        M[f"dq_wr_{tag}"] = b.op("gpsimd", f_, [s_bw, s_b1][ss], 16)
                    hist.append(tag)
                st["pkuse"][bi_] = f"nib_{mat}_{c0}_1"

            def gu_out(ch, rowbase):
                def f(pl, c0, w, ss):
                    return lambda e: e.dma_start(
                        out=blob_gu_t[ch].ap()[rowbase + 128 * pl: rowbase + 128 * pl + 128, :],
                        in_=dqo_ap(ss, w))
                return f

            def dn_out(pl, c0, w, ss):
                a0, na = c0 // D, w // D
                dviews = [blob_dn_t[j].ap().rearrange("(pl hc p) m -> pl hc p m", pl=2, hc=8)
                          for j in range(2)]
                return [
                    (lambda e, ai=ai, j=j: e.dma_start(
                        out=dviews[j][pl, :, :, (a0 + ai) * 128:(a0 + ai + 1) * 128]
                            .rearrange("hc p hh -> p hc hh"),
                        in_=dqo_ap(ss, w).rearrange("p (a hc hh) -> p a hc hh", a=na, hc=16)[:, ai, 8 * j:8 * j + 8, :]))
                    for ai in range(na) for j in range(2)]

            def dec_gu(ch):
                for mi, (mat, pk_src, sc_src, rowb) in enumerate(
                        [("g", gpk, gsc, 0), ("u", upk, usc, 256)]):
                    c0 = ch * NCHC
                    dq_pk_load(mat, mi, pk_src, c0, NCHC)
                    dq_sc_load(mat, sc_src, c0, NCHC, first=(ch == 0 and mi == 0))
                    dq_chunk(mat, mi, c0, NCHC, gu_out(ch, rowb))
                M[f"gu_dec0_{ch}"] = s_bw.n
                M[f"gu_dec1_{ch}"] = s_b1.n
                b.wait("gpsimd", s_bw, f"gu_dec0_{ch}")
                b.wait("gpsimd", s_b1, f"gu_dec1_{ch}")
                M[f"cc_gu_{ch}"] = b.op("gpsimd", lambda e, ch=ch: e.collective_compute(
                    "AllGather", ALU.bypass, replica_groups=RG,
                    ins=[blob_gu_t[ch].ap()], outs=[WGU_t[ch].ap()]), s_cc)

            ND = (AN * D + NCH - 1) // NCH

            def dec_dn(ch):
                c0, w = ch * NCH, min(NCH, AN * D - ch * NCH)
                dq_pk_load("dn", ch % 2, dpk, c0, w)
                dq_sc_load("dn", dsc, c0, w)
                dq_chunk("dn", ch % 2, c0, w, dn_out)
                if ch == ND - 1:
                    M["dn_dec0"] = s_bw.n
                    M["dn_dec1"] = s_b1.n
                    b.wait("gpsimd", s_bw, "dn_dec0")
                    b.wait("gpsimd", s_b1, "dn_dec1")
                    for j in range(2):
                        M[f"cc_dn_{j}"] = b.op("gpsimd", lambda e, j=j: e.collective_compute(
                            "AllGather", ALU.bypass, replica_groups=RG,
                            ins=[blob_dn_t[j].ap()], outs=[WDN_t[j].ap()]), s_cc)

            def dec_aw():
                for hl in range(2):
                    if hl == 0:
                        b.wait("gpsimd", s_dv, st["pkuse"][0])
                        b.wait("gpsimd", s_dv, st["pkuse"][1])
                        M["aw_pre0"] = s_bw.n
                        M["aw_pre1"] = s_b1.n
                        b.wait("gpsimd", s_bw, "aw_pre0")
                        b.wait("gpsimd", s_b1, "aw_pre1")
                    M[f"aw_ld{hl}"] = b.op("gpsimd", lambda e, hl=hl: e.dma_start(
                        out=awb_ap(), in_=aw[128 * hl:128 * hl + 128, :]), s_aw, 16)
                    b.wait("gpsimd", s_aw, f"aw_ld{hl}")
                    M[f"aw_wr{hl}"] = b.op("gpsimd", lambda e, hl=hl: e.dma_start(
                        out=blob_at[128 * hl:128 * hl + 128, :], in_=awb_ap()), s_bw, 16)
                    b.wait("gpsimd", s_bw, f"aw_wr{hl}")
                M["cc_at"] = b.op("gpsimd", lambda e: e.collective_compute(
                    "AllGather", ALU.bypass, replica_groups=RG,
                    ins=[blob_at_t.ap()], outs=[WAT_t.ap()]), s_cc)

            def ffn0_hook(sp):
                if DEC_INLINE:
                    return
                ch = sp // 2
                if sp % 2 == 0 and ch < ND:
                    dec_dn(ch)
                elif sp == 2 * ND:
                    dec_aw()

            def emit_phase0():
                emit_R_loads(0, xb[0], [[], []])
                layer_norm(0, None, None)
                for ch in range(NCHK):
                    dec_gu(ch)
                if DEC_INLINE:
                    for ch in range(ND):
                        dec_dn(ch)
                    dec_aw()

            def phase_barrier(ph):
                if ph == 0:
                    return
                M[f"bar_dv_{ph}"] = s_dv.n
                M[f"bar_ac_{ph}"] = s_ac.n
                b.wait("tensor", s_dv, f"bar_dv_{ph}")
                b.wait("tensor", s_ac, f"bar_ac_{ph}")

            def emit_R_loads(ph, src, prev_keys):
                for idx in range(32):
                    j = idx % 16
                    par = idx % 2
                    if idx >= 2:
                        if idx - 2 < 16:
                            b.wait("sync", s_pe, f"sx2_{ph}_{idx - 2}")
                            b.wait("sync", s_ac, f"sq_{ph}_{idx - 2}")
                        else:
                            b.wait("sync", s_dv, f"nm1_{ph}_{idx - 18}")
                    else:
                        for (sem, key) in prev_keys[idx]:
                            b.wait("sync", sem, key)
                    M[f"r_ld_{ph}_{idx}"] = b.op("sync", lambda e, j=j, par=par, src=src: e.dma_start(
                        out=rbuf[par][:, :], in_=src[128 * j:128 * j + 128, :]), sR[par], 16)

            def emit_res_loads(ph, src, prev_keys):
                for hc in range(16):
                    par = hc % 2
                    if hc >= 2:
                        b.wait("sync", s_dv, f"resfree_{ph}_{hc - 2}")
                    else:
                        for (sem, key) in prev_keys[hc]:
                            b.wait("sync", sem, key)
                    M[f"res_ld_{ph}_{hc}"] = b.op("sync", lambda e, hc=hc, par=par, src=src: e.dma_start(
                        out=rbuf[par][:, :], in_=src[128 * hc:128 * hc + 128, :]), sR[par], 16)

            def layer_norm(ph, prev_xn_key, prev_lnt_keys):
                phase_barrier(ph)
                for j in range(16):
                    par = j % 2
                    b.wait("scalar", sR[par], f"r_ld_{ph}_{j}")
                    if j >= 1:
                        b.wait("scalar", s_pe, f"sx2_{ph}_{j - 1}")
                    M[f"sq_{ph}_{j}"] = b.op("scalar", lambda e, par=par: e.activation(
                        out=sqb[:, :], in_=rbuf[par][:, :], func=AF.Square), s_ac)
                    b.wait("tensor", sR[par], f"r_ld_{ph}_{j}")
                    M[f"sx_{ph}_{j}"] = b.op("tensor", lambda e, j=j, par=par: e.matmul(
                        pst1[0:1, 0:TB], ones_col[:, :], rbuf[par][:, :],
                        start=(j == 0), stop=(j == 15), skip_group_check=True), s_pe)
                    b.wait("tensor", s_ac, f"sq_{ph}_{j}")
                    M[f"sx2_{ph}_{j}"] = b.op("tensor", lambda e, j=j: e.matmul(
                        pst2[0:1, 0:TB], ones_col[:, :], sqb[:, :],
                        start=(j == 0), stop=(j == 15), skip_group_check=True), s_pe)
                st_ = stat
                b.wait("vector", s_pe, f"sx2_{ph}_15")
                M[f"st1_{ph}"] = b.op("vector", lambda e: e.tensor_scalar(
                    out=st_[:, 0:TB], in0=pst1[0:1, 0:TB], scalar1=1.0 / D,
                    scalar2=None, op0=ALU.mult), s_dv)
                M[f"st2_{ph}"] = b.op("vector", lambda e: e.tensor_scalar(
                    out=st_[:, TB:2 * TB], in0=pst2[0:1, 0:TB], scalar1=1.0 / D,
                    scalar2=None, op0=ALU.mult), s_dv)
                b.wait("vector", s_dv, f"st2_{ph}")
                M[f"st3_{ph}"] = b.op("vector", lambda e: e.tensor_mul(
                    out=st_[:, 2 * TB:3 * TB], in0=st_[:, 0:TB], in1=st_[:, 0:TB]), s_dv)
                b.wait("vector", s_dv, f"st3_{ph}")
                M[f"st4_{ph}"] = b.op("vector", lambda e: e.tensor_sub(
                    out=st_[:, TB:2 * TB], in0=st_[:, TB:2 * TB], in1=st_[:, 2 * TB:3 * TB]), s_dv)
                b.wait("scalar", s_dv, f"st4_{ph}")
                M[f"st5_{ph}"] = b.op("scalar", lambda e: e.activation(
                    out=st_[:, 2 * TB:3 * TB], in_=st_[:, TB:2 * TB], func=AF.Ln,
                    bias=LN_EPS), s_ac)
                b.wait("scalar", s_ac, f"st5_{ph}")
                M[f"st6_{ph}"] = b.op("scalar", lambda e: e.activation(
                    out=st_[:, 3 * TB:4 * TB], in_=st_[:, 2 * TB:3 * TB], func=AF.Exp,
                    scale=-0.5), s_ac)
                b.wait("vector", s_ac, f"st6_{ph}")
                M[f"st7_{ph}"] = b.op("vector", lambda e: e.tensor_mul(
                    out=st_[:, 0:TB], in0=st_[:, 0:TB], in1=st_[:, 3 * TB:4 * TB]), s_dv)
                b.wait("vector", s_dv, f"st7_{ph}")
                M[f"st8_{ph}"] = b.op("vector", lambda e: e.tensor_scalar(
                    out=st_[:, 0:TB], in0=st_[:, 0:TB], scalar1=-1.0,
                    scalar2=None, op0=ALU.mult), s_dv)
                b.wait("tensor", s_ac, f"st6_{ph}")
                b.wait("tensor", s_dv, f"st8_{ph}")
                M[f"rep1_{ph}"] = b.op("tensor", lambda e: e.matmul(
                    psA[:, 0:TB], ones_row[:, :], stat[0:1, 3 * TB:4 * TB],
                    start=True, stop=True, skip_group_check=True), s_pe)
                M[f"rep2_{ph}"] = b.op("tensor", lambda e: e.matmul(
                    psB[:, 0:TB], ones_row[:, :], stat[0:1, 0:TB],
                    start=True, stop=True, skip_group_check=True), s_pe)
                b.wait("vector", s_pe, f"rep2_{ph}")
                M[f"arep_{ph}"] = b.op("vector", lambda e: e.tensor_copy(
                    out=arep[:, :], in_=psA[:, 0:TB]), s_dv)
                b.wait("vector", s_dv, f"arep_{ph}")
                M[f"brepc_{ph}"] = b.op("vector", lambda e: e.tensor_copy(
                    out=brep[:, :], in_=psB[:, 0:TB]), s_dv)
                for j in range(16):
                    jj = 16 + j
                    par = jj % 2
                    b.wait("vector", sR[par], f"r_ld_{ph}_{jj}")
                    b.wait("vector", s_dv, f"brepc_{ph}")
                    if j < 2 and prev_lnt_keys is not None:
                        for (sem, key) in prev_lnt_keys[j]:
                            b.wait("vector", sem, key)
                    if j >= 2:
                        b.wait("vector", s_dv, f"nm2_{ph}_{j - 2}")
                    M[f"nm1_{ph}_{j}"] = b.op("vector", lambda e, j=j, par=par: e.tensor_mul(
                        out=lnt[j % 2][:, :], in0=rbuf[par][:, :], in1=arep[:, :]), s_dv)
                    b.wait("vector", s_dv, f"nm1_{ph}_{j}")
                    if j == 0 and prev_xn_key is not None:
                        b.wait("vector", s_pe, prev_xn_key)
                    M[f"nm2_{ph}_{j}"] = b.op("vector", lambda e, j=j: e.tensor_add(
                        out=x_n[:, j * TB:(j + 1) * TB], in0=lnt[j % 2][:, :],
                        in1=brep[:, :]), s_dv)
                M[f"xn_{ph}"] = s_dv.n

            def ffn(ph, src, dst, prev_gu, prev_dn, prev_xn, prev_r, prev_wr, hook=None,
                    pre_ln=False):
                if not pre_ln:
                    emit_R_loads(ph, src, prev_r)
                    layer_norm(ph, prev_xn, None)
                spc = NCHC // 256
                for sp in range(NSP):
                    if hook is not None:
                        hook(sp)
                    half = sp % 2
                    kch = sp // spc
                    if sp >= 2:
                        b.wait("sync", s_pe, f"gu_fin_{ph}_{sp - 2}")
                    else:
                        for (sem, key) in prev_gu[sp]:
                            b.wait("sync", sem, key)
                    if ph == 0 and (sp % spc) < 2:
                        b.wait("sync", s_cc, f"cc_gu_{kch}")
                    for ph_ in range(2):
                        M[f"gu_ld_{ph}_{sp}"] = b.op("sync", lambda e, sp=sp, half=half, kch=kch, ph_=ph_: e.dma_start(
                            out=gu_view(half)[:, 4 * ph_:4 * ph_ + 4, :, :],
                            in_=WGU_v[kch][:, 4 * ph_:4 * ph_ + 4, :,
                                           256 * sp - kch * NCHC: 256 * sp - kch * NCHC + 256]), sS[half], 16)
                    b.wait("tensor", sS[half], f"gu_ld_{ph}_{sp}")
                    b.wait("tensor", s_dv, f"xn_{ph}")
                    for par2 in range(2):
                        g = 2 * sp + par2
                        pg, pu = (psA, psB) if par2 == 0 else (psE, psF)
                        if g >= 2:
                            b.wait("tensor", s_ac, f"gel_{ph}_{g - 2}")
                            b.wait("tensor", s_dv, f"inter_{ph}_{g - 2}")
                        else:
                            b.wait("tensor", s_dv, f"arep_{ph}")
                            b.wait("tensor", s_dv, f"brepc_{ph}")
                        for kc in range(16):
                            c0, pl = kc // 2, kc % 2
                            r = b.op("tensor", lambda e, half=half, c0=c0, pl=pl, par2=par2, kc=kc, pg=pg: e.matmul(
                                pg[:, 0:TB], gu_view(half)[:, c0, pl, par2:256:2],
                                x_n[:, (2 * c0 + pl) * TB:(2 * c0 + pl + 1) * TB],
                                start=(kc == 0), stop=(kc == 15), skip_group_check=True),
                                s_pe if kc == 15 else None)
                        M[f"gmm_{ph}_{g}"] = r
                        for kc in range(16):
                            c0, pl = kc // 2, kc % 2
                            r = b.op("tensor", lambda e, half=half, c0=c0, pl=pl, par2=par2, kc=kc, pu=pu: e.matmul(
                                pu[:, 0:TB], gu_view(half)[:, c0, 2 + pl, par2:256:2],
                                x_n[:, (2 * c0 + pl) * TB:(2 * c0 + pl + 1) * TB],
                                start=(kc == 0), stop=(kc == 15), skip_group_check=True),
                                s_pe if kc == 15 else None)
                        M[f"umm_{ph}_{g}"] = r
                        b.wait("scalar", s_pe, f"gmm_{ph}_{g}")
                        if g >= 2:
                            b.wait("scalar", s_dv, f"inter_{ph}_{g - 2}")
                        M[f"gel_{ph}_{g}"] = b.op("scalar", lambda e, g=g, pg=pg: e.activation(
                            out=gelu_b[g % 2][:, :], in_=pg[:, 0:TB],
                            func=AF.Gelu_apprx_tanh), s_ac)
                        b.wait("vector", s_ac, f"gel_{ph}_{g}")
                        b.wait("vector", s_pe, f"umm_{ph}_{g}")
                        M[f"inter_{ph}_{g}"] = b.op("vector", lambda e, g=g, pu=pu: e.tensor_tensor(
                            out=inter_ap(g), in0=pu[:, 0:TB], in1=gelu_b[g % 2][:, :],
                            op=ALU.mult), s_dv)
                    M[f"gu_fin_{ph}_{sp}"] = M[f"umm_{ph}_{2 * sp + 1}"]
                for hc in range(16):
                    par = hc % 2
                    if hc >= 2:
                        b.wait("sync", s_dv, f"resfree_{ph}_{hc - 2}")
                    else:
                        b.wait("sync", s_dv, f"nm1_{ph}_{14 + hc}")
                    M[f"res_ld_{ph}_{hc}"] = b.op("sync", lambda e, hc=hc, par=par, src=src: e.dma_start(
                        out=rbuf[par][:, :], in_=src[128 * hc:128 * hc + 128, :]), sR[par], 16)
                    pdn = psA if hc % 2 == 0 else psE
                    for dsp in range(8):
                        idx = hc * 8 + dsp
                        half = idx % 2
                        if idx >= 2:
                            b.wait("sync", s_pe, f"dn_fin_{ph}_{idx - 2}")
                        else:
                            for (sem, key) in prev_dn[idx]:
                                b.wait("sync", sem, key)
                        if ph == 0 and hc == 8 and dsp < 2:
                            b.wait("sync", s_cc, "cc_dn_1")
                        M[f"dn_ld_{ph}_{idx}"] = b.op("sync", lambda e, hc=hc, dsp=dsp, half=half: e.dma_start(
                            out=dn_view(half), in_=WDN_v[hc // 8][:, dsp, :, hc % 8, :]), sS[half], 16)
                    if hc == 0:
                        b.wait("tensor", s_dv, f"inter_{ph}_{NG - 1}")
                        b.wait("tensor", s_dv, f"inter_{ph}_{NG - 2}")
                        b.wait("tensor", s_ac, f"gel_{ph}_{NG - 1}")
                        b.wait("tensor", s_ac, f"gel_{ph}_{NG - 2}")
                    if hc >= 2:
                        b.wait("tensor", s_dv, f"resadd_{ph}_{hc - 2}")
                    first = True
                    for dsp in range(8):
                        idx = hc * 8 + dsp
                        half = idx % 2
                        b.wait("tensor", sS[half], f"dn_ld_{ph}_{idx}")
                        for pl in range(2):
                            for a in range(AN):
                                g = 2 * (dsp * AN + a) + pl
                                stop_ = (dsp == 7 and pl == 1 and a == AN - 1)
                                mark = (pl == 1 and a == AN - 1)
                                r = b.op("tensor", lambda e, half=half, pl=pl, a=a, g=g, pdn=pdn, first=first, stop_=stop_: e.matmul(
                                    pdn[:, 0:TB], dn_view(half)[:, pl, a * 128:(a + 1) * 128],
                                    inter_ap(g), start=first, stop=stop_,
                                    skip_group_check=True), s_pe if mark else None)
                                first = False
                        M[f"dn_fin_{ph}_{idx}"] = r
                    M[f"dn_mm_{ph}_{hc}"] = r
                    par = hc % 2
                    b.wait("vector", s_pe, f"dn_mm_{ph}_{hc}")
                    b.wait("vector", sR[par], f"res_ld_{ph}_{hc}")
                    if hc >= 2:
                        b.wait("vector", sW[par], f"rwr_{ph}_{hc - 2}")
                    else:
                        for (sem, key) in prev_wr[hc]:
                            b.wait("vector", sem, key)
                    M[f"resadd_{ph}_{hc}"] = b.op("vector", lambda e, par=par, pdn=pdn: e.tensor_add(
                        out=outb[par][:, :], in0=pdn[:, 0:TB], in1=rbuf[par][:, :]), s_dv)
                    M[f"resfree_{ph}_{hc}"] = M[f"resadd_{ph}_{hc}"]
                    b.wait("gpsimd", s_dv, f"resadd_{ph}_{hc}")
                    M[f"rwr_{ph}_{hc}"] = b.op("gpsimd", lambda e, hc=hc, par=par, dst=dst: e.dma_start(
                        out=dst[128 * hc:128 * hc + 128, :], in_=outb[par][:, :]), sW[par], 16)
                M[f"ffn_wr_tot0_{ph}"] = s_w0.n
                M[f"ffn_wr_tot1_{ph}"] = s_w1.n

            R1 = R1_t.ap()
            R2 = R2_t.ap()

            emit_phase0()
            ffn(0, xb[0], R1[0],
                prev_gu=[[], []],
                prev_dn=[[(s_cc, "cc_dn_0"), (s_pe, "gu_fin_0_%d" % (NSP - 2))],
                         [(s_cc, "cc_dn_0"), (s_pe, "gu_fin_0_%d" % (NSP - 1))]],
                prev_xn=None,
                prev_r=[[], []],
                prev_wr=[[], []],
                hook=ffn0_hook,
                pre_ln=True)
            ffn(1, xb[1], R1[1],
                prev_gu=[[(s_pe, "dn_fin_0_126")], [(s_pe, "dn_fin_0_127")]],
                prev_dn=[[(s_pe, "gu_fin_1_%d" % (NSP - 2))],
                         [(s_pe, "gu_fin_1_%d" % (NSP - 1))]],
                prev_xn="gu_fin_0_%d" % (NSP - 1),
                prev_r=[[(s_dv, "resadd_0_14")], [(s_dv, "resadd_0_15")]],
                prev_wr=[[(s_w0, "rwr_0_14")], [(s_w1, "rwr_0_15")]])

            for bi in range(2):
                ph = 10 + bi
                b.wait("sync", sW[0], f"ffn_wr_tot0_{bi}")
                b.wait("sync", sW[1], f"ffn_wr_tot1_{bi}")
                if bi == 0:
                    b.wait("sync", s_dv, "resadd_1_15")
                    b.wait("sync", s_pe, "dn_mm_1_15")
                    M["bvr_ld"] = b.op("sync", lambda e: e.dma_start(
                        out=bv_row_ap(), in_=bvr), s_in, 16)
                    b.wait("tensor", s_in, "bvr_ld")
                    for i in range(D // 512):
                        pg2 = psA if i % 2 == 0 else psB
                        if i >= 2:
                            b.wait("tensor", s_dv, f"bvcp{i-2}")
                        M[f"bvrep{i}"] = b.op("tensor", lambda e, i=i, pg2=pg2: e.matmul(
                            pg2[:, :], ones_row[:, :], bv_row_ap()[:, 512 * i:512 * i + 512],
                            start=True, stop=True, skip_group_check=True), s_pe)
                        b.wait("vector", s_pe, f"bvrep{i}")
                        M[f"bvcp{i}"] = b.op("vector", lambda e, i=i, pg2=pg2: e.tensor_copy(
                            out=bv_rep_ap()[:, 512 * i:512 * i + 512], in_=pg2[:, :]), s_dv)
                    M["bv_done"] = s_dv.n
                emit_R_loads(ph, R1[bi],
                             [[(s_dv, "resadd_1_14")], [(s_dv, "resadd_1_15")]] if bi == 0
                             else [[(s_dv, "nm1_10_14")], [(s_dv, "nm1_10_15")]])
                layer_norm(ph, prev_xn_key=("gu_fin_1_%d" % (NSP - 1)) if bi == 0 else "aw_fin_0_11",
                           prev_lnt_keys=None)
                for eb in range(12):
                    half = eb % 2
                    if eb >= 2:
                        b.wait("sync", s_pe, f"aw_fin_{bi}_{eb - 2}")
                    elif bi == 0:
                        b.wait("sync", s_cc, "cc_at")
                        b.wait("sync", s_pe, f"dn_fin_1_{126 + eb}")
                    else:
                        b.wait("sync", s_pe, f"aw_fin_0_{10 + eb}")
                    M[f"aw_ld_{bi}_{eb}"] = b.op("sync", lambda e, eb=eb, half=half: e.dma_start(
                        out=at_view(half), in_=WATkc[:, :, 512 * eb:512 * eb + 512]), sS[half], 16)
                    b.wait("tensor", sS[half], f"aw_ld_{bi}_{eb}")
                    b.wait("tensor", s_dv, f"xn_{ph}")
                    if eb < 8:
                        for sub in range(4):
                            ec = eb * 4 + sub
                            pg = psA if ec % 2 == 0 else psB
                            if ec >= 2:
                                b.wait("tensor", s_dv, f"qkev_{bi}_{ec - 2}")
                            else:
                                b.wait("tensor", s_dv, f"arep_{ph}")
                                b.wait("tensor", s_dv, f"brepc_{ph}")
                            for kc in range(16):
                                r = b.op("tensor", lambda e, half=half, kc=kc, sub=sub, pg=pg: e.matmul(
                                    pg[:, 0:TB], at_view(half)[:, kc, sub * 128:(sub + 1) * 128],
                                    x_n[:, kc * TB:(kc + 1) * TB],
                                    start=(kc == 0), stop=(kc == 15), skip_group_check=True),
                                    s_pe if kc == 15 else None)
                            M[f"qkmm_{bi}_{ec}"] = r
                            b.wait("vector", s_pe, f"qkmm_{bi}_{ec}")
                            if ec >= 2:
                                b.wait("vector", sQ[ec % 2], f"qkwr_{bi}_{ec - 2}")
                            elif bi == 1:
                                b.wait("vector", sQ[ec % 2], f"vwr_0_{4 * NTQ - 2 + ec}")
                            isq = ec < 16
                            M[f"qkev_{bi}_{ec}"] = b.op("vector", lambda e, ec=ec, pg=pg, isq=isq: e.tensor_scalar(
                                out=evb[ec % 2][:, 0:TB], in0=pg[:, 0:TB],
                                scalar1=QK_SCALE if isq else 1.0,
                                scalar2=bqk_sb[:, ec:ec + 1],
                                op0=ALU.mult, op1=ALU.add), s_dv)
                            row = (0 if isq else D) + (ec % 16) * 128
                            b.wait("sync", s_dv, f"qkev_{bi}_{ec}")
                            M[f"qkwr_{bi}_{ec}"] = b.op("sync", lambda e, bi=bi, row=row, ec=ec: e.dma_start(
                                out=QKIN[bi].ap()[row:row + 128, :], in_=evb[ec % 2][:, 0:TB]), sQ[ec % 2], 16)
                        M[f"aw_fin_{bi}_{eb}"] = M[f"qkmm_{bi}_{eb * 4 + 3}"]
                    else:
                        evs = eb - 8
                        for ts in range(NTQ):
                            vi = evs * NTQ + ts
                            pg = psA if vi % 2 == 0 else psB
                            if vi >= 2:
                                b.wait("tensor", s_dv, f"vev_{bi}_{vi - 2}")
                            elif evs == 0:
                                b.wait("tensor", s_dv, f"qkev_{bi}_30")
                                b.wait("tensor", s_dv, f"qkev_{bi}_31")
                            for kc in range(16):
                                r = b.op("tensor", lambda e, half=half, kc=kc, ts=ts, pg=pg: e.matmul(
                                    pg[0:TQ, 0:512],
                                    x_n[:, kc * TB + ts * TQ: kc * TB + (ts + 1) * TQ],
                                    at_view(half)[:, kc, :],
                                    start=(kc == 0), stop=(kc == 15), skip_group_check=True),
                                    s_pe if kc == 15 else None)
                            M[f"vmm_{bi}_{vi}"] = r
                            b.wait("vector", s_pe, f"vmm_{bi}_{vi}")
                            if vi >= 2:
                                b.wait("vector", sQ[vi % 2], f"vwr_{bi}_{vi - 2}")
                            else:
                                b.wait("vector", sQ[vi % 2], f"qkwr_{bi}_{30 + vi}")
                            M[f"vev_{bi}_{vi}"] = b.op("vector", lambda e, vi=vi, evs=evs, pg=pg: e.tensor_tensor(
                                out=evb[vi % 2][0:TQ, 0:512], in0=pg[0:TQ, 0:512],
                                in1=bv_rep_ap()[0:TQ, 512 * evs:512 * evs + 512], op=ALU.add), s_dv)
                            b.wait("sync", s_dv, f"vev_{bi}_{vi}")
                            M[f"vwr_{bi}_{vi}"] = b.op("sync", lambda e, bi=bi, vi=vi, evs=evs, ts=ts: e.dma_start(
                                out=VIN[bi].ap().rearrange("t (h s) -> t h s", s=VS)[
                                    ts * TQ:(ts + 1) * TQ, 4 * evs:4 * evs + 4, 0:HD],
                                in_=evb[vi % 2][0:TQ, 0:512].rearrange("t (h s) -> t h s", s=HD)), sQ[vi % 2], 16)
                        M[f"aw_fin_{bi}_{eb}"] = M[f"vmm_{bi}_{(eb - 8 + 1) * NTQ - 1}"]
                if bi == 1:
                    b.wait("sync", s_cc, "cc_v0")
                for ts in range(NTQ):
                    M[f"vones_{bi}"] = b.op("sync", lambda e, bi=bi, ts=ts: e.dma_start(
                        out=VIN[bi].ap().rearrange("(s p) (h v) -> p s h v", p=TQ, v=VS)[:, ts, :, HD:VS],
                        in_=onesv[0:TQ, 0:NH].unsqueeze(2),
                        allow_slow_non_contiguous=True), s_vo, 16)
                M[f"q0_tot_{bi}"] = sQ[0].n
                M[f"q1_tot_{bi}"] = sQ[1].n
                b.wait("gpsimd", sQ[0], f"q0_tot_{bi}")
                b.wait("gpsimd", sQ[1], f"q1_tot_{bi}")
                b.wait("gpsimd", s_vo, f"vones_{bi}")
                M[f"cc_qk{bi}"] = b.op("gpsimd", lambda e, bi=bi: e.collective_compute(
                    "AllGather", ALU.bypass, replica_groups=RG,
                    ins=[QKIN[bi].ap()], outs=[QKOUT[bi].ap()]), s_cc)
                M[f"cc_v{bi}"] = b.op("gpsimd", lambda e, bi=bi: e.collective_compute(
                    "AllGather", ALU.bypass, replica_groups=RG,
                    ins=[VIN[bi].ap()], outs=[VOUT[bi].ap()]), s_cc)

            def kv_rank(gb):
                return gb if gb < 8 else 15 - gb

            def slot(gb):
                return (0 if gb < 8 else 8) + kv_rank(gb)

            def kj_off(j, gb):
                return off_k + (j * NBLK + slot(gb)) * TB

            def vj_off(j, gb):
                return off_v + (j * NBLK + slot(gb)) * NTQ * VS

            def qj_off(j, gb):
                return off_q + (j * NBLK + slot(gb)) * TB

            for hf in range(2):
                skv_h = s_kv if hf == 0 else s_k1
                b.wait("sync", s_cc, "cc_v0" if hf == 0 else "cc_v1")
                for j in range(2):
                    b.op("sync", lambda e, hf=hf, j=j: e.dma_start(
                        out=AR(off_k + (j * NBLK + hf * 8) * TB, 8 * TB)
                            .rearrange("p (r t) -> p r t", r=8),
                        in_=bass.AP(QKOUT[hf],
                                    b.dynoff(e, 256 * TB, D * TB + (128 * TB) * j),
                                    [[TB, HD], [2 * D * TB, 8], [1, TB]])), skv_h, 16)
                    b.op("sync", lambda e, hf=hf, j=j: e.dma_start(
                        out=AR(off_v + (j * NBLK + hf * 8) * NTQ * VS, 8 * NTQ * VS)
                            .rearrange("p (rs v) -> p rs v", v=VS)[0:TQ],
                        in_=bass.AP(VOUT[hf],
                                    b.dynoff(e, 2 * VS, VS * j),
                                    [[NH * VS, TQ], [TQ * NH * VS, 8 * NTQ], [1, VS]])), skv_h, 16)
                for j in range(2):
                    b.op("sync", lambda e, hf=hf, j=j: e.dma_start(
                        out=AR(off_q + (j * NBLK + hf * 8) * TB, 8 * TB)
                            .rearrange("p (r t) -> p r t", r=8)[0:HD],
                        in_=bass.AP(QKOUT[hf],
                                    b.dynoff(e, 256 * TB, (128 * TB) * j),
                                    [[TB, HD], [2 * D * TB, 8], [1, TB]])), skv_h, 16)
                if hf == 0:
                    M["kv_half0"] = s_kv.n
            M["kv_all"] = s_k1.n

            phase_barrier(99)
            b.wait("gpsimd", s_dv, "resadd_1_15")
            b.wait("gpsimd", s_pe, "dn_mm_1_15")
            M["mz"] = b.op("gpsimd", lambda e: e.memset(maskT_ap(), 0.0), s_gp)
            b.wait("gpsimd", s_gp, "mz")
            for tkc in range(NTQ):
                M["mbuild"] = b.op("gpsimd", lambda e, tkc=tkc: e.affine_select(
                    out=maskT_ap()[:, tkc * TB:(tkc + 1) * TB],
                    in_=maskT_ap()[:, tkc * TB:(tkc + 1) * TB],
                    compare_op=ALU.is_ge, fill=MASKVAL, base=-tkc * TQ,
                    pattern=[[1, TB]], channel_multiplier=-1), s_gp)
            b.wait("tensor", s_kv, "kv_half0")
            for gb in range(NBLK):
                if gb == 8:
                    b.wait("tensor", s_k1, "kv_all")
                for j in range(2):
                    nch_ = (gb + 1) * NTQ
                    ci = 0
                    if gb + j > 0:
                        pj, pgb = (0, gb) if j == 1 else (1, gb - 1)
                        for tqs in range(NTQ):
                            b.wait("tensor", s_dv, f"aofin_{pgb}_{pj}_{tqs}")
                    for kv in range(gb + 1):
                        for tkc in range(NTQ):
                            pss = psA if ci % 2 == 0 else psB
                            half = ci % 2
                            if ci >= 2:
                                b.wait("tensor", s_ac, f"exp_{gb}_{j}_{ci - 2}")
                            elif gb + j > 0:
                                pj2, pgb2 = (0, gb) if j == 1 else (1, gb - 1)
                                pn2 = (pgb2 + 1) * NTQ
                                li = pn2 - 1 - ((pn2 - 1 - ci) % 2)
                                if li >= 0:
                                    b.wait("tensor", s_ac, f"exp_{pgb2}_{pj2}_{li}")
                            M[f"smm_{gb}_{j}_{ci}"] = b.op("tensor", lambda e, j=j, kv=kv, tkc=tkc, gb=gb, pss=pss: e.matmul(
                                pss[0:TQ, 0:TB],
                                arena[0:HD, kj_off(j, kv) + tkc * TQ: kj_off(j, kv) + (tkc + 1) * TQ],
                                AR(qj_off(j, gb), TB)[0:HD],
                                start=True, stop=True, skip_group_check=True), s_pe)
                            if ci >= 2:
                                b.wait("scalar", s_pe, f"pv_{gb}_{j}_{ci - 2}")
                            elif gb + j > 0:
                                pj2, pgb2 = (0, gb) if j == 1 else (1, gb - 1)
                                b.wait("scalar", s_pe, f"pvmm_{pgb2}_{pj2}")
                            if kv == gb:
                                b.wait("vector", s_pe, f"smm_{gb}_{j}_{ci}")
                                b.wait("vector", s_gp, "mbuild")
                                if tkc > 0:
                                    b.wait("vector", s_ac, f"exp_{gb}_{j}_{ci - 1}")
                                elif gb + j > 0:
                                    pj2, pgb2 = (0, gb) if j == 1 else (1, gb - 1)
                                    pn2 = (pgb2 + 1) * NTQ
                                    b.wait("vector", s_ac, f"exp_{pgb2}_{pj2}_{pn2 - 1}")
                                M[f"msk_{gb}_{j}_{ci}"] = b.op("vector", lambda e, tkc=tkc, pss=pss: e.tensor_add(
                                    out=smask_ap()[0:TQ, :], in0=pss[0:TQ, 0:TB],
                                    in1=maskT_ap()[0:TQ, tkc * TB:(tkc + 1) * TB]), s_dv)
                                b.wait("scalar", s_dv, f"msk_{gb}_{j}_{ci}")
                                M[f"exp_{gb}_{j}_{ci}"] = b.op("scalar", lambda e, half=half: e.activation(
                                    out=AR(off_p + half * TB, TB)[0:TQ, :],
                                    in_=smask_ap()[0:TQ, 0:TB], func=AF.Exp, bias=EXP_BIAS), s_ac)
                            else:
                                b.wait("scalar", s_pe, f"smm_{gb}_{j}_{ci}")
                                M[f"exp_{gb}_{j}_{ci}"] = b.op("scalar", lambda e, half=half, pss=pss: e.activation(
                                    out=AR(off_p + half * TB, TB)[0:TQ, :],
                                    in_=pss[0:TQ, 0:TB], func=AF.Exp, bias=EXP_BIAS), s_ac)
                            b.wait("tensor", s_ac, f"exp_{gb}_{j}_{ci}")
                            for tqs in range(NTQ):
                                r = b.op("tensor", lambda e, half=half, tqs=tqs, j=j, kv=kv, tkc=tkc, ci=ci, nch_=nch_: e.matmul(
                                    pqo[tqs][0:TQ, 0:VS],
                                    arena[0:TQ, off_p + half * TB + tqs * TQ: off_p + half * TB + (tqs + 1) * TQ],
                                    AR(vj_off(j, kv), NTQ * VS).rearrange("p (s v) -> p s v", v=VS)[0:TQ, tkc, :],
                                    start=(ci == 0), stop=(ci == nch_ - 1),
                                    skip_group_check=True),
                                    s_pe if tqs == NTQ - 1 else None)
                            M[f"pv_{gb}_{j}_{ci}"] = r
                            ci += 1
                    M[f"pvmm_{gb}_{j}"] = r
                    for tqs in range(NTQ):
                        b.wait("vector", s_pe, f"pvmm_{gb}_{j}")
                        if gb + j > 0:
                            pj, pgb = (0, gb) if j == 1 else (1, gb - 1)
                            b.wait("vector", s_ao, f"ao_grp_{pgb}_{pj}")
                        M[f"lc_{gb}_{j}_{tqs}"] = b.op("vector", lambda e, tqs=tqs: e.tensor_copy(
                            out=lrec[0:TQ, 0:1], in_=pqo[tqs][0:TQ, HD:HD + 1]), s_dv)
                        b.wait("vector", s_dv, f"lc_{gb}_{j}_{tqs}")
                        M[f"lr_{gb}_{j}_{tqs}"] = b.op("vector", lambda e: e.reciprocal(
                            out=lrec[0:TQ, 1:2], in_=lrec[0:TQ, 0:1]), s_dv)
                        b.wait("vector", s_dv, f"lr_{gb}_{j}_{tqs}")
                        M[f"aofin_{gb}_{j}_{tqs}"] = b.op("vector", lambda e, tqs=tqs, j=j: e.tensor_scalar(
                            out=AR(off_ao + j * NTQ * HD + tqs * HD, HD)[0:TQ, :],
                            in0=pqo[tqs][0:TQ, 0:HD], scalar1=lrec[0:TQ, 1:2],
                            scalar2=None, op0=ALU.mult), s_dv)
                        sl = tqs % 2
                        ptt = pst1 if sl == 0 else pst2
                        b.wait("tensor", s_dv, f"aofin_{gb}_{j}_{tqs}")
                        if gb + j + tqs > 0:
                            b.wait("tensor", s_dv, "aotc_prev")
                        M["aotr_cur"] = b.op("tensor", lambda e, tqs=tqs, j=j, ptt=ptt: e.transpose(
                            ptt[:, 0:128].bitcast(BF16)[:, 0:TQ],
                            AR(off_ao + j * NTQ * HD + tqs * HD, HD)[0:TQ, :],
                            idbf[0:TQ, 0:TQ]), s_pe)
                        b.wait("vector", s_pe, "aotr_cur")
                        if gb + j + tqs > 0:
                            b.wait("vector", s_ao, "aowr_prev")
                        M["aotc_prev"] = b.op("vector", lambda e, sl=sl, ptt=ptt: e.tensor_copy(
                            out=AR(off_aot + sl * TQ, TQ),
                            in_=ptt[:, 0:128].bitcast(BF16)[:, 0:TQ]), s_dv)
                        b.wait("sync", s_dv, "aotc_prev")
                        b.op("sync", lambda e, gb=gb, j=j, tqs=tqs, sl=sl: e.dma_start(
                            out=AIN[0 if gb < 8 else 1].ap()[
                                128 * j:128 * j + 128,
                                (gb % 8) * TB + tqs * TQ:(gb % 8) * TB + (tqs + 1) * TQ],
                            in_=AR(off_aot + sl * TQ, TQ)), s_ao, 16)
                        M["aowr_prev"] = s_ao.n
                    M[f"ao_grp_{gb}_{j}"] = s_ao.n
                M[f"fl_fin_{gb}"] = M[f"pvmm_{gb}_1"]
                if gb == 7:
                    M["ao_tot_0"] = s_ao.n
                    b.wait("gpsimd", s_ao, "ao_tot_0")
                    M["cc_ao0"] = b.op("gpsimd", lambda e: e.collective_compute(
                        "AllGather", ALU.bypass, replica_groups=RG,
                        ins=[AIN[0].ap()], outs=[AOUT[0].ap()]), s_cc)
            M["ao_tot_1"] = s_ao.n
            b.wait("gpsimd", s_ao, "ao_tot_1")
            M["cc_ao1"] = b.op("gpsimd", lambda e: e.collective_compute(
                "AllGather", ALU.bypass, replica_groups=RG,
                ins=[AIN[1].ap()], outs=[AOUT[1].ap()]), s_cc)

            for bi in range(2):
                ph = 20 + bi
                phase_barrier(ph)
                b.wait("sync", s_cc, "cc_ao0" if bi == 0 else "cc_ao1")
                if bi == 1:
                    b.wait("sync", s_pe, f"op_all_0")
                coeff = TB if bi == 0 else -TB
                cnst = 0 if bi == 0 else 7 * TB
                for he in range(2):
                    b.op("sync", lambda e, bi=bi, he=he, coeff=coeff, cnst=cnst: e.dma_start(
                        out=arena[:, off_att:off_att + 16 * TB]
                            .rearrange("p (ec t) -> p ec t", t=TB)[:, he::2, :],
                        in_=bass.AP(AOUT[bi], b.dynoff(e, coeff, cnst + 128 * he),
                                    [[8 * TB, 128], [256 * 8 * TB, 8], [1, TB]])), s_q0, 16)
                M[f"atld_{bi}"] = s_q0.n
                for wb in range(4):
                    half = wb % 2
                    if wb >= 2:
                        b.wait("sync", s_pe, f"op_fin_{bi}_{wb - 2}")
                    elif bi == 0:
                        b.wait("sync", s_pe, "aw_fin_1_10" if wb == 0 else "aw_fin_1_11")
                    else:
                        b.wait("sync", s_pe, f"op_fin_0_{2 + wb}")
                    M[f"ow_ld_{bi}_{wb}"] = b.op("sync", lambda e, wb=wb, half=half: e.dma_start(
                        out=at_view(half), in_=WATkc[:, :, 512 * (12 + wb):512 * (12 + wb) + 512]), sS[half], 16)
                    b.wait("tensor", sS[half], f"ow_ld_{bi}_{wb}")
                    b.wait("tensor", s_q0, f"atld_{bi}")
                    for sub in range(4):
                        dc = wb * 4 + sub
                        par = dc % 2
                        pg = psE if dc % 2 == 0 else psF
                        if dc >= 2:
                            b.wait("sync", s_dv, f"resfree_{ph}_{dc - 2}")
                        elif bi == 0:
                            b.wait("sync", s_dv, "nm1_11_%d" % (14 + dc))
                        else:
                            b.wait("sync", s_dv, "resfree_20_%d" % (14 + dc))
                        M[f"res_ld_{ph}_{dc}"] = b.op("sync", lambda e, dc=dc, par=par, bi=bi: e.dma_start(
                            out=rbuf[par][:, :], in_=R1[bi][128 * dc:128 * dc + 128, :]), sR[par], 16)
                        if dc >= 2:
                            b.wait("tensor", s_dv, f"opadd_{bi}_{dc - 2}")
                        for ec in range(16):
                            r = b.op("tensor", lambda e, half=half, ec=ec, sub=sub, pg=pg: e.matmul(
                                pg[:, 0:TB], at_view(half)[:, ec, sub * 128:(sub + 1) * 128],
                                arena[:, off_att + ec * TB: off_att + (ec + 1) * TB],
                                start=(ec == 0), stop=(ec == 15), skip_group_check=True),
                                s_pe if ec == 15 else None)
                        M[f"opmm_{bi}_{dc}"] = r
                        b.wait("vector", s_pe, f"opmm_{bi}_{dc}")
                        b.wait("vector", sR[par], f"res_ld_{ph}_{dc}")
                        if dc >= 2:
                            b.wait("vector", sW[par], f"r2wr_{bi}_{dc - 2}")
                        elif bi == 0:
                            b.wait("vector", sW[par], f"rwr_1_{14 + dc}")
                        else:
                            b.wait("vector", sW[par], f"r2wr_0_{14 + dc}")
                        M[f"opadd_{bi}_{dc}"] = b.op("vector", lambda e, par=par, dc=dc, pg=pg: e.scalar_tensor_tensor(
                            out=outb[par][:, :], in0=pg[:, 0:TB],
                            scalar=bo_sb[:, dc:dc + 1], in1=rbuf[par][:, :],
                            op0=ALU.add, op1=ALU.add), s_dv)
                        M[f"resfree_{ph}_{dc}"] = M[f"opadd_{bi}_{dc}"]
                        b.wait("gpsimd", s_dv, f"opadd_{bi}_{dc}")
                        M[f"r2wr_{bi}_{dc}"] = b.op("gpsimd", lambda e, bi=bi, dc=dc, par=par: e.dma_start(
                            out=R2[bi][128 * dc:128 * dc + 128, :], in_=outb[par][:, :]), sW[par], 16)
                    M[f"op_fin_{bi}_{wb}"] = M[f"opmm_{bi}_{4 * wb + 3}"]
                M[f"op_all_{bi}"] = M[f"opmm_{bi}_15"]
                M[f"op_wr_tot0_{bi}"] = s_w0.n
                M[f"op_wr_tot1_{bi}"] = s_w1.n

            ffn(2, R2[0], ob[0],
                prev_gu=[[(s_pe, "op_fin_1_2")], [(s_pe, "op_fin_1_3")]],
                prev_dn=[[(s_pe, "gu_fin_2_%d" % (NSP - 2))],
                         [(s_pe, "gu_fin_2_%d" % (NSP - 1))]],
                prev_xn="op_fin_1_3",
                prev_r=[[(s_dv, "resfree_21_14"), (s_w0, "op_wr_tot0_0"), (s_w1, "op_wr_tot1_0")],
                        [(s_dv, "resfree_21_15")]],
                prev_wr=[[(s_w0, "r2wr_1_14")], [(s_w1, "r2wr_1_15")]])
            ffn(3, R2[1], ob[1],
                prev_gu=[[(s_pe, "dn_fin_2_126")], [(s_pe, "dn_fin_2_127")]],
                prev_dn=[[(s_pe, "gu_fin_3_%d" % (NSP - 2))],
                         [(s_pe, "gu_fin_3_%d" % (NSP - 1))]],
                prev_xn="gu_fin_2_%d" % (NSP - 1),
                prev_r=[[(s_dv, "resadd_2_14"), (s_w0, "op_wr_tot0_1"), (s_w1, "op_wr_tot1_1")],
                        [(s_dv, "resadd_2_15")]],
                prev_wr=[[(s_w0, "rwr_2_14")], [(s_w1, "rwr_2_15")]])

            b.wait("sync", s_w0, "ffn_wr_tot0_3")
            b.wait("sync", s_w1, "ffn_wr_tot1_3")

        b.begin(None, None)
        program()
        dry_counts = [s.n for s in sems]
        dry_M = dict(b.M)
        with nc.Block() as blk:
            for name, deco in (("sync", blk.sync), ("gpsimd", blk.gpsimd),
                               ("vector", blk.vector), ("scalar", blk.scalar),
                               ("tensor", blk.tensor)):
                def mk(name):
                    def f(eng):
                        b.begin(name, eng)
                        program()
                        assert [s.n for s in sems] == dry_counts, name
                        assert b.M == dry_M, name
                    return f
                deco(mk(name))

    return nc


_PI = pi_perm()


def _shard_inputs(L0_attn_x, L0_attn_w_qkv, L0_attn_b_qkv, L0_attn_w_out,
                  L0_attn_b_out, L0_ffn_gate_W_packed, L0_ffn_gate_scales,
                  L0_ffn_up_W_packed, L0_ffn_up_scales, L0_ffn_down_W_packed,
                  L0_ffn_down_scales, MLP, S):
    TB = S // NBLK
    AN = MLP // 2048
    x = np.ascontiguousarray(np.asarray(L0_attn_x, dtype=np.float32)[0])
    xT = np.ascontiguousarray(x.T[_PI])
    wq = np.asarray(L0_attn_w_qkv, dtype=np.float32)
    wo = np.asarray(L0_attn_w_out, dtype=np.float32)
    bq = np.asarray(L0_attn_b_qkv, dtype=np.float32)
    bo_ = np.asarray(L0_attn_b_out, dtype=np.float32)
    wqT = np.ascontiguousarray(wq.T[_PI])
    woT = np.ascontiguousarray(wo.T[:, _PI])

    def u8(a):
        return np.ascontiguousarray(np.asarray(a).view(np.uint8)[:, 0::4])

    g8, u8_, d8 = u8(L0_ffn_gate_W_packed), u8(L0_ffn_up_W_packed), u8(L0_ffn_down_W_packed)
    gs = np.asarray(L0_ffn_gate_scales, dtype=np.float32)
    us = np.asarray(L0_ffn_up_scales, dtype=np.float32)
    ds = np.asarray(L0_ffn_down_scales, dtype=np.float32)
    d8p = d8[_PI]
    dsp_ = ds[_PI]

    awc = np.concatenate([wqT, woT], axis=1)
    in_maps = []
    for c in range(8):
        m = {}
        m["xb0"] = np.ascontiguousarray(xT[:, TB * c:TB * (c + 1)])
        m["xb1"] = np.ascontiguousarray(xT[:, TB * (15 - c):TB * (16 - c)])
        m["gpk"] = np.ascontiguousarray(g8[:, 128 * c:128 * (c + 1)].T)
        m["upk"] = np.ascontiguousarray(u8_[:, 128 * c:128 * (c + 1)].T)
        pidx = 8 * c + np.arange(128) // 16
        m["gsc"] = np.ascontiguousarray(gs[:, pidx].T)
        m["usc"] = np.ascontiguousarray(us[:, pidx].T)
        nbw = MLP // 16
        dslice = d8p[:, nbw * c: nbw * (c + 1)]
        m["dpk"] = np.ascontiguousarray(
            dslice.T.reshape(AN, 128, 2048).transpose(1, 0, 2).reshape(128, AN * 2048))
        a_i = np.arange(AN)
        p_i = np.arange(128)
        col = (MLP // 256) * c + (a_i[None, :] * 8 + (p_i[:, None] // 16))
        m["dsc"] = np.ascontiguousarray(
            dsp_[:, col.reshape(-1)].reshape(2048, 128, AN).transpose(1, 2, 0).reshape(128, AN * 2048))
        m["aw"] = np.ascontiguousarray(awc[256 * c:256 * (c + 1), :])
        m["bqk"] = np.ascontiguousarray(bq[0:4096].reshape(32, 128).T)
        m["bvr"] = np.ascontiguousarray(bq[4096:6144].reshape(1, 2048))
        m["bo"] = np.ascontiguousarray(bo_[_PI].reshape(16, 128).T)
        in_maps.append(m)
    return in_maps


def _assemble(results, MLP, S):
    TB = S // NBLK
    inv = np.empty_like(_PI)
    inv[_PI] = np.arange(D)
    out = np.empty((1, S, D), dtype=np.float32)
    for c in range(8):
        out[0, TB * c:TB * (c + 1), :] = results[c]["ob0"].T[:, inv]
        out[0, TB * (15 - c):TB * (16 - c), :] = results[c]["ob1"].T[:, inv]
    return out


def kernel(**inputs):
    MLP, S = 16384, 8192
    in_maps = _shard_inputs(MLP=MLP, S=S, **inputs)
    nc = build_program(MLP, S)
    res = run_bass_kernel_spmd(nc, in_maps, core_ids=list(range(8)))
    return _assemble(res.results, MLP, S)


# version 17 (final)
# speedup vs baseline: 1.0795x; 1.0006x over previous
import numpy as np
import ml_dtypes
from contextlib import ExitStack
import concourse.bass as bass
import concourse.mybir as mybir
from concourse.bass_utils import run_bass_kernel_spmd

F32 = mybir.dt.float32
BF16 = mybir.dt.bfloat16
U8 = mybir.dt.uint8
AF = mybir.ActivationFunctionType
ALU = mybir.AluOpType

D = 2048
NH = 16
HD = 128
NBLK = 16
QK_SCALE = float(1.0 / np.sqrt(HD))
EXP_BIAS = -16.0
MASKVAL = -1e9
LN_EPS = 1e-5


def pi_perm():
    idx = []
    for c in range(8):
        idx.extend(range(256 * c, 256 * c + 256, 2))
        idx.extend(range(256 * c + 1, 256 * c + 256, 2))
    return np.array(idx, dtype=np.int64)


class Sem:
    def __init__(self, ctx, nc, name):
        self.h = ctx.enter_context(nc.semaphore(name))
        self.n = 0


class B:

    def __init__(self, sems):
        self.sems = sems
        self.mode = None
        self.eng = None
        self.M = {}

    def begin(self, mode, eng):
        self.mode = mode
        self.eng = eng
        for s in self.sems:
            s.n = 0

    def op(self, name, fn, sem=None, k=1):
        if self.mode == name:
            ins = fn(self.eng)
            if sem is not None:
                ins.then_inc(sem.h, k)
        if sem is not None:
            sem.n += k
            return sem.n
        return None

    def dynoff(self, e, coeff, const):
        if getattr(self, "_dreg_eng", None) is not e:
            self._dreg_eng = e
            self._pid = e.partition_id()
            self._dbase = e.alloc_register()
            self._dreg = e.alloc_register()
            self._dcoeff = None
        if self._dcoeff != coeff:
            self._dcoeff = coeff
            e.reg_alu(self._dbase, self._pid, coeff, mybir.AluOpType.mult)
        e.reg_alu(self._dreg, self._dbase, const, mybir.AluOpType.add)
        return self._dreg

    def wait(self, name, sem, key):
        if isinstance(key, str):
            if key not in self.M:
                assert self.mode is None, f"missing milestone {key} in live pass"
                return
            val = self.M[key]
        else:
            val = key
        if self.mode == name:
            self.eng.wait_ge(sem.h, val)


DEC_INLINE = True


def build_program(MLP, S):
    TB = S // NBLK
    TQ = min(128, TB)
    NTQ = TB // TQ
    NG = MLP // 128
    NSP = MLP // 256
    AN = MLP // 2048
    VS = HD + 1
    NCH = min(4096, MLP)
    assert MLP % 2048 == 0 and S % NBLK == 0

    nc = bass.Bass("TRN2", target_bir_lowering=False)

    xb = [nc.dram_tensor(f"xb{b}", [D, TB], F32, kind="ExternalInput").ap() for b in range(2)]
    gpk = nc.dram_tensor("gpk", [128, MLP], U8, kind="ExternalInput").ap()
    gsc = nc.dram_tensor("gsc", [128, MLP], F32, kind="ExternalInput").ap()
    upk = nc.dram_tensor("upk", [128, MLP], U8, kind="ExternalInput").ap()
    usc = nc.dram_tensor("usc", [128, MLP], F32, kind="ExternalInput").ap()
    dpk = nc.dram_tensor("dpk", [128, AN * D], U8, kind="ExternalInput").ap()
    dsc = nc.dram_tensor("dsc", [128, AN * D], F32, kind="ExternalInput").ap()
    aw = nc.dram_tensor("aw", [256, 4 * D], F32, kind="ExternalInput").ap()
    bqk = nc.dram_tensor("bqk", [128, 32], F32, kind="ExternalInput").ap()
    bvr = nc.dram_tensor("bvr", [1, D], F32, kind="ExternalInput").ap()
    bo = nc.dram_tensor("bo", [128, 16], F32, kind="ExternalInput").ap()
    ob = [nc.dram_tensor(f"ob{b}", [D, TB], F32, kind="ExternalOutput").ap() for b in range(2)]

    NCHK = max(MLP // 4096, 1)
    NCHC = MLP // NCHK
    blob_gu_t = [nc.dram_tensor(f"blob_gu{k}", [512, NCHC], BF16) for k in range(NCHK)]
    blob_dn_t = [nc.dram_tensor(f"blob_dn{j}", [2 * 8 * 128, AN * 128], BF16)
                 for j in range(2)]
    blob_at_t = nc.dram_tensor("blob_at", [256, 4 * D], BF16)
    WGU_t = [nc.dram_tensor(f"WGU{k}", [8 * 512, NCHC], BF16, addr_space="Shared") for k in range(NCHK)]
    WDN_t = [nc.dram_tensor(f"WDN{j}", [8 * 2 * 8 * 128, AN * 128], BF16, addr_space="Shared")
             for j in range(2)]
    WAT_t = nc.dram_tensor("WAT", [8 * 256, 4 * D], BF16, addr_space="Shared")
    QKIN = [nc.dram_tensor(f"QKIN{i}", [2 * D, TB], BF16) for i in range(2)]
    VIN = [nc.dram_tensor(f"VIN{i}", [TB, NH * VS], BF16) for i in range(2)]
    QKOUT = [nc.dram_tensor(f"QKOUT{i}", [8 * 2 * D, TB], BF16, addr_space="Shared") for i in range(2)]
    VOUT = [nc.dram_tensor(f"VOUT{i}", [8 * TB, NH * VS], BF16, addr_space="Shared") for i in range(2)]
    AIN = [nc.dram_tensor(f"AIN{i}", [256, 8 * TB], BF16) for i in range(2)]
    AOUT = [nc.dram_tensor(f"AOUT{i}", [8 * 256, 8 * TB], BF16, addr_space="Shared") for i in range(2)]
    R1_t = nc.dram_tensor("R1", [2, D, TB], F32)
    R2_t = nc.dram_tensor("R2", [2, D, TB], F32)

    with ExitStack() as ctx:
        def sbt(name, shape, dt):
            return ctx.enter_context(nc.sbuf_tensor(name, shape, dt))

        off_k = 0
        off_v = off_k + 2 * NBLK * TB
        off_q = off_v + 2 * NBLK * NTQ * VS
        off_p = off_q + 2 * NBLK * TB
        off_ao = off_p + 2 * TB
        off_aot = off_ao + 2 * NTQ * HD
        off_att = off_aot + 2 * TQ
        off_mask = off_att + 16 * TB
        off_smask = off_mask + NTQ * TB
        attn_el = off_smask + 2 * TB
        dk_pk = [0, NCH]
        dk_dqu = [[(2 + 2 * s + i) * NCH for i in range(2)] for s in range(2)]
        dk_dqf = [[(3 + 3 * s + i) * NCH for i in range(3)] for s in range(2)]
        dk_dqs = (9 * NCH) // 2
        dk_dqo = [11 * NCH + s * NCH for s in range(2)]
        dk_awb = 13 * NCH
        decode_el = dk_awb + 4 * D
        AR_EL = max(NG * TB, attn_el, decode_el)
        arena = sbt("arena", [128, AR_EL], BF16)

        def AR(off, width):
            return arena[:, off:off + width]

        arf = arena[:, :].bitcast(F32)
        aru = arena[:, :].bitcast(U8)
        off_bvrep = off_k

        strm = sbt("strm", [128, 16384], BF16)
        x_n = sbt("x_n", [128, 16 * TB], BF16)
        rbuf = [sbt(f"rbuf{i}", [128, TB], F32) for i in range(2)]
        arep = sbt("arep", [128, TB], F32)
        brep = sbt("brep", [128, TB], F32)
        lnt = [sbt(f"lnt{i}", [128, TB], F32) for i in range(2)]
        sqb = lnt[0]
        gelu_b = [sbt(f"gelu{i}", [128, TB], BF16) for i in range(2)]
        outb = [sbt(f"outb{i}", [128, TB], F32) for i in range(2)]
        evb = [sbt(f"evb{i}", [128, 512], BF16) for i in range(2)]
        stat = sbt("stat", [1, 4 * TB], F32)
        ones_col = sbt("ones_col", [128, 1], F32)
        ones_row = sbt("ones_row", [1, 128], F32)
        bqk_sb = sbt("bqk_sb", [128, 32], F32)
        bo_sb = sbt("bo_sb", [128, 16], F32)
        onesv = sbt("onesv", [128, max(NH, 1)], BF16)
        lrec = sbt("lrec", [128, 2], F32)
        cb = sbt("cb", [128, 6], F32)
        idf = sbt("idf", [128, 128], F32)
        idbf = sbt("idbf", [128, 128], BF16)

        for i, val in enumerate([EXP_BIAS, LN_EPS, 1.0, 0.5, -2.0, -6.0]):
            nc.const_aps.register(F32, val, cb[:, i:i + 1])

        def dqu_ap(s, i, w):
            return aru[:, dk_dqu[s][i]:dk_dqu[s][i] + w]

        def dqf_ap(s, i, w):
            return arena[:, dk_dqf[s][i]:dk_dqf[s][i] + w]

        def dqs_ap(w):
            return arf[:, dk_dqs:dk_dqs + w]

        def dqo_ap(s, w):
            return arena[:, dk_dqo[s]:dk_dqo[s] + w]

        def pk_ap(i, w):
            return aru[:, dk_pk[i]:dk_pk[i] + w]

        awb_ap = lambda: arena[:, dk_awb:dk_awb + 4 * D]
        bv_row_ap = lambda: arf[0:1, off_v // 2:off_v // 2 + D]
        bv_rep_ap = lambda: arena[:, off_bvrep:off_bvrep + D]
        maskT_ap = lambda: arena[:, off_mask:off_mask + NTQ * TB]
        smask_ap = lambda: arf[:, off_smask // 2:off_smask // 2 + TB]

        psA = ctx.enter_context(nc.psum_tensor("psA", [128, 512], F32))
        psB = ctx.enter_context(nc.psum_tensor("psB", [128, 512], F32))
        psC = ctx.enter_context(nc.psum_tensor("psC", [128, 512], F32))
        psD = ctx.enter_context(nc.psum_tensor("psD", [128, 512], F32))
        psE = ctx.enter_context(nc.psum_tensor("psE", [128, 512], F32))
        psF = ctx.enter_context(nc.psum_tensor("psF", [128, 512], F32))
        pst1 = ctx.enter_context(nc.psum_tensor("pst1", [128, 512], F32))
        pst2 = ctx.enter_context(nc.psum_tensor("pst2", [128, 512], F32))
        pqo = [psC, psD, psE, psF][:NTQ]

        s_pk = Sem(ctx, nc, "s_pk")
        s_in = Sem(ctx, nc, "s_in")
        s_pq = Sem(ctx, nc, "s_pq")
        s_vo = Sem(ctx, nc, "s_vo")
        s_k1 = Sem(ctx, nc, "s_k1")
        s_b1 = Sem(ctx, nc, "s_b1")
        s_s0 = Sem(ctx, nc, "s_s0")
        s_s1 = Sem(ctx, nc, "s_s1")
        s_r0 = Sem(ctx, nc, "s_r0")
        s_r1 = Sem(ctx, nc, "s_r1")
        s_w0 = Sem(ctx, nc, "s_w0")
        s_w1 = Sem(ctx, nc, "s_w1")
        s_q0 = Sem(ctx, nc, "s_q0")
        s_q1 = Sem(ctx, nc, "s_q1")
        s_kv = Sem(ctx, nc, "s_kv")
        s_ao = Sem(ctx, nc, "s_ao")
        s_bw = Sem(ctx, nc, "s_bw")
        s_aw = Sem(ctx, nc, "s_aw")
        s_cc = Sem(ctx, nc, "s_cc")
        s_pe = Sem(ctx, nc, "s_pe")
        s_ac = Sem(ctx, nc, "s_ac")
        s_dv = Sem(ctx, nc, "s_dv")
        s_gp = Sem(ctx, nc, "s_gp")
        sems = [s_pk, s_in, s_pq, s_vo, s_k1, s_b1, s_s0, s_s1, s_r0, s_r1, s_w0, s_w1, s_q0, s_q1,
                s_kv, s_ao, s_bw, s_aw, s_cc, s_pe, s_ac, s_dv, s_gp]
        sS = [s_s0, s_s1]
        sR = [s_r0, s_r1]
        sW = [s_w0, s_w1]
        sQ = [s_q0, s_q1]

        b = B(sems)
        M = b.M

        WGU_v = [t.ap().rearrange("(c q p) n -> p c q n", q=4, p=128) for t in WGU_t]
        WDN_v = [WDN_t[j].ap().rearrange("(c pl hc p) m -> p c pl hc m", pl=2, hc=8, p=128)
                 for j in range(2)]
        WATkc = WAT_t.ap().rearrange("(kc p) e -> p kc e", p=128)

        def gu_view(half):
            return strm[:, half * 8192:half * 8192 + 8192].rearrange(
                "p (c q n) -> p c q n", c=8, q=4)

        def dn_view(half):
            return strm[:, half * 8192:half * 8192 + 2 * AN * 128].rearrange(
                "p (pl m) -> p pl m", pl=2)

        def at_view(half):
            return strm[:, half * 8192:half * 8192 + 16 * 512].rearrange(
                "p (kc e) -> p kc e", kc=16)

        def inter_ap(g):
            return arena[:, g * TB:(g + 1) * TB]

        RG = [list(range(8))]

        def program():
            for i, val in enumerate([EXP_BIAS, LN_EPS, 1.0, 0.5, -2.0, -6.0]):
                b.op("gpsimd", lambda e, i=i, val=val: e.memset(cb[:, i:i + 1], val), s_gp)
            b.op("gpsimd", lambda e: e.memset(ones_col[:, :], 1.0), s_gp)
            b.op("gpsimd", lambda e: e.memset(ones_row[:, :], 1.0), s_gp)
            b.op("gpsimd", lambda e: e.memset(onesv[:, :], 1.0), s_gp)
            M["idz"] = b.op("gpsimd", lambda e: e.memset(idf[:, :], 0.0), s_gp)
            b.wait("gpsimd", s_gp, "idz")
            M["idsel"] = b.op("gpsimd", lambda e: e.affine_select(
                out=idf[:, :], in_=idf[:, :], compare_op=ALU.not_equal, fill=1.0,
                base=0, pattern=[[-1, 128]], channel_multiplier=1), s_gp)
            M["consts"] = s_gp.n
            for en in ("tensor", "vector", "scalar", "sync"):
                b.wait(en, s_gp, "consts")
            M["idbf"] = b.op("vector", lambda e: e.tensor_copy(out=idbf[:, :], in_=idf[:, :]), s_dv)

            b.op("sync", lambda e: e.dma_start(out=bqk_sb[:, :], in_=bqk), s_in, 16)
            b.op("sync", lambda e: e.dma_start(out=bo_sb[:, :], in_=bo), s_in, 16)
            M["bias_ld"] = s_in.n
            b.wait("vector", s_in, "bias_ld")
            M["bqs"] = b.op("vector", lambda e: e.tensor_scalar(
                out=bqk_sb[:, 0:16], in0=bqk_sb[:, 0:16], scalar1=QK_SCALE,
                scalar2=None, op0=ALU.mult), s_dv)

            blob_at = blob_at_t.ap()

            st = {"hist": [], "pkuse": [None, None], "ss": 0}

            def dq_pk_load(mat, bi_, pk_src, c0, w):
                if st["pkuse"][bi_] is not None:
                    b.wait("scalar", s_dv, st["pkuse"][bi_])
                M[f"pk_{mat}_{c0}"] = b.op("scalar", lambda e, pk_src=pk_src, bi_=bi_, c0=c0, w=w: e.dma_start(
                    out=pk_ap(bi_, w), in_=pk_src[:, c0:c0 + w]), [s_pk, s_pq][bi_], 16)

            def dq_sc_load(mat, sc_src, c0, w, first=False):
                if st["hist"]:
                    b.wait("scalar", s_gp, f"v4_{st['hist'][-1]}")
                elif first:
                    b.wait("scalar", s_dv, "bqs")
                M[f"sc_{mat}_{c0}"] = b.op("scalar", lambda e, sc_src=sc_src, c0=c0, w=w: e.dma_start(
                    out=dqs_ap(w), in_=sc_src[:, c0:c0 + w]), s_in, 16)

            def dq_chunk(mat, bi_, c0, w, out_fn):
                for pl in range(2):
                    tag = f"{mat}_{c0}_{pl}"
                    ss = st["ss"]
                    st["ss"] ^= 1
                    hist = st["hist"]
                    pt2 = hist[-2] if len(hist) >= 2 else None
                    scm = 1.0 if pl == 0 else (1.0 / 16.0)
                    sa = 0.5 if mat == "dn" else 1.0
                    mm_ = 7 if pl == 0 else 0x70
                    b.wait("vector", [s_pk, s_pq][bi_], f"pk_{mat}_{c0}")
                    if pt2 is not None:
                        b.wait("vector", s_ac, f"sgn_{pt2}")
                    if pl == 0:
                        M[f"nib_{tag}"] = b.op("vector", lambda e, bi_=bi_, w=w, ss=ss: (
                            e.tensor_scalar(out=dqu_ap(ss, 0, w), in0=pk_ap(bi_, w),
                                            scalar1=7, scalar2=None, op0=ALU.bitwise_and),
                            e.tensor_scalar(out=dqu_ap(ss, 1, w), in0=pk_ap(bi_, w),
                                            scalar1=8, scalar2=None, op0=ALU.bitwise_and))[-1], s_dv)
                    else:
                        M[f"nib_{tag}"] = b.op("vector", lambda e, bi_=bi_, w=w, ss=ss: (
                            e.tensor_scalar(out=dqu_ap(ss, 0, w), in0=pk_ap(bi_, w),
                                            scalar1=0x70, scalar2=None, op0=ALU.bitwise_and),
                            e.tensor_scalar(out=dqu_ap(ss, 1, w), in0=pk_ap(bi_, w),
                                            scalar1=128, scalar2=None, op0=ALU.is_ge))[-1], s_dv)
                    b.wait("scalar", s_dv, f"nib_{tag}")
                    b.wait("scalar", s_gp, "consts")
                    if pt2 is not None:
                        b.wait("scalar", s_gp, f"v4_{pt2}")
                    M[f"dqa_{tag}"] = b.op("scalar", lambda e, w=w, ss=ss, scm=scm: (
                        e.activation(out=dqf_ap(ss, 0, w), in_=dqu_ap(ss, 0, w), func=AF.Copy, scale=0.5 * scm),
                        e.activation(out=dqf_ap(ss, 1, w), in_=dqu_ap(ss, 0, w), func=AF.Relu, scale=0.5 * scm, bias=-2.0),
                        e.activation(out=dqf_ap(ss, 2, w), in_=dqu_ap(ss, 0, w), func=AF.Relu, scale=scm, bias=-6.0))[-1], s_ac)
                    b.wait("vector", s_ac, f"dqa_{tag}")
                    M[f"v1_{tag}"] = b.op("vector", lambda e, w=w, ss=ss: e.tensor_add(
                        out=dqf_ap(ss, 0, w), in0=dqf_ap(ss, 0, w), in1=dqf_ap(ss, 1, w)), s_dv)
                    sgs = (-0.25 * sa) if pl == 0 else (-2.0 * sa)
                    b.wait("scalar", s_dv, f"v1_{tag}")
                    M[f"sgn_{tag}"] = b.op("scalar", lambda e, w=w, ss=ss, sgs=sgs, sa=sa: e.activation(
                        out=dqf_ap(ss, 1, w), in_=dqu_ap(ss, 1, w), func=AF.Copy,
                        scale=sgs, bias=sa), s_ac)
                    b.wait("vector", s_dv, f"v1_{tag}")
                    M[f"v2_{tag}"] = b.op("vector", lambda e, w=w, ss=ss: e.tensor_add(
                        out=dqf_ap(ss, 0, w), in0=dqf_ap(ss, 0, w), in1=dqf_ap(ss, 2, w)), s_dv)
                    b.wait("gpsimd", s_dv, f"v2_{tag}")
                    b.wait("gpsimd", s_ac, f"sgn_{tag}")
                    M[f"v3_{tag}"] = b.op("gpsimd", lambda e, w=w, ss=ss: e.tensor_mul(
                        out=dqf_ap(ss, 0, w), in0=dqf_ap(ss, 0, w), in1=dqf_ap(ss, 1, w)), s_gp)
                    b.wait("gpsimd", s_gp, f"v3_{tag}")
                    b.wait("gpsimd", s_in, f"sc_{mat}_{c0}")
                    if pt2 is not None:
                        b.wait("gpsimd", [s_bw, s_b1][ss], f"dq_wr_{pt2}")
                    M[f"v4_{tag}"] = b.op("gpsimd", lambda e, w=w, ss=ss: e.tensor_mul(
                        out=dqo_ap(ss, w), in0=dqf_ap(ss, 0, w), in1=dqs_ap(w)), s_gp)
                    b.wait("gpsimd", s_gp, f"v4_{tag}")
                    fns = out_fn(pl, c0, w, ss)
                    if not isinstance(fns, list):
                        fns = [fns]
                    for f_ in fns:
                        M[f"dq_wr_{tag}"] = b.op("gpsimd", f_, [s_bw, s_b1][ss], 16)
                    hist.append(tag)
                st["pkuse"][bi_] = f"nib_{mat}_{c0}_1"

            def gu_out(ch, rowbase):
                def f(pl, c0, w, ss):
                    return lambda e: e.dma_start(
                        out=blob_gu_t[ch].ap()[rowbase + 128 * pl: rowbase + 128 * pl + 128, :],
                        in_=dqo_ap(ss, w))
                return f

            def dn_out(pl, c0, w, ss):
                a0, na = c0 // D, w // D
                dviews = [blob_dn_t[j].ap().rearrange("(pl hc p) m -> pl hc p m", pl=2, hc=8)
                          for j in range(2)]
                return [
                    (lambda e, ai=ai, j=j: e.dma_start(
                        out=dviews[j][pl, :, :, (a0 + ai) * 128:(a0 + ai + 1) * 128]
                            .rearrange("hc p hh -> p hc hh"),
                        in_=dqo_ap(ss, w).rearrange("p (a hc hh) -> p a hc hh", a=na, hc=16)[:, ai, 8 * j:8 * j + 8, :]))
                    for ai in range(na) for j in range(2)]

            def dec_gu(ch):
                for mi, (mat, pk_src, sc_src, rowb) in enumerate(
                        [("g", gpk, gsc, 0), ("u", upk, usc, 256)]):
                    c0 = ch * NCHC
                    dq_pk_load(mat, mi, pk_src, c0, NCHC)
                    dq_sc_load(mat, sc_src, c0, NCHC, first=(ch == 0 and mi == 0))
                    dq_chunk(mat, mi, c0, NCHC, gu_out(ch, rowb))
                M[f"gu_dec0_{ch}"] = s_bw.n
                M[f"gu_dec1_{ch}"] = s_b1.n
                b.wait("gpsimd", s_bw, f"gu_dec0_{ch}")
                b.wait("gpsimd", s_b1, f"gu_dec1_{ch}")
                M[f"cc_gu_{ch}"] = b.op("gpsimd", lambda e, ch=ch: e.collective_compute(
                    "AllGather", ALU.bypass, replica_groups=RG,
                    ins=[blob_gu_t[ch].ap()], outs=[WGU_t[ch].ap()]), s_cc)

            ND = (AN * D + NCH - 1) // NCH

            def dec_dn(ch):
                c0, w = ch * NCH, min(NCH, AN * D - ch * NCH)
                dq_pk_load("dn", ch % 2, dpk, c0, w)
                dq_sc_load("dn", dsc, c0, w)
                dq_chunk("dn", ch % 2, c0, w, dn_out)
                if ch == ND - 1:
                    M["dn_dec0"] = s_bw.n
                    M["dn_dec1"] = s_b1.n
                    b.wait("gpsimd", s_bw, "dn_dec0")
                    b.wait("gpsimd", s_b1, "dn_dec1")
                    for j in range(2):
                        M[f"cc_dn_{j}"] = b.op("gpsimd", lambda e, j=j: e.collective_compute(
                            "AllGather", ALU.bypass, replica_groups=RG,
                            ins=[blob_dn_t[j].ap()], outs=[WDN_t[j].ap()]), s_cc)

            def dec_aw():
                for hl in range(2):
                    if hl == 0:
                        b.wait("gpsimd", s_dv, st["pkuse"][0])
                        b.wait("gpsimd", s_dv, st["pkuse"][1])
                        M["aw_pre0"] = s_bw.n
                        M["aw_pre1"] = s_b1.n
                        b.wait("gpsimd", s_bw, "aw_pre0")
                        b.wait("gpsimd", s_b1, "aw_pre1")
                    M[f"aw_ld{hl}"] = b.op("gpsimd", lambda e, hl=hl: e.dma_start(
                        out=awb_ap(), in_=aw[128 * hl:128 * hl + 128, :]), s_aw, 16)
                    b.wait("gpsimd", s_aw, f"aw_ld{hl}")
                    M[f"aw_wr{hl}"] = b.op("gpsimd", lambda e, hl=hl: e.dma_start(
                        out=blob_at[128 * hl:128 * hl + 128, :], in_=awb_ap()), s_bw, 16)
                    b.wait("gpsimd", s_bw, f"aw_wr{hl}")
                M["cc_at"] = b.op("gpsimd", lambda e: e.collective_compute(
                    "AllGather", ALU.bypass, replica_groups=RG,
                    ins=[blob_at_t.ap()], outs=[WAT_t.ap()]), s_cc)

            def ffn0_hook(sp):
                if DEC_INLINE:
                    return
                ch = sp // 2
                if sp % 2 == 0 and ch < ND:
                    dec_dn(ch)
                elif sp == 2 * ND:
                    dec_aw()

            def emit_phase0():
                emit_R_loads(0, xb[0], [[], []])
                layer_norm(0, None, None)
                for ch in range(NCHK):
                    dec_gu(ch)
                if DEC_INLINE:
                    for ch in range(ND):
                        dec_dn(ch)
                    dec_aw()
                M["dec_gp"] = s_gp.n
                M["dec_bw"] = s_bw.n
                M["dec_b1"] = s_b1.n

            def phase_barrier(ph):
                if ph == 0:
                    return
                M[f"bar_dv_{ph}"] = s_dv.n
                M[f"bar_ac_{ph}"] = s_ac.n
                b.wait("tensor", s_dv, f"bar_dv_{ph}")
                b.wait("tensor", s_ac, f"bar_ac_{ph}")

            def emit_R_loads(ph, src, prev_keys):
                for idx in range(32):
                    j = idx % 16
                    par = idx % 2
                    if idx >= 2:
                        if idx - 2 < 16:
                            b.wait("sync", s_pe, f"sx2_{ph}_{idx - 2}")
                            b.wait("sync", s_ac, f"sq_{ph}_{idx - 2}")
                        else:
                            b.wait("sync", s_dv, f"nm1_{ph}_{idx - 18}")
                    else:
                        for (sem, key) in prev_keys[idx]:
                            b.wait("sync", sem, key)
                    M[f"r_ld_{ph}_{idx}"] = b.op("sync", lambda e, j=j, par=par, src=src: e.dma_start(
                        out=rbuf[par][:, :], in_=src[128 * j:128 * j + 128, :]), sR[par], 16)

            def emit_res_loads(ph, src, prev_keys):
                for hc in range(16):
                    par = hc % 2
                    if hc >= 2:
                        b.wait("sync", s_dv, f"resfree_{ph}_{hc - 2}")
                    else:
                        for (sem, key) in prev_keys[hc]:
                            b.wait("sync", sem, key)
                    M[f"res_ld_{ph}_{hc}"] = b.op("sync", lambda e, hc=hc, par=par, src=src: e.dma_start(
                        out=rbuf[par][:, :], in_=src[128 * hc:128 * hc + 128, :]), sR[par], 16)

            def layer_norm(ph, prev_xn_key, prev_lnt_keys):
                phase_barrier(ph)
                for j in range(16):
                    par = j % 2
                    b.wait("scalar", sR[par], f"r_ld_{ph}_{j}")
                    if j >= 1:
                        b.wait("scalar", s_pe, f"sx2_{ph}_{j - 1}")
                    M[f"sq_{ph}_{j}"] = b.op("scalar", lambda e, par=par: e.activation(
                        out=sqb[:, :], in_=rbuf[par][:, :], func=AF.Square), s_ac)
                    b.wait("tensor", sR[par], f"r_ld_{ph}_{j}")
                    M[f"sx_{ph}_{j}"] = b.op("tensor", lambda e, j=j, par=par: e.matmul(
                        pst1[0:1, 0:TB], ones_col[:, :], rbuf[par][:, :],
                        start=(j == 0), stop=(j == 15), skip_group_check=True), s_pe)
                    b.wait("tensor", s_ac, f"sq_{ph}_{j}")
                    M[f"sx2_{ph}_{j}"] = b.op("tensor", lambda e, j=j: e.matmul(
                        pst2[0:1, 0:TB], ones_col[:, :], sqb[:, :],
                        start=(j == 0), stop=(j == 15), skip_group_check=True), s_pe)
                st_ = stat
                b.wait("vector", s_pe, f"sx2_{ph}_15")
                M[f"st1_{ph}"] = b.op("vector", lambda e: e.tensor_scalar(
                    out=st_[:, 0:TB], in0=pst1[0:1, 0:TB], scalar1=1.0 / D,
                    scalar2=None, op0=ALU.mult), s_dv)
                M[f"st2_{ph}"] = b.op("vector", lambda e: e.tensor_scalar(
                    out=st_[:, TB:2 * TB], in0=pst2[0:1, 0:TB], scalar1=1.0 / D,
                    scalar2=None, op0=ALU.mult), s_dv)
                b.wait("vector", s_dv, f"st2_{ph}")
                M[f"st3_{ph}"] = b.op("vector", lambda e: e.tensor_mul(
                    out=st_[:, 2 * TB:3 * TB], in0=st_[:, 0:TB], in1=st_[:, 0:TB]), s_dv)
                b.wait("vector", s_dv, f"st3_{ph}")
                M[f"st4_{ph}"] = b.op("vector", lambda e: e.tensor_sub(
                    out=st_[:, TB:2 * TB], in0=st_[:, TB:2 * TB], in1=st_[:, 2 * TB:3 * TB]), s_dv)
                b.wait("scalar", s_dv, f"st4_{ph}")
                M[f"st5_{ph}"] = b.op("scalar", lambda e: e.activation(
                    out=st_[:, 2 * TB:3 * TB], in_=st_[:, TB:2 * TB], func=AF.Ln,
                    bias=LN_EPS), s_ac)
                b.wait("scalar", s_ac, f"st5_{ph}")
                M[f"st6_{ph}"] = b.op("scalar", lambda e: e.activation(
                    out=st_[:, 3 * TB:4 * TB], in_=st_[:, 2 * TB:3 * TB], func=AF.Exp,
                    scale=-0.5), s_ac)
                b.wait("vector", s_ac, f"st6_{ph}")
                M[f"st7_{ph}"] = b.op("vector", lambda e: e.tensor_mul(
                    out=st_[:, 0:TB], in0=st_[:, 0:TB], in1=st_[:, 3 * TB:4 * TB]), s_dv)
                b.wait("vector", s_dv, f"st7_{ph}")
                M[f"st8_{ph}"] = b.op("vector", lambda e: e.tensor_scalar(
                    out=st_[:, 0:TB], in0=st_[:, 0:TB], scalar1=-1.0,
                    scalar2=None, op0=ALU.mult), s_dv)
                b.wait("tensor", s_ac, f"st6_{ph}")
                b.wait("tensor", s_dv, f"st8_{ph}")
                M[f"rep1_{ph}"] = b.op("tensor", lambda e: e.matmul(
                    psA[:, 0:TB], ones_row[:, :], stat[0:1, 3 * TB:4 * TB],
                    start=True, stop=True, skip_group_check=True), s_pe)
                M[f"rep2_{ph}"] = b.op("tensor", lambda e: e.matmul(
                    psB[:, 0:TB], ones_row[:, :], stat[0:1, 0:TB],
                    start=True, stop=True, skip_group_check=True), s_pe)
                b.wait("vector", s_pe, f"rep2_{ph}")
                M[f"arep_{ph}"] = b.op("vector", lambda e: e.tensor_copy(
                    out=arep[:, :], in_=psA[:, 0:TB]), s_dv)
                b.wait("vector", s_dv, f"arep_{ph}")
                M[f"brepc_{ph}"] = b.op("vector", lambda e: e.tensor_copy(
                    out=brep[:, :], in_=psB[:, 0:TB]), s_dv)
                for j in range(16):
                    jj = 16 + j
                    par = jj % 2
                    b.wait("vector", sR[par], f"r_ld_{ph}_{jj}")
                    b.wait("vector", s_dv, f"brepc_{ph}")
                    if j < 2 and prev_lnt_keys is not None:
                        for (sem, key) in prev_lnt_keys[j]:
                            b.wait("vector", sem, key)
                    if j >= 2:
                        b.wait("vector", s_dv, f"nm2_{ph}_{j - 2}")
                    M[f"nm1_{ph}_{j}"] = b.op("vector", lambda e, j=j, par=par: e.tensor_mul(
                        out=lnt[j % 2][:, :], in0=rbuf[par][:, :], in1=arep[:, :]), s_dv)
                    b.wait("vector", s_dv, f"nm1_{ph}_{j}")
                    if j == 0 and prev_xn_key is not None:
                        b.wait("vector", s_pe, prev_xn_key)
                    M[f"nm2_{ph}_{j}"] = b.op("vector", lambda e, j=j: e.tensor_add(
                        out=x_n[:, j * TB:(j + 1) * TB], in0=lnt[j % 2][:, :],
                        in1=brep[:, :]), s_dv)
                M[f"xn_{ph}"] = s_dv.n

            def ffn(ph, src, dst, prev_gu, prev_dn, prev_xn, prev_r, prev_wr, hook=None,
                    pre_ln=False):
                if not pre_ln:
                    emit_R_loads(ph, src, prev_r)
                    layer_norm(ph, prev_xn, None)
                spc = NCHC // 256
                for sp in range(NSP):
                    if hook is not None:
                        hook(sp)
                    half = sp % 2
                    kch = sp // spc
                    if sp >= 2:
                        b.wait("sync", s_pe, f"gu_fin_{ph}_{sp - 2}")
                    else:
                        for (sem, key) in prev_gu[sp]:
                            b.wait("sync", sem, key)
                    if ph == 0 and (sp % spc) < 2:
                        b.wait("sync", s_cc, f"cc_gu_{kch}")
                    for ph_ in range(2):
                        M[f"gu_ld_{ph}_{sp}"] = b.op("sync", lambda e, sp=sp, half=half, kch=kch, ph_=ph_: e.dma_start(
                            out=gu_view(half)[:, 4 * ph_:4 * ph_ + 4, :, :],
                            in_=WGU_v[kch][:, 4 * ph_:4 * ph_ + 4, :,
                                           256 * sp - kch * NCHC: 256 * sp - kch * NCHC + 256]), sS[half], 16)
                    b.wait("tensor", sS[half], f"gu_ld_{ph}_{sp}")
                    b.wait("tensor", s_dv, f"xn_{ph}")
                    for par2 in range(2):
                        g = 2 * sp + par2
                        pg, pu = (psA, psB) if par2 == 0 else (psE, psF)
                        if g >= 2:
                            b.wait("tensor", s_ac, f"gel_{ph}_{g - 2}")
                            b.wait("tensor", s_dv, f"inter_{ph}_{g - 2}")
                        else:
                            b.wait("tensor", s_dv, f"arep_{ph}")
                            b.wait("tensor", s_dv, f"brepc_{ph}")
                        for kc in range(16):
                            c0, pl = kc // 2, kc % 2
                            r = b.op("tensor", lambda e, half=half, c0=c0, pl=pl, par2=par2, kc=kc, pg=pg: e.matmul(
                                pg[:, 0:TB], gu_view(half)[:, c0, pl, par2:256:2],
                                x_n[:, (2 * c0 + pl) * TB:(2 * c0 + pl + 1) * TB],
                                start=(kc == 0), stop=(kc == 15), skip_group_check=True),
                                s_pe if kc == 15 else None)
                        M[f"gmm_{ph}_{g}"] = r
                        for kc in range(16):
                            c0, pl = kc // 2, kc % 2
                            r = b.op("tensor", lambda e, half=half, c0=c0, pl=pl, par2=par2, kc=kc, pu=pu: e.matmul(
                                pu[:, 0:TB], gu_view(half)[:, c0, 2 + pl, par2:256:2],
                                x_n[:, (2 * c0 + pl) * TB:(2 * c0 + pl + 1) * TB],
                                start=(kc == 0), stop=(kc == 15), skip_group_check=True),
                                s_pe if kc == 15 else None)
                        M[f"umm_{ph}_{g}"] = r
                        b.wait("scalar", s_pe, f"gmm_{ph}_{g}")
                        if g >= 2:
                            b.wait("scalar", s_dv, f"inter_{ph}_{g - 2}")
                        M[f"gel_{ph}_{g}"] = b.op("scalar", lambda e, g=g, pg=pg: e.activation(
                            out=gelu_b[g % 2][:, :], in_=pg[:, 0:TB],
                            func=AF.Gelu_apprx_tanh), s_ac)
                        b.wait("vector", s_ac, f"gel_{ph}_{g}")
                        b.wait("vector", s_pe, f"umm_{ph}_{g}")
                        if ph == 0 and g == 0:
                            b.wait("vector", s_gp, "dec_gp")
                            b.wait("vector", s_bw, "dec_bw")
                            b.wait("vector", s_b1, "dec_b1")
                        M[f"inter_{ph}_{g}"] = b.op("vector", lambda e, g=g, pu=pu: e.tensor_tensor(
                            out=inter_ap(g), in0=pu[:, 0:TB], in1=gelu_b[g % 2][:, :],
                            op=ALU.mult), s_dv)
                    M[f"gu_fin_{ph}_{sp}"] = M[f"umm_{ph}_{2 * sp + 1}"]
                for hc in range(16):
                    par = hc % 2
                    if hc >= 2:
                        b.wait("sync", s_dv, f"resfree_{ph}_{hc - 2}")
                    else:
                        b.wait("sync", s_dv, f"nm1_{ph}_{14 + hc}")
                    M[f"res_ld_{ph}_{hc}"] = b.op("sync", lambda e, hc=hc, par=par, src=src: e.dma_start(
                        out=rbuf[par][:, :], in_=src[128 * hc:128 * hc + 128, :]), sR[par], 16)
                    pdn = psA if hc % 2 == 0 else psE
                    for dsp in range(8):
                        idx = hc * 8 + dsp
                        half = idx % 2
                        if idx >= 2:
                            b.wait("sync", s_pe, f"dn_fin_{ph}_{idx - 2}")
                        else:
                            for (sem, key) in prev_dn[idx]:
                                b.wait("sync", sem, key)
                        if ph == 0 and hc == 8 and dsp < 2:
                            b.wait("sync", s_cc, "cc_dn_1")
                        M[f"dn_ld_{ph}_{idx}"] = b.op("sync", lambda e, hc=hc, dsp=dsp, half=half: e.dma_start(
                            out=dn_view(half), in_=WDN_v[hc // 8][:, dsp, :, hc % 8, :]), sS[half], 16)
                    if hc == 0:
                        b.wait("tensor", s_dv, f"inter_{ph}_{NG - 1}")
                        b.wait("tensor", s_dv, f"inter_{ph}_{NG - 2}")
                        b.wait("tensor", s_ac, f"gel_{ph}_{NG - 1}")
                        b.wait("tensor", s_ac, f"gel_{ph}_{NG - 2}")
                    if hc >= 2:
                        b.wait("tensor", s_dv, f"resadd_{ph}_{hc - 2}")
                    first = True
                    for dsp in range(8):
                        idx = hc * 8 + dsp
                        half = idx % 2
                        b.wait("tensor", sS[half], f"dn_ld_{ph}_{idx}")
                        for pl in range(2):
                            for a in range(AN):
                                g = 2 * (dsp * AN + a) + pl
                                stop_ = (dsp == 7 and pl == 1 and a == AN - 1)
                                mark = (pl == 1 and a == AN - 1)
                                r = b.op("tensor", lambda e, half=half, pl=pl, a=a, g=g, pdn=pdn, first=first, stop_=stop_: e.matmul(
                                    pdn[:, 0:TB], dn_view(half)[:, pl, a * 128:(a + 1) * 128],
                                    inter_ap(g), start=first, stop=stop_,
                                    skip_group_check=True), s_pe if mark else None)
                                first = False
                        M[f"dn_fin_{ph}_{idx}"] = r
                    M[f"dn_mm_{ph}_{hc}"] = r
                    par = hc % 2
                    b.wait("vector", s_pe, f"dn_mm_{ph}_{hc}")
                    b.wait("vector", sR[par], f"res_ld_{ph}_{hc}")
                    if hc >= 2:
                        b.wait("vector", sW[par], f"rwr_{ph}_{hc - 2}")
                    else:
                        for (sem, key) in prev_wr[hc]:
                            b.wait("vector", sem, key)
                    M[f"resadd_{ph}_{hc}"] = b.op("vector", lambda e, par=par, pdn=pdn: e.tensor_add(
                        out=outb[par][:, :], in0=pdn[:, 0:TB], in1=rbuf[par][:, :]), s_dv)
                    M[f"resfree_{ph}_{hc}"] = M[f"resadd_{ph}_{hc}"]
                    b.wait("gpsimd", s_dv, f"resadd_{ph}_{hc}")
                    M[f"rwr_{ph}_{hc}"] = b.op("gpsimd", lambda e, hc=hc, par=par, dst=dst: e.dma_start(
                        out=dst[128 * hc:128 * hc + 128, :], in_=outb[par][:, :]), sW[par], 16)
                M[f"ffn_wr_tot0_{ph}"] = s_w0.n
                M[f"ffn_wr_tot1_{ph}"] = s_w1.n

            R1 = R1_t.ap()
            R2 = R2_t.ap()

            emit_phase0()
            ffn(0, xb[0], R1[0],
                prev_gu=[[], []],
                prev_dn=[[(s_cc, "cc_dn_0"), (s_pe, "gu_fin_0_%d" % (NSP - 2))],
                         [(s_cc, "cc_dn_0"), (s_pe, "gu_fin_0_%d" % (NSP - 1))]],
                prev_xn=None,
                prev_r=[[], []],
                prev_wr=[[], []],
                hook=ffn0_hook,
                pre_ln=True)
            ffn(1, xb[1], R1[1],
                prev_gu=[[(s_pe, "dn_fin_0_126")], [(s_pe, "dn_fin_0_127")]],
                prev_dn=[[(s_pe, "gu_fin_1_%d" % (NSP - 2))],
                         [(s_pe, "gu_fin_1_%d" % (NSP - 1))]],
                prev_xn="gu_fin_0_%d" % (NSP - 1),
                prev_r=[[(s_dv, "resadd_0_14")], [(s_dv, "resadd_0_15")]],
                prev_wr=[[(s_w0, "rwr_0_14")], [(s_w1, "rwr_0_15")]])

            for bi in range(2):
                ph = 10 + bi
                b.wait("sync", sW[0], f"ffn_wr_tot0_{bi}")
                b.wait("sync", sW[1], f"ffn_wr_tot1_{bi}")
                if bi == 0:
                    b.wait("sync", s_dv, "resadd_1_15")
                    b.wait("sync", s_pe, "dn_mm_1_15")
                    M["bvr_ld"] = b.op("sync", lambda e: e.dma_start(
                        out=bv_row_ap(), in_=bvr), s_in, 16)
                    b.wait("tensor", s_in, "bvr_ld")
                    for i in range(D // 512):
                        pg2 = psA if i % 2 == 0 else psB
                        if i >= 2:
                            b.wait("tensor", s_dv, f"bvcp{i-2}")
                        M[f"bvrep{i}"] = b.op("tensor", lambda e, i=i, pg2=pg2: e.matmul(
                            pg2[:, :], ones_row[:, :], bv_row_ap()[:, 512 * i:512 * i + 512],
                            start=True, stop=True, skip_group_check=True), s_pe)
                        b.wait("vector", s_pe, f"bvrep{i}")
                        M[f"bvcp{i}"] = b.op("vector", lambda e, i=i, pg2=pg2: e.tensor_copy(
                            out=bv_rep_ap()[:, 512 * i:512 * i + 512], in_=pg2[:, :]), s_dv)
                    M["bv_done"] = s_dv.n
                emit_R_loads(ph, R1[bi],
                             [[(s_dv, "resadd_1_14")], [(s_dv, "resadd_1_15")]] if bi == 0
                             else [[(s_dv, "nm1_10_14")], [(s_dv, "nm1_10_15")]])
                layer_norm(ph, prev_xn_key=("gu_fin_1_%d" % (NSP - 1)) if bi == 0 else "aw_fin_0_11",
                           prev_lnt_keys=None)
                for eb in range(12):
                    half = eb % 2
                    if eb >= 2:
                        b.wait("sync", s_pe, f"aw_fin_{bi}_{eb - 2}")
                    elif bi == 0:
                        b.wait("sync", s_cc, "cc_at")
                        b.wait("sync", s_pe, f"dn_fin_1_{126 + eb}")
                    else:
                        b.wait("sync", s_pe, f"aw_fin_0_{10 + eb}")
                    M[f"aw_ld_{bi}_{eb}"] = b.op("sync", lambda e, eb=eb, half=half: e.dma_start(
                        out=at_view(half), in_=WATkc[:, :, 512 * eb:512 * eb + 512]), sS[half], 16)
                    b.wait("tensor", sS[half], f"aw_ld_{bi}_{eb}")
                    b.wait("tensor", s_dv, f"xn_{ph}")
                    if eb < 8:
                        for sub in range(4):
                            ec = eb * 4 + sub
                            pg = psA if ec % 2 == 0 else psB
                            if ec >= 2:
                                b.wait("tensor", s_dv, f"qkev_{bi}_{ec - 2}")
                            else:
                                b.wait("tensor", s_dv, f"arep_{ph}")
                                b.wait("tensor", s_dv, f"brepc_{ph}")
                            for kc in range(16):
                                r = b.op("tensor", lambda e, half=half, kc=kc, sub=sub, pg=pg: e.matmul(
                                    pg[:, 0:TB], at_view(half)[:, kc, sub * 128:(sub + 1) * 128],
                                    x_n[:, kc * TB:(kc + 1) * TB],
                                    start=(kc == 0), stop=(kc == 15), skip_group_check=True),
                                    s_pe if kc == 15 else None)
                            M[f"qkmm_{bi}_{ec}"] = r
                            b.wait("vector", s_pe, f"qkmm_{bi}_{ec}")
                            if ec >= 2:
                                b.wait("vector", sQ[ec % 2], f"qkwr_{bi}_{ec - 2}")
                            elif bi == 1:
                                b.wait("vector", sQ[ec % 2], f"vwr_0_{4 * NTQ - 2 + ec}")
                            isq = ec < 16
                            M[f"qkev_{bi}_{ec}"] = b.op("vector", lambda e, ec=ec, pg=pg, isq=isq: e.tensor_scalar(
                                out=evb[ec % 2][:, 0:TB], in0=pg[:, 0:TB],
                                scalar1=QK_SCALE if isq else 1.0,
                                scalar2=bqk_sb[:, ec:ec + 1],
                                op0=ALU.mult, op1=ALU.add), s_dv)
                            row = (0 if isq else D) + (ec % 16) * 128
                            b.wait("sync", s_dv, f"qkev_{bi}_{ec}")
                            M[f"qkwr_{bi}_{ec}"] = b.op("sync", lambda e, bi=bi, row=row, ec=ec: e.dma_start(
                                out=QKIN[bi].ap()[row:row + 128, :], in_=evb[ec % 2][:, 0:TB]), sQ[ec % 2], 16)
                        M[f"aw_fin_{bi}_{eb}"] = M[f"qkmm_{bi}_{eb * 4 + 3}"]
                    else:
                        evs = eb - 8
                        for ts in range(NTQ):
                            vi = evs * NTQ + ts
                            pg = psA if vi % 2 == 0 else psB
                            if vi >= 2:
                                b.wait("tensor", s_dv, f"vev_{bi}_{vi - 2}")
                            elif evs == 0:
                                b.wait("tensor", s_dv, f"qkev_{bi}_30")
                                b.wait("tensor", s_dv, f"qkev_{bi}_31")
                            for kc in range(16):
                                r = b.op("tensor", lambda e, half=half, kc=kc, ts=ts, pg=pg: e.matmul(
                                    pg[0:TQ, 0:512],
                                    x_n[:, kc * TB + ts * TQ: kc * TB + (ts + 1) * TQ],
                                    at_view(half)[:, kc, :],
                                    start=(kc == 0), stop=(kc == 15), skip_group_check=True),
                                    s_pe if kc == 15 else None)
                            M[f"vmm_{bi}_{vi}"] = r
                            b.wait("vector", s_pe, f"vmm_{bi}_{vi}")
                            if vi >= 2:
                                b.wait("vector", sQ[vi % 2], f"vwr_{bi}_{vi - 2}")
                            else:
                                b.wait("vector", sQ[vi % 2], f"qkwr_{bi}_{30 + vi}")
                            M[f"vev_{bi}_{vi}"] = b.op("vector", lambda e, vi=vi, evs=evs, pg=pg: e.tensor_tensor(
                                out=evb[vi % 2][0:TQ, 0:512], in0=pg[0:TQ, 0:512],
                                in1=bv_rep_ap()[0:TQ, 512 * evs:512 * evs + 512], op=ALU.add), s_dv)
                            b.wait("sync", s_dv, f"vev_{bi}_{vi}")
                            M[f"vwr_{bi}_{vi}"] = b.op("sync", lambda e, bi=bi, vi=vi, evs=evs, ts=ts: e.dma_start(
                                out=VIN[bi].ap().rearrange("t (h s) -> t h s", s=VS)[
                                    ts * TQ:(ts + 1) * TQ, 4 * evs:4 * evs + 4, 0:HD],
                                in_=evb[vi % 2][0:TQ, 0:512].rearrange("t (h s) -> t h s", s=HD)), sQ[vi % 2], 16)
                        M[f"aw_fin_{bi}_{eb}"] = M[f"vmm_{bi}_{(eb - 8 + 1) * NTQ - 1}"]
                if bi == 1:
                    b.wait("sync", s_cc, "cc_v0")
                for ts in range(NTQ):
                    M[f"vones_{bi}"] = b.op("sync", lambda e, bi=bi, ts=ts: e.dma_start(
                        out=VIN[bi].ap().rearrange("(s p) (h v) -> p s h v", p=TQ, v=VS)[:, ts, :, HD:VS],
                        in_=onesv[0:TQ, 0:NH].unsqueeze(2),
                        allow_slow_non_contiguous=True), s_vo, 16)
                M[f"q0_tot_{bi}"] = sQ[0].n
                M[f"q1_tot_{bi}"] = sQ[1].n
                b.wait("gpsimd", sQ[0], f"q0_tot_{bi}")
                b.wait("gpsimd", sQ[1], f"q1_tot_{bi}")
                b.wait("gpsimd", s_vo, f"vones_{bi}")
                M[f"cc_qk{bi}"] = b.op("gpsimd", lambda e, bi=bi: e.collective_compute(
                    "AllGather", ALU.bypass, replica_groups=RG,
                    ins=[QKIN[bi].ap()], outs=[QKOUT[bi].ap()]), s_cc)
                M[f"cc_v{bi}"] = b.op("gpsimd", lambda e, bi=bi: e.collective_compute(
                    "AllGather", ALU.bypass, replica_groups=RG,
                    ins=[VIN[bi].ap()], outs=[VOUT[bi].ap()]), s_cc)

            def kv_rank(gb):
                return gb if gb < 8 else 15 - gb

            def slot(gb):
                return (0 if gb < 8 else 8) + kv_rank(gb)

            def kj_off(j, gb):
                return off_k + (j * NBLK + slot(gb)) * TB

            def vj_off(j, gb):
                return off_v + (j * NBLK + slot(gb)) * NTQ * VS

            def qj_off(j, gb):
                return off_q + (j * NBLK + slot(gb)) * TB

            for hf in range(2):
                skv_h = s_kv if hf == 0 else s_k1
                b.wait("sync", s_cc, "cc_v0" if hf == 0 else "cc_v1")
                for j in range(2):
                    b.op("sync", lambda e, hf=hf, j=j: e.dma_start(
                        out=AR(off_k + (j * NBLK + hf * 8) * TB, 8 * TB)
                            .rearrange("p (r t) -> p r t", r=8),
                        in_=bass.AP(QKOUT[hf],
                                    b.dynoff(e, 256 * TB, D * TB + (128 * TB) * j),
                                    [[TB, HD], [2 * D * TB, 8], [1, TB]])), skv_h, 16)
                    b.op("sync", lambda e, hf=hf, j=j: e.dma_start(
                        out=AR(off_v + (j * NBLK + hf * 8) * NTQ * VS, 8 * NTQ * VS)
                            .rearrange("p (rs v) -> p rs v", v=VS)[0:TQ],
                        in_=bass.AP(VOUT[hf],
                                    b.dynoff(e, 2 * VS, VS * j),
                                    [[NH * VS, TQ], [TQ * NH * VS, 8 * NTQ], [1, VS]])), skv_h, 16)
                for j in range(2):
                    b.op("sync", lambda e, hf=hf, j=j: e.dma_start(
                        out=AR(off_q + (j * NBLK + hf * 8) * TB, 8 * TB)
                            .rearrange("p (r t) -> p r t", r=8)[0:HD],
                        in_=bass.AP(QKOUT[hf],
                                    b.dynoff(e, 256 * TB, (128 * TB) * j),
                                    [[TB, HD], [2 * D * TB, 8], [1, TB]])), skv_h, 16)
                if hf == 0:
                    M["kv_half0"] = s_kv.n
            M["kv_all"] = s_k1.n

            phase_barrier(99)
            b.wait("gpsimd", s_dv, "resadd_1_15")
            b.wait("gpsimd", s_pe, "dn_mm_1_15")
            M["mz"] = b.op("gpsimd", lambda e: e.memset(maskT_ap(), 0.0), s_gp)
            b.wait("gpsimd", s_gp, "mz")
            for tkc in range(NTQ):
                M["mbuild"] = b.op("gpsimd", lambda e, tkc=tkc: e.affine_select(
                    out=maskT_ap()[:, tkc * TB:(tkc + 1) * TB],
                    in_=maskT_ap()[:, tkc * TB:(tkc + 1) * TB],
                    compare_op=ALU.is_ge, fill=MASKVAL, base=-tkc * TQ,
                    pattern=[[1, TB]], channel_multiplier=-1), s_gp)
            b.wait("tensor", s_kv, "kv_half0")
            for gb in range(NBLK):
                if gb == 8:
                    b.wait("tensor", s_k1, "kv_all")
                for j in range(2):
                    nch_ = (gb + 1) * NTQ
                    ci = 0
                    if gb + j > 0:
                        pj, pgb = (0, gb) if j == 1 else (1, gb - 1)
                        for tqs in range(NTQ):
                            b.wait("tensor", s_dv, f"aofin_{pgb}_{pj}_{tqs}")
                    for kv in range(gb + 1):
                        for tkc in range(NTQ):
                            pss = psA if ci % 2 == 0 else psB
                            half = ci % 2
                            if ci >= 2:
                                b.wait("tensor", s_ac, f"exp_{gb}_{j}_{ci - 2}")
                            elif gb + j > 0:
                                pj2, pgb2 = (0, gb) if j == 1 else (1, gb - 1)
                                pn2 = (pgb2 + 1) * NTQ
                                li = pn2 - 1 - ((pn2 - 1 - ci) % 2)
                                if li >= 0:
                                    b.wait("tensor", s_ac, f"exp_{pgb2}_{pj2}_{li}")
                            M[f"smm_{gb}_{j}_{ci}"] = b.op("tensor", lambda e, j=j, kv=kv, tkc=tkc, gb=gb, pss=pss: e.matmul(
                                pss[0:TQ, 0:TB],
                                arena[0:HD, kj_off(j, kv) + tkc * TQ: kj_off(j, kv) + (tkc + 1) * TQ],
                                AR(qj_off(j, gb), TB)[0:HD],
                                start=True, stop=True, skip_group_check=True), s_pe)
                            if ci >= 2:
                                b.wait("scalar", s_pe, f"pv_{gb}_{j}_{ci - 2}")
                            elif gb + j > 0:
                                pj2, pgb2 = (0, gb) if j == 1 else (1, gb - 1)
                                b.wait("scalar", s_pe, f"pvmm_{pgb2}_{pj2}")
                            if kv == gb:
                                b.wait("vector", s_pe, f"smm_{gb}_{j}_{ci}")
                                b.wait("vector", s_gp, "mbuild")
                                if tkc > 0:
                                    b.wait("vector", s_ac, f"exp_{gb}_{j}_{ci - 1}")
                                elif gb + j > 0:
                                    pj2, pgb2 = (0, gb) if j == 1 else (1, gb - 1)
                                    pn2 = (pgb2 + 1) * NTQ
                                    b.wait("vector", s_ac, f"exp_{pgb2}_{pj2}_{pn2 - 1}")
                                M[f"msk_{gb}_{j}_{ci}"] = b.op("vector", lambda e, tkc=tkc, pss=pss: e.tensor_add(
                                    out=smask_ap()[0:TQ, :], in0=pss[0:TQ, 0:TB],
                                    in1=maskT_ap()[0:TQ, tkc * TB:(tkc + 1) * TB]), s_dv)
                                b.wait("scalar", s_dv, f"msk_{gb}_{j}_{ci}")
                                M[f"exp_{gb}_{j}_{ci}"] = b.op("scalar", lambda e, half=half: e.activation(
                                    out=AR(off_p + half * TB, TB)[0:TQ, :],
                                    in_=smask_ap()[0:TQ, 0:TB], func=AF.Exp, bias=EXP_BIAS), s_ac)
                            else:
                                b.wait("scalar", s_pe, f"smm_{gb}_{j}_{ci}")
                                M[f"exp_{gb}_{j}_{ci}"] = b.op("scalar", lambda e, half=half, pss=pss: e.activation(
                                    out=AR(off_p + half * TB, TB)[0:TQ, :],
                                    in_=pss[0:TQ, 0:TB], func=AF.Exp, bias=EXP_BIAS), s_ac)
                            b.wait("tensor", s_ac, f"exp_{gb}_{j}_{ci}")
                            for tqs in range(NTQ):
                                r = b.op("tensor", lambda e, half=half, tqs=tqs, j=j, kv=kv, tkc=tkc, ci=ci, nch_=nch_: e.matmul(
                                    pqo[tqs][0:TQ, 0:VS],
                                    arena[0:TQ, off_p + half * TB + tqs * TQ: off_p + half * TB + (tqs + 1) * TQ],
                                    AR(vj_off(j, kv), NTQ * VS).rearrange("p (s v) -> p s v", v=VS)[0:TQ, tkc, :],
                                    start=(ci == 0), stop=(ci == nch_ - 1),
                                    skip_group_check=True),
                                    s_pe if tqs == NTQ - 1 else None)
                            M[f"pv_{gb}_{j}_{ci}"] = r
                            ci += 1
                    M[f"pvmm_{gb}_{j}"] = r
                    for tqs in range(NTQ):
                        b.wait("vector", s_pe, f"pvmm_{gb}_{j}")
                        if gb + j > 0:
                            pj, pgb = (0, gb) if j == 1 else (1, gb - 1)
                            b.wait("vector", s_ao, f"ao_grp_{pgb}_{pj}")
                        M[f"lc_{gb}_{j}_{tqs}"] = b.op("vector", lambda e, tqs=tqs: e.tensor_copy(
                            out=lrec[0:TQ, 0:1], in_=pqo[tqs][0:TQ, HD:HD + 1]), s_dv)
                        b.wait("vector", s_dv, f"lc_{gb}_{j}_{tqs}")
                        M[f"lr_{gb}_{j}_{tqs}"] = b.op("vector", lambda e: e.reciprocal(
                            out=lrec[0:TQ, 1:2], in_=lrec[0:TQ, 0:1]), s_dv)
                        b.wait("vector", s_dv, f"lr_{gb}_{j}_{tqs}")
                        M[f"aofin_{gb}_{j}_{tqs}"] = b.op("vector", lambda e, tqs=tqs, j=j: e.tensor_scalar(
                            out=AR(off_ao + j * NTQ * HD + tqs * HD, HD)[0:TQ, :],
                            in0=pqo[tqs][0:TQ, 0:HD], scalar1=lrec[0:TQ, 1:2],
                            scalar2=None, op0=ALU.mult), s_dv)
                        sl = tqs % 2
                        ptt = pst1 if sl == 0 else pst2
                        b.wait("tensor", s_dv, f"aofin_{gb}_{j}_{tqs}")
                        if gb + j + tqs > 0:
                            b.wait("tensor", s_dv, "aotc_prev")
                        M["aotr_cur"] = b.op("tensor", lambda e, tqs=tqs, j=j, ptt=ptt: e.transpose(
                            ptt[:, 0:128].bitcast(BF16)[:, 0:TQ],
                            AR(off_ao + j * NTQ * HD + tqs * HD, HD)[0:TQ, :],
                            idbf[0:TQ, 0:TQ]), s_pe)
                        b.wait("vector", s_pe, "aotr_cur")
                        if gb + j + tqs > 0:
                            b.wait("vector", s_ao, "aowr_prev")
                        M["aotc_prev"] = b.op("vector", lambda e, sl=sl, ptt=ptt: e.tensor_copy(
                            out=AR(off_aot + sl * TQ, TQ),
                            in_=ptt[:, 0:128].bitcast(BF16)[:, 0:TQ]), s_dv)
                        b.wait("sync", s_dv, "aotc_prev")
                        b.op("sync", lambda e, gb=gb, j=j, tqs=tqs, sl=sl: e.dma_start(
                            out=AIN[0 if gb < 8 else 1].ap()[
                                128 * j:128 * j + 128,
                                (gb % 8) * TB + tqs * TQ:(gb % 8) * TB + (tqs + 1) * TQ],
                            in_=AR(off_aot + sl * TQ, TQ)), s_ao, 16)
                        M["aowr_prev"] = s_ao.n
                    M[f"ao_grp_{gb}_{j}"] = s_ao.n
                M[f"fl_fin_{gb}"] = M[f"pvmm_{gb}_1"]
                if gb == 7:
                    M["ao_tot_0"] = s_ao.n
                    b.wait("gpsimd", s_ao, "ao_tot_0")
                    M["cc_ao0"] = b.op("gpsimd", lambda e: e.collective_compute(
                        "AllGather", ALU.bypass, replica_groups=RG,
                        ins=[AIN[0].ap()], outs=[AOUT[0].ap()]), s_cc)
            M["ao_tot_1"] = s_ao.n
            b.wait("gpsimd", s_ao, "ao_tot_1")
            M["cc_ao1"] = b.op("gpsimd", lambda e: e.collective_compute(
                "AllGather", ALU.bypass, replica_groups=RG,
                ins=[AIN[1].ap()], outs=[AOUT[1].ap()]), s_cc)

            for bi in range(2):
                ph = 20 + bi
                phase_barrier(ph)
                b.wait("sync", s_cc, "cc_ao0" if bi == 0 else "cc_ao1")
                if bi == 1:
                    b.wait("sync", s_pe, f"op_all_0")
                coeff = TB if bi == 0 else -TB
                cnst = 0 if bi == 0 else 7 * TB
                for he in range(2):
                    b.op("sync", lambda e, bi=bi, he=he, coeff=coeff, cnst=cnst: e.dma_start(
                        out=arena[:, off_att:off_att + 16 * TB]
                            .rearrange("p (ec t) -> p ec t", t=TB)[:, he::2, :],
                        in_=bass.AP(AOUT[bi], b.dynoff(e, coeff, cnst + 128 * he),
                                    [[8 * TB, 128], [256 * 8 * TB, 8], [1, TB]])), s_q0, 16)
                M[f"atld_{bi}"] = s_q0.n
                for wb in range(4):
                    half = wb % 2
                    if wb >= 2:
                        b.wait("sync", s_pe, f"op_fin_{bi}_{wb - 2}")
                    elif bi == 0:
                        b.wait("sync", s_pe, "aw_fin_1_10" if wb == 0 else "aw_fin_1_11")
                    else:
                        b.wait("sync", s_pe, f"op_fin_0_{2 + wb}")
                    M[f"ow_ld_{bi}_{wb}"] = b.op("sync", lambda e, wb=wb, half=half: e.dma_start(
                        out=at_view(half), in_=WATkc[:, :, 512 * (12 + wb):512 * (12 + wb) + 512]), sS[half], 16)
                    b.wait("tensor", sS[half], f"ow_ld_{bi}_{wb}")
                    b.wait("tensor", s_q0, f"atld_{bi}")
                    for sub in range(4):
                        dc = wb * 4 + sub
                        par = dc % 2
                        pg = psE if dc % 2 == 0 else psF
                        if dc >= 2:
                            b.wait("sync", s_dv, f"resfree_{ph}_{dc - 2}")
                        elif bi == 0:
                            b.wait("sync", s_dv, "nm1_11_%d" % (14 + dc))
                        else:
                            b.wait("sync", s_dv, "resfree_20_%d" % (14 + dc))
                        M[f"res_ld_{ph}_{dc}"] = b.op("sync", lambda e, dc=dc, par=par, bi=bi: e.dma_start(
                            out=rbuf[par][:, :], in_=R1[bi][128 * dc:128 * dc + 128, :]), sR[par], 16)
                        if dc >= 2:
                            b.wait("tensor", s_dv, f"opadd_{bi}_{dc - 2}")
                        for ec in range(16):
                            r = b.op("tensor", lambda e, half=half, ec=ec, sub=sub, pg=pg: e.matmul(
                                pg[:, 0:TB], at_view(half)[:, ec, sub * 128:(sub + 1) * 128],
                                arena[:, off_att + ec * TB: off_att + (ec + 1) * TB],
                                start=(ec == 0), stop=(ec == 15), skip_group_check=True),
                                s_pe if ec == 15 else None)
                        M[f"opmm_{bi}_{dc}"] = r
                        b.wait("vector", s_pe, f"opmm_{bi}_{dc}")
                        b.wait("vector", sR[par], f"res_ld_{ph}_{dc}")
                        if dc >= 2:
                            b.wait("vector", sW[par], f"r2wr_{bi}_{dc - 2}")
                        elif bi == 0:
                            b.wait("vector", sW[par], f"rwr_1_{14 + dc}")
                        else:
                            b.wait("vector", sW[par], f"r2wr_0_{14 + dc}")
                        M[f"opadd_{bi}_{dc}"] = b.op("vector", lambda e, par=par, dc=dc, pg=pg: e.scalar_tensor_tensor(
                            out=outb[par][:, :], in0=pg[:, 0:TB],
                            scalar=bo_sb[:, dc:dc + 1], in1=rbuf[par][:, :],
                            op0=ALU.add, op1=ALU.add), s_dv)
                        M[f"resfree_{ph}_{dc}"] = M[f"opadd_{bi}_{dc}"]
                        b.wait("gpsimd", s_dv, f"opadd_{bi}_{dc}")
                        M[f"r2wr_{bi}_{dc}"] = b.op("gpsimd", lambda e, bi=bi, dc=dc, par=par: e.dma_start(
                            out=R2[bi][128 * dc:128 * dc + 128, :], in_=outb[par][:, :]), sW[par], 16)
                    M[f"op_fin_{bi}_{wb}"] = M[f"opmm_{bi}_{4 * wb + 3}"]
                M[f"op_all_{bi}"] = M[f"opmm_{bi}_15"]
                M[f"op_wr_tot0_{bi}"] = s_w0.n
                M[f"op_wr_tot1_{bi}"] = s_w1.n

            ffn(2, R2[0], ob[0],
                prev_gu=[[(s_pe, "op_fin_1_2")], [(s_pe, "op_fin_1_3")]],
                prev_dn=[[(s_pe, "gu_fin_2_%d" % (NSP - 2))],
                         [(s_pe, "gu_fin_2_%d" % (NSP - 1))]],
                prev_xn="op_fin_1_3",
                prev_r=[[(s_dv, "resfree_21_14"), (s_w0, "op_wr_tot0_0"), (s_w1, "op_wr_tot1_0")],
                        [(s_dv, "resfree_21_15")]],
                prev_wr=[[(s_w0, "r2wr_1_14")], [(s_w1, "r2wr_1_15")]])
            ffn(3, R2[1], ob[1],
                prev_gu=[[(s_pe, "dn_fin_2_126")], [(s_pe, "dn_fin_2_127")]],
                prev_dn=[[(s_pe, "gu_fin_3_%d" % (NSP - 2))],
                         [(s_pe, "gu_fin_3_%d" % (NSP - 1))]],
                prev_xn="gu_fin_2_%d" % (NSP - 1),
                prev_r=[[(s_dv, "resadd_2_14"), (s_w0, "op_wr_tot0_1"), (s_w1, "op_wr_tot1_1")],
                        [(s_dv, "resadd_2_15")]],
                prev_wr=[[(s_w0, "rwr_2_14")], [(s_w1, "rwr_2_15")]])

            b.wait("sync", s_w0, "ffn_wr_tot0_3")
            b.wait("sync", s_w1, "ffn_wr_tot1_3")

        b.begin(None, None)
        program()
        dry_counts = [s.n for s in sems]
        dry_M = dict(b.M)
        with nc.Block() as blk:
            for name, deco in (("sync", blk.sync), ("gpsimd", blk.gpsimd),
                               ("vector", blk.vector), ("scalar", blk.scalar),
                               ("tensor", blk.tensor)):
                def mk(name):
                    def f(eng):
                        b.begin(name, eng)
                        program()
                        assert [s.n for s in sems] == dry_counts, name
                        assert b.M == dry_M, name
                    return f
                deco(mk(name))

    return nc


_PI = pi_perm()


def _shard_inputs(L0_attn_x, L0_attn_w_qkv, L0_attn_b_qkv, L0_attn_w_out,
                  L0_attn_b_out, L0_ffn_gate_W_packed, L0_ffn_gate_scales,
                  L0_ffn_up_W_packed, L0_ffn_up_scales, L0_ffn_down_W_packed,
                  L0_ffn_down_scales, MLP, S):
    TB = S // NBLK
    AN = MLP // 2048
    x = np.ascontiguousarray(np.asarray(L0_attn_x, dtype=np.float32)[0])
    xT = np.ascontiguousarray(x.T[_PI])
    wq = np.asarray(L0_attn_w_qkv, dtype=np.float32)
    wo = np.asarray(L0_attn_w_out, dtype=np.float32)
    bq = np.asarray(L0_attn_b_qkv, dtype=np.float32)
    bo_ = np.asarray(L0_attn_b_out, dtype=np.float32)
    wqT = np.ascontiguousarray(wq.T[_PI])
    woT = np.ascontiguousarray(wo.T[:, _PI])

    def u8(a):
        return np.ascontiguousarray(np.asarray(a).view(np.uint8)[:, 0::4])

    g8, u8_, d8 = u8(L0_ffn_gate_W_packed), u8(L0_ffn_up_W_packed), u8(L0_ffn_down_W_packed)
    gs = np.asarray(L0_ffn_gate_scales, dtype=np.float32)
    us = np.asarray(L0_ffn_up_scales, dtype=np.float32)
    ds = np.asarray(L0_ffn_down_scales, dtype=np.float32)
    d8p = d8[_PI]
    dsp_ = ds[_PI]

    awc = np.concatenate([wqT, woT], axis=1)
    in_maps = []
    for c in range(8):
        m = {}
        m["xb0"] = np.ascontiguousarray(xT[:, TB * c:TB * (c + 1)])
        m["xb1"] = np.ascontiguousarray(xT[:, TB * (15 - c):TB * (16 - c)])
        m["gpk"] = np.ascontiguousarray(g8[:, 128 * c:128 * (c + 1)].T)
        m["upk"] = np.ascontiguousarray(u8_[:, 128 * c:128 * (c + 1)].T)
        pidx = 8 * c + np.arange(128) // 16
        m["gsc"] = np.ascontiguousarray(gs[:, pidx].T)
        m["usc"] = np.ascontiguousarray(us[:, pidx].T)
        nbw = MLP // 16
        dslice = d8p[:, nbw * c: nbw * (c + 1)]
        m["dpk"] = np.ascontiguousarray(
            dslice.T.reshape(AN, 128, 2048).transpose(1, 0, 2).reshape(128, AN * 2048))
        a_i = np.arange(AN)
        p_i = np.arange(128)
        col = (MLP // 256) * c + (a_i[None, :] * 8 + (p_i[:, None] // 16))
        m["dsc"] = np.ascontiguousarray(
            dsp_[:, col.reshape(-1)].reshape(2048, 128, AN).transpose(1, 2, 0).reshape(128, AN * 2048))
        m["aw"] = np.ascontiguousarray(awc[256 * c:256 * (c + 1), :])
        m["bqk"] = np.ascontiguousarray(bq[0:4096].reshape(32, 128).T)
        m["bvr"] = np.ascontiguousarray(bq[4096:6144].reshape(1, 2048))
        m["bo"] = np.ascontiguousarray(bo_[_PI].reshape(16, 128).T)
        in_maps.append(m)
    return in_maps


def _assemble(results, MLP, S):
    TB = S // NBLK
    inv = np.empty_like(_PI)
    inv[_PI] = np.arange(D)
    out = np.empty((1, S, D), dtype=np.float32)
    for c in range(8):
        out[0, TB * c:TB * (c + 1), :] = results[c]["ob0"].T[:, inv]
        out[0, TB * (15 - c):TB * (16 - c), :] = results[c]["ob1"].T[:, inv]
    return out


def kernel(**inputs):
    MLP, S = 16384, 8192
    in_maps = _shard_inputs(MLP=MLP, S=S, **inputs)
    nc = build_program(MLP, S)
    res = run_bass_kernel_spmd(nc, in_maps, core_ids=list(range(8)))
    return _assemble(res.results, MLP, S)
